# version 03
# baseline (speedup 1.0000x reference)
import contextlib
import numpy as np
import ml_dtypes
import concourse.bass as bass
import concourse.mybir as mybir
from concourse.bass_utils import run_bass_kernel_spmd

F32 = mybir.dt.float32
BF16 = mybir.dt.bfloat16
AF = mybir.ActivationFunctionType
ALU = mybir.AluOpType
AX = mybir.AxisListType





A_NT = 17
A_NS = 16
A_W = 8


def pos_weights_row(s):
    out = np.zeros(8, dtype=np.float64)
    w = min(s, A_W)
    for k in range(8):
        idx = s - A_W + k
        valid = (idx >= 0) and (s >= 1)
        j = k - (A_W - w)
        lin = (0.1 + 0.9 * j / max(w - 1, 1)) if w > 1 else 0.1
        out[k] = lin if valid else 0.0
    return out


def band_pair(s0):
    Bc = np.zeros((128, 128), dtype=np.float64)
    Bm = np.zeros((128, 128), dtype=np.float64)
    for ss in range(128):
        s = s0 + ss
        pw = pos_weights_row(s)
        for k in range(8):
            t = s - A_W + k
            tt = t - (s0 - 128)
            if 0 <= tt < 128:
                Bc[tt, ss] = pw[k]
            tt2 = t - s0
            if 0 <= tt2 < 128:
                Bm[tt2, ss] = pw[k]
    return Bc, Bm


def host_inputs_a(x0_full, g_full, core):
    b, q = core // 4, core % 4
    r0 = q * 2048
    lo, hi = r0 - 128, r0 + 2049
    pad_lo, pad_hi = max(0, -lo), max(0, hi - 8192)
    sl = x0_full[b, max(0, lo):min(8192, hi)]
    x0s = np.pad(sl, ((pad_lo, pad_hi), (0, 0)))
    if r0 == 0:
        Bc0, Bm0 = band_pair(0)
    else:
        Bc0, Bm0 = band_pair(8 * 128)
    Bc0_f, Bm0_f = (band_pair(0) if r0 == 0 else band_pair(8 * 128))
    BcS_f, BmS_f = band_pair(8 * 128)
    pwsum = np.stack([Bc0_f.sum(0) + Bm0_f.sum(0), BcS_f.sum(0) + BmS_f.sum(0)], axis=1)
    return {
        "pwsum": np.ascontiguousarray(pwsum, dtype=np.float32),
        "x0s": np.ascontiguousarray(x0s, dtype=np.float32),
        "g": np.ascontiguousarray(g_full, dtype=np.float32),
        "band0c": Bc0.astype(ml_dtypes.bfloat16),
        "band0m": Bm0.astype(ml_dtypes.bfloat16),
    }


def build_phase_a(nc):
    x0s = nc.dram_tensor("x0s", [2177, 1024], F32, kind="ExternalInput").ap()
    g_in = nc.dram_tensor("g", [512, 1024], F32, kind="ExternalInput").ap()
    b0c_in = nc.dram_tensor("band0c", [128, 128], BF16, kind="ExternalInput").ap()
    b0m_in = nc.dram_tensor("band0m", [128, 128], BF16, kind="ExternalInput").ap()
    pws_in = nc.dram_tensor("pwsum", [128, 2], F32, kind="ExternalInput").ap()
    x1_out = nc.dram_tensor("x1", [2048, 1024], F32, kind="ExternalOutput").ap()

    BcS, BmS = band_pair(8 * 128)
    bstd_np = np.concatenate([BcS, BmS], axis=1).astype(ml_dtypes.bfloat16)
    bstd_c = nc.inline_tensor(np.ascontiguousarray(bstd_np), name="bstd").ap()
    ident_c = nc.inline_tensor(np.eye(128, dtype=np.float32), name="identf32").ap()

    SA0, SV0, ST0 = 8, 8, 4
    SA1, SV1, ST1 = SA0 + 6 * A_NT, SV0 + 9 * A_NT, ST0 + 4 * A_NT

    import contextlib
    with contextlib.ExitStack() as _st:
        ident_sb = _st.enter_context(nc.sbuf_tensor("ident_sb", [128, 128], F32))
        g_f32 = _st.enter_context(nc.sbuf_tensor("g_f32", [128, 4 * 1024], F32))
        ghat_fm = _st.enter_context(nc.sbuf_tensor("ghat_fm", [128, 8 * 512], F32))
        g_gm = _st.enter_context(nc.sbuf_tensor("g_gm", [128, 4 * 1024], F32))
        bands = _st.enter_context(nc.sbuf_tensor("bands", [128, 4 * 128], BF16))
        gcols = _st.enter_context(nc.sbuf_tensor("gcols", [128, 16], F32))
        pws = _st.enter_context(nc.sbuf_tensor("pws", [128, 2], F32))
        xa = _st.enter_context(nc.sbuf_tensor("xa", [128, 2 * 1024], F32))
        xbb = _st.enter_context(nc.sbuf_tensor("xbb", [128, 2 * 1024], F32))
        diff = _st.enter_context(nc.sbuf_tensor("diff", [128, 2 * 1024], F32))
        sqscr = _st.enter_context(nc.sbuf_tensor("sqscr", [128, 1024], F32))
        normedf = _st.enter_context(nc.sbuf_tensor("normedf", [128, 2 * 1024], F32))
        nfm = _st.enter_context(nc.sbuf_tensor("nfm", [128, 1024], F32))
        Etile = _st.enter_context(nc.sbuf_tensor("Etile", [128, 512], F32))
        gw04 = _st.enter_context(nc.sbuf_tensor("gw04", [128, 512], F32))
        gwT = _st.enter_context(nc.sbuf_tensor("gwT", [128, 512], F32))
        inflmw = _st.enter_context(nc.sbuf_tensor("inflmw", [128, 1024], F32))
        mbl = _st.enter_context(nc.sbuf_tensor("mbl", [128, A_NT * 1024], BF16))
        mwb = _st.enter_context(nc.sbuf_tensor("mwb", [128, A_NT], BF16))
        cols = _st.enter_context(nc.sbuf_tensor("cols", [128, 8 * A_NT], F32))
        xres = _st.enter_context(nc.sbuf_tensor("xres", [128, 2 * 1024], F32))
        res = _st.enter_context(nc.sbuf_tensor("res", [128, 1024], F32))
        scols = _st.enter_context(nc.sbuf_tensor("scols", [128, 8 * A_NS], F32))
        x1b = _st.enter_context(nc.sbuf_tensor("x1b", [128, 2 * 1024], F32))
        p_tr = _st.enter_context(nc.psum_tensor("p_tr", [128, 1024], F32))
        p_sims = _st.enter_context(nc.psum_tensor("p_sims", [128, 512], F32))
        p_infl = _st.enter_context(nc.psum_tensor("p_infl", [128, 1024], F32))
        p_num = _st.enter_context(nc.psum_tensor("p_num", [128, 1024], F32))
        p_den = _st.enter_context(nc.psum_tensor("p_den", [128, 8], F32))
        lc = _st.enter_context(nc.semaphore("lc"))
        ld0 = _st.enter_context(nc.semaphore("ld0"))
        ld1 = _st.enter_context(nc.semaphore("ld1"))
        lr0 = _st.enter_context(nc.semaphore("lr0"))
        lr1 = _st.enter_context(nc.semaphore("lr1"))
        so0 = _st.enter_context(nc.semaphore("so0"))
        so1 = _st.enter_context(nc.semaphore("so1"))
        sgv = _st.enter_context(nc.semaphore("sgv"))
        sga = _st.enter_context(nc.semaphore("sga"))
        sp = _st.enter_context(nc.semaphore("sp"))
        sa = _st.enter_context(nc.semaphore("sa"))
        sv = _st.enter_context(nc.semaphore("sv"))
        stt = _st.enter_context(nc.semaphore("stt"))
        block = _st.enter_context(nc.Block())
        MS, MSC, INVM, MAG, E4, MW06, MW04, SUME = range(8)
        def col(i, j):
            return cols[:, 8 * i + j:8 * i + j + 1]
        SDEN, SRD, SMU, SVAR, SSQS, SSUM, SRSTD = range(7)
        def scol(m, j):
            return scols[:, 8 * m + j:8 * m + j + 1]

        def xa_s(i):
            return xa[:, 1024 * (i % 2):1024 * (i % 2) + 1024]
        def xbb_s(i):
            return xbb[:, 1024 * (i % 2):1024 * (i % 2) + 1024]
        def diff_s(i):
            return diff[:, 1024 * (i % 2):1024 * (i % 2) + 1024]
        def normed_s(i):
            return normedf[:, 1024 * (i % 2):1024 * (i % 2) + 1024]
        def mbl_t(i, half=None):
            if half is None:
                return mbl[:, 1024 * i:1024 * i + 1024]
            return mbl[:, 1024 * i + 512 * half:1024 * i + 512 * half + 512]
        def xres_s(m):
            return xres[:, 1024 * (m % 2):1024 * (m % 2) + 1024]
        def x1b_s(m):
            return x1b[:, 1024 * (m % 2):1024 * (m % 2) + 1024]
        def bandsel(m):
            if m == 0:
                return bands[:, 0:128], bands[:, 128:256]
            return bands[:, 256:384], bands[:, 384:512]
        ldp = [ld0, ld1]
        lrp = [lr0, lr1]
        sop = [so0, so1]
        sgc = {"v": 0, "a": 0}
        gsem = {}
        def settle(eng, ins, which="v"):
            gsem = {"v": sgv, "a": sga}[which]
            sgc[which] += 1
            ins.then_inc(gsem, 1)
            eng.wait_ge(gsem, sgc[which])

        @block.sync
        def _(sync):
            sync.dma_start(out=ident_sb[:], in_=ident_c).then_inc(lc, 16)
            for gt in range(4):
                sync.dma_start(out=g_f32[:, 1024 * gt:1024 * gt + 1024],
                               in_=g_in[128 * gt:128 * gt + 128, :]).then_inc(lc, 16)
            sync.dma_start(out=bands[:, 256:512].rearrange("p (a x) -> p a x", a=2),
                           in_=bstd_c.rearrange("p (a x) -> p a x", a=2)).then_inc(lc, 16)
            sync.dma_start(out=bands[:, 0:128], in_=b0c_in).then_inc(lc, 16)
            sync.dma_start(out=bands[:, 128:256], in_=b0m_in).then_inc(lc, 16)
            sync.dma_start(out=pws[:], in_=pws_in).then_inc(lc, 16)
            for i in range(A_NT):
                if i >= 2:
                    sync.wait_ge(sp, i - 1)
                sync.dma_start(out=xa_s(i), in_=x0s[128 * i:128 * i + 128, :]).then_inc(ldp[i % 2], 16)
                sync.dma_start(out=xbb_s(i), in_=x0s[128 * i + 1:128 * i + 129, :]).then_inc(ldp[i % 2], 16)
            for m in range(A_NS):
                if m >= 2:
                    sync.wait_ge(sv, SV1 + 4 * (m - 2) + 4)
                sync.dma_start(out=xres_s(m), in_=x0s[128 + 128 * m:256 + 128 * m, :]).then_inc(lrp[m % 2], 16)
                if m >= 1:
                    sync.wait_ge(sv, SV1 + 4 * (m - 1) + 4)
                    sync.dma_start(out=x1_out[128 * (m - 1):128 * m, :], in_=x1b_s(m - 1)).then_inc(sop[(m - 1) % 2], 16)
            sync.wait_ge(sv, SV1 + 4 * (A_NS - 1) + 4)
            sync.dma_start(out=x1_out[128 * (A_NS - 1):128 * A_NS, :], in_=x1b_s(A_NS - 1)).then_inc(sop[(A_NS - 1) % 2], 16)
            sync.wait_ge(so0, 16 * 8)
            sync.wait_ge(so1, 16 * 8)

        @block.gpsimd
        def _(g):
            for i in range(A_NT):
                g.wait_ge(ldp[i % 2], 32 * (i // 2 + 1))
                if i >= 2:
                    g.wait_ge(sa, SA0 + 6 * (i - 2) + 4)
                g.tensor_sub(diff_s(i), xbb_s(i), xa_s(i)).then_inc(sp, 1)

        @block.scalar
        def _(s):
            s.wait_ge(lc, 144)
            for gt in range(4):
                s.activation(sqscr[:], g_f32[:, 1024 * gt:1024 * gt + 1024], AF.Square,
                             accum_out=gcols[:, 4 * gt:4 * gt + 1]).then_inc(sa, 1)
                s.wait_ge(sa, 2 * gt + 1)
                settle(s, s.activation(gcols[:, 4 * gt + 1:4 * gt + 2], gcols[:, 4 * gt:4 * gt + 1], AF.Ln), "a")
                s.activation(gcols[:, 4 * gt + 1:4 * gt + 2], gcols[:, 4 * gt + 1:4 * gt + 2],
                             AF.Exp, scale=0.5).then_inc(sa, 1)
            for i in range(A_NT):
                s.wait_ge(sp, i + 1)
                s.activation(sqscr[:], diff_s(i), AF.Square, accum_out=col(i, MS)).then_inc(sa, 1)
                s.wait_ge(sv, SV0 + 9 * i + 1)
                s.activation(col(i, INVM), col(i, MSC), AF.Ln).then_inc(sa, 1)
                s.wait_ge(sa, SA0 + 6 * i + 2)
                s.activation(col(i, INVM), col(i, INVM), AF.Exp, scale=-0.5).then_inc(sa, 1)
                if i >= 2:
                    s.wait_ge(sv, SV0 + 9 * (i - 2) + 9)
                s.wait_ge(sa, SA0 + 6 * i + 3)
                s.activation(normed_s(i), diff_s(i), AF.Copy, scale=col(i, INVM)).then_inc(sa, 1)
                s.wait_ge(sv, SV0 + 9 * i + 2)
                s.activation(col(i, E4), col(i, MAG), AF.Exp, scale=-4.0).then_inc(sa, 1)
                s.wait_ge(sv, SV0 + 9 * i + 5)
                s.wait_ge(stt, ST0 + 4 * i + 2)
                if i >= 1:
                    s.wait_ge(sv, SV0 + 9 * (i - 1) + 6)
                s.activation(Etile[:], p_sims[:], AF.Exp, scale=2.0, bias=col(i, E4),
                             accum_out=col(i, SUME)).then_inc(sa, 1)
            for m in range(A_NS):
                s.wait_ge(sv, SV1 + 4 * m + 2)
                s.activation(sqscr[:], res[:], AF.Square, accum_out=scol(m, SSQS)).then_inc(sa, 1)
                s.activation(sqscr[:], res[:], AF.Copy, accum_out=scol(m, SSUM)).then_inc(sa, 1)
                s.wait_ge(sv, SV1 + 4 * m + 3)
                s.activation(scol(m, SRSTD), scol(m, SVAR), AF.Ln).then_inc(sa, 1)
                s.wait_ge(sa, SA1 + 4 * m + 3)
                s.activation(scol(m, SRSTD), scol(m, SRSTD), AF.Exp, scale=-0.5).then_inc(sa, 1)

        @block.vector
        def _(v):
            for gt in range(4):
                v.wait_ge(sa, 2 * (gt + 1))
                c1 = gcols[:, 4 * gt + 1:4 * gt + 2]
                c2 = gcols[:, 4 * gt + 2:4 * gt + 3]
                settle(v, v.tensor_scalar_max(c1, c1, 1e-8))
                settle(v, v.reciprocal(c2, c1))
                if gt >= 2:
                    v.wait_ge(stt, gt - 1)
                v.tensor_scalar_mul(normed_s(gt), g_f32[:, 1024 * gt:1024 * gt + 1024], c2)
                v.tensor_copy(g_gm[:, 1024 * gt:1024 * gt + 1024],
                              g_f32[:, 1024 * gt:1024 * gt + 1024]).then_inc(sv, 1)
                v.wait_ge(stt, gt + 1)
                v.tensor_copy(
                    ghat_fm.rearrange("p (fc g) -> p fc g", fc=8)[:, :, 128 * gt:128 * gt + 128],
                    p_tr[:].rearrange("p (fc g) -> p fc g", fc=8)).then_inc(sv, 1)
            for i in range(A_NT):
                v.wait_ge(sa, SA0 + 6 * i + 1)
                v.tensor_scalar_max(col(i, MSC), col(i, MS), 1e-30).then_inc(sv, 1)
                v.wait_ge(sa, SA0 + 6 * i + 3)
                v.tensor_mul(col(i, MAG), col(i, MSC), col(i, INVM)).then_inc(sv, 1)
                v.wait_ge(sa, SA0 + 6 * i + 5)
                settle(v, v.tensor_scalar(col(i, MSC), col(i, E4), -1.0, 1.0, ALU.mult, ALU.add))
                settle(v, v.tensor_scalar_add(col(i, MAG), col(i, E4), 1.0))
                settle(v, v.reciprocal(col(i, MAG), col(i, MAG)))
                settle(v, v.tensor_mul(col(i, MW04), col(i, MSC), col(i, MAG)))
                v.tensor_copy(mwb[:, i:i + 1], col(i, MW04))
                v.tensor_scalar_mul(col(i, MW06), col(i, MW04), 0.6)
                v.tensor_scalar_mul(col(i, MW04), col(i, MW04), 1.0).then_inc(sv, 1)
                v.wait_ge(stt, ST0 + 4 * i + 1)
                v.tensor_copy(nfm[:], p_tr[:]).then_inc(sv, 1)
                v.wait_ge(stt, ST0 + 4 * i + 2)
                settle(v, v.reduce_max(col(i, E4), p_sims[:], AX.X))
                v.tensor_scalar_mul(col(i, E4), col(i, E4), -2.0).then_inc(sv, 1)
                v.wait_ge(sa, SA0 + 6 * i + 6)
                if i >= 1:
                    v.wait_ge(stt, ST0 + 4 * (i - 1) + 3)
                settle(v, v.reciprocal(col(i, MSC), col(i, SUME)))
                settle(v, v.tensor_scalar_mul(col(i, MSC), col(i, MSC), 0.4))
                v.tensor_scalar_mul(gw04[:], Etile[:], col(i, MSC)).then_inc(sv, 1)
                v.wait_ge(stt, ST0 + 4 * i + 3)
                v.tensor_copy(gwT[:], p_tr[:, 0:512]).then_inc(sv, 1)
                v.wait_ge(stt, ST0 + 4 * i + 4)
                v.tensor_scalar_mul(inflmw[:], p_infl[:], col(i, MW04)).then_inc(sv, 1)
                v.scalar_tensor_tensor(mbl_t(i), normed_s(i), col(i, MW06), inflmw[:],
                                       ALU.mult, ALU.add).then_inc(sv, 1)
            for m in range(A_NS):
                v.wait_ge(stt, ST1 + 2 * m + 1)
                settle(v, v.tensor_scalar_max(scol(m, SDEN), p_den[:, 0:1], 1e-8))
                v.reciprocal(scol(m, SRD), scol(m, SDEN)).then_inc(sv, 1)
                v.wait_ge(sv, SV1 + 4 * m + 1)
                v.wait_ge(stt, ST1 + 2 * m + 2)
                v.wait_ge(lrp[m % 2], 16 * (m // 2 + 1))
                if m >= 1:
                    v.wait_ge(sa, SA1 + 4 * (m - 1) + 2)
                v.scalar_tensor_tensor(res[:], p_num[:], scol(m, SRD), xres_s(m),
                                       ALU.mult, ALU.add).then_inc(sv, 1)
                v.wait_ge(sa, SA1 + 4 * m + 2)
                settle(v, v.tensor_scalar_mul(scol(m, SMU), scol(m, SSUM), 1.0 / 1024))
                v.tensor_scalar_mul(scol(m, SVAR), scol(m, SSQS), 1.0 / 1024)
                settle(v, v.tensor_mul(scol(m, SSQS), scol(m, SMU), scol(m, SMU)))
                settle(v, v.tensor_sub(scol(m, SVAR), scol(m, SVAR), scol(m, SSQS)))
                v.tensor_scalar_add(scol(m, SVAR), scol(m, SVAR), 1e-5).then_inc(sv, 1)
                v.wait_ge(sa, SA1 + 4 * m + 4)
                if m >= 2:
                    v.wait_ge(sop[m % 2], 16 * (m // 2))
                v.tensor_scalar(x1b_s(m), res[:], scol(m, SMU), scol(m, SRSTD),
                                ALU.subtract, ALU.mult).then_inc(sv, 1)

        @block.tensor
        def _(t):
            t.wait_ge(lc, 144)
            for gt in range(4):
                t.wait_ge(sv, 2 * gt + 1)
                if gt >= 1:
                    t.wait_ge(sv, 2 * (gt - 1) + 2)
                for fc in range(8):
                    ins_ = t.transpose(p_tr[:, 128 * fc:128 * fc + 128],
                                       normed_s(gt)[:, 128 * fc:128 * fc + 128], ident_sb[:])
                    if fc == 7:
                        ins_.then_inc(stt, 1)
            for i in range(A_NT):
                t.wait_ge(sa, SA0 + 6 * i + 4)
                pv = SV0 + 9 * (i - 1) + 7 if i >= 1 else 8
                t.wait_ge(sv, pv)
                for fc in range(8):
                    ins_ = t.transpose(p_tr[:, 128 * fc:128 * fc + 128],
                                       normed_s(i)[:, 128 * fc:128 * fc + 128], ident_sb[:])
                    if fc == 7:
                        ins_.then_inc(stt, 1)
                t.wait_ge(sv, SV0 + 9 * i + 4)
                if i >= 1:
                    t.wait_ge(sa, SA0 + 6 * (i - 1) + 6)
                for fc in range(8):
                    ins_ = t.matmul(p_sims[:], nfm[:, 128 * fc:128 * fc + 128],
                                    ghat_fm[:, 512 * fc:512 * fc + 512],
                                    start=(fc == 0), stop=(fc == 7))
                    if fc == 7:
                        ins_.then_inc(stt, 1)
                t.wait_ge(sv, SV0 + 9 * i + 6)
                for gc in range(4):
                    ins_ = t.transpose(p_tr[:, 128 * gc:128 * gc + 128],
                                       gw04[:, 128 * gc:128 * gc + 128], ident_sb[:])
                    if gc == 3:
                        ins_.then_inc(stt, 1)
                t.wait_ge(sv, SV0 + 9 * i + 7)
                if i >= 1:
                    t.wait_ge(sv, SV0 + 9 * (i - 1) + 8)
                for half in range(2):
                    for gc in range(4):
                        ins_ = t.matmul(p_infl[:, 512 * half:512 * half + 512],
                                        gwT[:, 128 * gc:128 * gc + 128],
                                        g_gm[:, 1024 * gc + 512 * half:1024 * gc + 512 * half + 512],
                                        start=(gc == 0), stop=(gc == 3))
                        if half == 1 and gc == 3:
                            ins_.then_inc(stt, 1)
            for m in range(A_NS):
                Bc, Bm = bandsel(m)
                t.wait_ge(sv, SV0 + 9 * (m + 1) + 3)
                if m >= 1:
                    t.wait_ge(sv, SV1 + 4 * (m - 1) + 1)
                t.matmul(p_den[:, 0:1], Bc, mwb[:, m:m + 1], start=True, stop=False)
                t.matmul(p_den[:, 0:1], Bm, mwb[:, m + 1:m + 2], start=False, stop=True).then_inc(stt, 1)
                t.wait_ge(sv, SV0 + 9 * (m + 1) + 9)
                if m >= 1:
                    t.wait_ge(sv, SV1 + 4 * (m - 1) + 2)
                for half in range(2):
                    t.matmul(p_num[:, 512 * half:512 * half + 512], Bc, mbl_t(m, half),
                             start=True, stop=False)
                    ins_ = t.matmul(p_num[:, 512 * half:512 * half + 512], Bm, mbl_t(m + 1, half),
                                    start=False, stop=True)
                    if half == 1:
                        ins_.then_inc(stt, 1)

    return nc





B_NTQ, B_NFC, B_NTT = 2, 16, 8
B_NG = 32
PE_TQ = 16 + 16 * 48
VE_TQ = 24 + 16 * 48 + 16
SA_TQ = 16 + 16 * 32 + 32


def host_inputs_b(x1_full, wr, wgt, wgate, wup, wdown, core):
    b, q = core // 4, core % 4
    r0 = q * 2048
    return {
        "x1": np.ascontiguousarray(x1_full[b, r0:r0 + 2048], dtype=np.float32),
        "wrg": np.ascontiguousarray(np.concatenate([wr, wgt], axis=1), dtype=np.float32),
        "w_gate": np.ascontiguousarray(wgate, dtype=np.float32),
        "w_up": np.ascontiguousarray(wup, dtype=np.float32),
        "w_down": np.ascontiguousarray(wdown, dtype=np.float32),
    }


def build_phase_b(nc):
    x1_ap = nc.dram_tensor("x1", [2048, 1024], F32, kind="ExternalInput").ap()
    x2_ap = nc.dram_tensor("x2", [2048, 1024], F32, kind="ExternalOutput").ap()
    wrg = nc.dram_tensor("wrg", [1024, 16], F32, kind="ExternalInput").ap()
    wgate = nc.dram_tensor("w_gate", [8, 1024, 4096], F32, kind="ExternalInput").ap()
    wup = nc.dram_tensor("w_up", [8, 1024, 4096], F32, kind="ExternalInput").ap()
    wdown = nc.dram_tensor("w_down", [8, 4096, 1024], F32, kind="ExternalInput").ap()
    ident_c = nc.inline_tensor(np.eye(128, dtype=ml_dtypes.bfloat16), name="identb").ap()

    def pe_b0(TT, k):
        return PE_TQ * (TT // 8) + 2 * (TT % 8) + k
    def pe_gu(G, idx):
        return PE_TQ * (G // 16) + 16 + 48 * (G % 16) + idx + 1
    def pe_y(G, yidx):
        return PE_TQ * (G // 16) + 16 + 48 * (G % 16) + 32 + yidx + 1
    def ve_b0(TT, k):
        return VE_TQ * (TT // 8) + 3 * (TT % 8) + k
    def ve_h(G, idx):
        return VE_TQ * (G // 16) + 24 + 48 * (G % 16) + idx + 1
    def ve_ev(G, yidx):
        return VE_TQ * (G // 16) + 24 + 48 * (G % 16) + 32 + yidx + 1
    def ve_ln(tq, tile, k):
        return VE_TQ * tq + 24 + 16 * 48 + 2 * tile + k
    def sa_b0(TT, k):
        return SA_TQ * (TT // 8) + 2 * (TT % 8) + k
    def sa_silu(G, idx):
        return SA_TQ * (G // 16) + 16 + 32 * (G % 16) + idx + 1
    def sa_ln(tq, tile, k):
        return SA_TQ * tq + 16 + 16 * 32 + 4 * tile + k

    with contextlib.ExitStack() as _st:
        ident_sb = _st.enter_context(nc.sbuf_tensor("b_ident", [128, 128], BF16))
        wrgb = _st.enter_context(nc.sbuf_tensor("b_wrgb", [128, 8 * 16], BF16))
        x_fm = _st.enter_context(nc.sbuf_tensor("b_xfm", [128, 8 * 1024], BF16))
        xbf = _st.enter_context(nc.sbuf_tensor("b_xbf", [128, 2 * 1024], BF16))
        cw = _st.enter_context(nc.sbuf_tensor("b_cw", [128, 8 * 8], F32))
        ccols = _st.enter_context(nc.sbuf_tensor("b_ccols", [128, 8 * 8], F32))
        eg = _st.enter_context(nc.sbuf_tensor("b_eg", [128, 16], F32))
        pgb = _st.enter_context(nc.sbuf_tensor("b_pg", [128, 8], F32))
        wgb = _st.enter_context(nc.sbuf_tensor("b_wgb", [128, 2 * 8 * 512], BF16))
        wub = _st.enter_context(nc.sbuf_tensor("b_wub", [128, 2 * 8 * 512], BF16))
        wdb = _st.enter_context(nc.sbuf_tensor("b_wdb", [128, 2 * 16 * 1024], BF16))
        hb = _st.enter_context(nc.sbuf_tensor("b_h", [128, 16 * 1024], BF16))
        sgb = _st.enter_context(nc.sbuf_tensor("b_sgb", [128, 2 * 512], BF16))
        y_sb = _st.enter_context(nc.sbuf_tensor("b_ysb", [128, 8 * 1024], F32))
        lncols = _st.enter_context(nc.sbuf_tensor("b_lncols", [128, 8 * 8], F32))
        sqscr = _st.enter_context(nc.sbuf_tensor("b_sqscr", [128, 1024], F32))
        x2b = _st.enter_context(nc.sbuf_tensor("b_x2b", [128, 2 * 1024], F32))
        p_tr = _st.enter_context(nc.psum_tensor("b_ptr", [128, 1024], BF16))
        p_rg = _st.enter_context(nc.psum_tensor("b_prg", [128, 16], F32))
        gps = _st.enter_context(nc.psum_tensor("b_gps", [128, 2 * 512], F32))
        ups = _st.enter_context(nc.psum_tensor("b_ups", [128, 2 * 512], F32))
        yps = _st.enter_context(nc.psum_tensor("b_yps", [128, 2 * 512], F32))
        lc = _st.enter_context(nc.semaphore("b_lc"))
        lcg = _st.enter_context(nc.semaphore("b_lcg"))
        lx0 = _st.enter_context(nc.semaphore("b_lx0"))
        lx1 = _st.enter_context(nc.semaphore("b_lx1"))
        lg0 = _st.enter_context(nc.semaphore("b_lg0"))
        lg1 = _st.enter_context(nc.semaphore("b_lg1"))
        lu0 = _st.enter_context(nc.semaphore("b_lu0"))
        lu1 = _st.enter_context(nc.semaphore("b_lu1"))
        lw0 = _st.enter_context(nc.semaphore("b_lw0"))
        lw1 = _st.enter_context(nc.semaphore("b_lw1"))
        ly = _st.enter_context(nc.semaphore("b_ly"))
        so0 = _st.enter_context(nc.semaphore("b_so0"))
        so1 = _st.enter_context(nc.semaphore("b_so1"))
        sa = _st.enter_context(nc.semaphore("b_sa"))
        sv = _st.enter_context(nc.semaphore("b_sv"))
        stp = _st.enter_context(nc.semaphore("b_stp"))
        sgv = _st.enter_context(nc.semaphore("b_sgv"))
        sga = _st.enter_context(nc.semaphore("b_sga"))
        blk = _st.enter_context(nc.Block())

        lxp, lgp, lup, lwp, sop = [lx0, lx1], [lg0, lg1], [lu0, lu1], [lw0, lw1], [so0, so1]
        sgc = {"v": 0, "a": 0}
        def settle(eng, ins, which="v"):
            gsem = {"v": sgv, "a": sga}[which]
            sgc[which] += 1
            ins.then_inc(gsem, 1)
            eng.wait_ge(gsem, sgc[which])

        def xfm_sl(kc, t0, n):
            return x_fm[:, 1024 * kc + t0:1024 * kc + t0 + n]
        def h_sl(fc, t0, n):
            return hb[:, 1024 * fc + t0:1024 * fc + t0 + n]
        def ysb_t(tile):
            return y_sb[:, 1024 * tile:1024 * tile + 1024]
        def cw_col(tile, e):
            return cw[:, 8 * tile + e:8 * tile + e + 1]
        def ccol(tile, j):
            return ccols[:, 8 * tile + j:8 * tile + j + 1]
        def lcol(tile, j):
            return lncols[:, 8 * tile + j:8 * tile + j + 1]
        RMX, RSUM, GMX, GSUM, CPROD, CWS, CRS = range(7)
        SMU, SVAR, SSQS, SSUM, SRSTD = range(5)

        def glist():
            out = []
            for tq in range(2):
                for e in range(8):
                    for fh in range(2):
                        out.append((tq * 16 + e * 2 + fh, tq, e, fh))
            return out

        @blk.sync
        def _(sync):
            sync.dma_start(out=ident_sb[:], in_=ident_c).then_inc(lc, 16)
            for tq in range(2):
                for tile in range(8):
                    if tq >= 1:
                        sync.wait_ge(sv, ve_ln(tq - 1, tile, 2))
                    sync.dma_start(out=ysb_t(tile),
                                   in_=x1_ap[1024 * tq + 128 * tile:1024 * tq + 128 * tile + 128, :]).then_inc(ly, 16)
                for tile in range(8):
                    sync.wait_ge(sv, ve_ln(tq, tile, 2))
                    sync.dma_start(out=x2_ap[1024 * tq + 128 * tile:1024 * tq + 128 * tile + 128, :],
                                   in_=x2b[:, 1024 * (tile % 2):1024 * (tile % 2) + 1024]).then_inc(sop[tile % 2], 16)
            sync.wait_ge(so0, 16 * 8)
            sync.wait_ge(so1, 16 * 8)

        @blk.gpsimd
        def _(g):
            g.dma_start(out=wrgb[:].rearrange("p (kc e) -> p kc e", kc=8),
                        in_=wrg.rearrange("(kc p) e -> p kc e", p=128)).then_inc(lcg, 16)
            for tq in range(2):
                for tile in range(8):
                    TT = 8 * tq + tile
                    if TT >= 2:
                        g.wait_ge(stp, pe_b0(TT - 2, 1))
                    g.dma_start(out=xbf[:, 1024 * (tile % 2):1024 * (tile % 2) + 1024],
                                in_=x1_ap[1024 * tq + 128 * tile:1024 * tq + 128 * tile + 128, :]
                                ).then_inc(lxp[tile % 2], 16)
                for e in range(8):
                    for fh in range(2):
                        G = tq * 16 + e * 2 + fh
                        for p in range(4):
                            if G == 0 and p < 2:
                                pass
                            elif p < 2:
                                g.wait_ge(stp, pe_gu(G - 1, 8 * (p + 2) + 7))
                            else:
                                g.wait_ge(stp, pe_gu(G, 8 * (p - 2) + 7))
                            src = wgate[e, :, 2048 * fh + 512 * p:2048 * fh + 512 * (p + 1)]
                            g.dma_start(out=wgb[:, 4096 * (p % 2):4096 * (p % 2) + 4096]
                                        .rearrange("p (kc f) -> p kc f", kc=8),
                                        in_=src.rearrange("(kc q) f -> q kc f", q=128)
                                        ).then_inc(lgp[p % 2], 16)
                            src = wup[e, :, 2048 * fh + 512 * p:2048 * fh + 512 * (p + 1)]
                            g.dma_start(out=wub[:, 4096 * (p % 2):4096 * (p % 2) + 4096]
                                        .rearrange("p (kc f) -> p kc f", kc=8),
                                        in_=src.rearrange("(kc q) f -> q kc f", q=128)
                                        ).then_inc(lup[p % 2], 16)
                        for p in range(4):
                            if G >= 2 and p == 0:
                                g.wait_ge(stp, pe_y(G - 2, 15))
                            src = wdown[e, 2048 * fh + 512 * p:2048 * fh + 512 * (p + 1), :]
                            g.dma_start(out=wdb[:, 16384 * (G % 2) + 4096 * p:16384 * (G % 2) + 4096 * (p + 1)]
                                        .rearrange("p (fc y) -> p fc y", fc=4),
                                        in_=src.rearrange("(fc q) y -> q fc y", q=128)
                                        ).then_inc(lwp[G % 2], 16)

        @blk.tensor
        def _(t):
            t.wait_ge(lc, 16)
            t.wait_ge(lcg, 16)
            for tq in range(2):
                for tile in range(8):
                    TT = 8 * tq + tile
                    t.wait_ge(lxp[tile % 2], 16 * (4 * tq + tile // 2 + 1))
                    if TT >= 1:
                        t.wait_ge(sv, ve_b0(TT - 1, 1))
                    for fc in range(8):
                        ins = t.transpose(p_tr[:, 128 * fc:128 * fc + 128],
                                          xbf[:, 1024 * (tile % 2) + 128 * fc:1024 * (tile % 2) + 128 * fc + 128],
                                          ident_sb[:])
                        if fc == 7:
                            ins.then_inc(stp, 1)
                    t.wait_ge(sv, ve_b0(TT, 1))
                    if TT >= 1:
                        t.wait_ge(sa, sa_b0(TT - 1, 2))
                    for kc in range(8):
                        ins = t.matmul(p_rg[:], xfm_sl(kc, 128 * tile, 128),
                                       wrgb[:, 16 * kc:16 * kc + 16],
                                       start=(kc == 0), stop=(kc == 7))
                        if kc == 7:
                            ins.then_inc(stp, 1)
                for e in range(8):
                    for fh in range(2):
                        G = tq * 16 + e * 2 + fh
                        for fc in range(16):
                            p = fc // 4
                            for ts in range(2):
                                idx = fc * 2 + ts
                                if ts == 0 and fc % 4 == 0:
                                    t.wait_ge(lgp[p % 2], 16 * (2 * G + p // 2 + 1))
                                    t.wait_ge(lup[p % 2], 16 * (2 * G + p // 2 + 1))
                                prev = idx - 2
                                if prev >= 0:
                                    t.wait_ge(sa, sa_silu(G, prev))
                                    t.wait_ge(sv, ve_h(G, prev))
                                elif G >= 1:
                                    t.wait_ge(sa, sa_silu(G - 1, prev + 32))
                                    t.wait_ge(sv, ve_h(G - 1, prev + 32))
                                par = idx % 2
                                for kc in range(8):
                                    t.matmul(gps[:, 512 * par:512 * par + 512],
                                             wgb[:, 4096 * (p % 2) + 512 * kc + 128 * (fc % 4):
                                                 4096 * (p % 2) + 512 * kc + 128 * (fc % 4) + 128],
                                             xfm_sl(kc, 512 * ts, 512),
                                             start=(kc == 0), stop=(kc == 7))
                                for kc in range(8):
                                    ins = t.matmul(ups[:, 512 * par:512 * par + 512],
                                                   wub[:, 4096 * (p % 2) + 512 * kc + 128 * (fc % 4):
                                                       4096 * (p % 2) + 512 * kc + 128 * (fc % 4) + 128],
                                                   xfm_sl(kc, 512 * ts, 512),
                                                   start=(kc == 0), stop=(kc == 7))
                                    if kc == 7:
                                        ins.then_inc(stp, 1)
                        t.wait_ge(sv, ve_h(G, 31))
                        t.wait_ge(lwp[G % 2], 16 * 4 * (G // 2 + 1))
                        for tt in range(8):
                            for ysl in range(2):
                                yidx = tt * 2 + ysl
                                prev = yidx - 2
                                if prev >= 0:
                                    t.wait_ge(sv, ve_ev(G, prev))
                                elif G >= 1:
                                    t.wait_ge(sv, ve_ev(G - 1, prev + 16))
                                for fc in range(16):
                                    ins = t.matmul(yps[:, 512 * (yidx % 2):512 * (yidx % 2) + 512],
                                                   h_sl(fc, 128 * tt, 128),
                                                   wdb[:, 16384 * (G % 2) + 1024 * fc + 512 * ysl:
                                                       16384 * (G % 2) + 1024 * fc + 512 * ysl + 512],
                                                   start=(fc == 0), stop=(fc == 15))
                                    if fc == 15:
                                        ins.then_inc(stp, 1)

        @blk.scalar
        def _(s):
            for tq in range(2):
                for tile in range(8):
                    TT = 8 * tq + tile
                    s.wait_ge(stp, pe_b0(TT, 2))
                    s.wait_ge(sv, ve_b0(TT, 2))
                    if TT >= 1:
                        s.wait_ge(sv, ve_b0(TT - 1, 3))
                    s.activation(eg[:, 0:8], p_rg[:, 0:8], AF.Exp, bias=ccol(tile, RMX),
                                 accum_out=ccol(tile, RSUM)).then_inc(sa, 1)
                    s.activation(eg[:, 8:16], p_rg[:, 8:16], AF.Exp, scale=10.0, bias=ccol(tile, GMX),
                                 accum_out=ccol(tile, GSUM)).then_inc(sa, 1)
                for e in range(8):
                    for fh in range(2):
                        G = tq * 16 + e * 2 + fh
                        for idx in range(32):
                            s.wait_ge(stp, pe_gu(G, idx))
                            prev = idx - 2
                            if prev >= 0:
                                s.wait_ge(sv, ve_h(G, prev))
                            elif G >= 1:
                                s.wait_ge(sv, ve_h(G - 1, prev + 32))
                            s.activation(sgb[:, 512 * (idx % 2):512 * (idx % 2) + 512],
                                         gps[:, 512 * (idx % 2):512 * (idx % 2) + 512],
                                         AF.Silu).then_inc(sa, 1)
                for tile in range(8):
                    s.wait_ge(sv, ve_ev(tq * 16 + 15, 2 * tile + 2))
                    s.activation(sqscr[:], ysb_t(tile), AF.Square,
                                 accum_out=lcol(tile, SSQS)).then_inc(sa, 1)
                    s.activation(sqscr[:], ysb_t(tile), AF.Copy,
                                 accum_out=lcol(tile, SSUM)).then_inc(sa, 1)
                    s.wait_ge(sv, ve_ln(tq, tile, 1))
                    s.activation(lcol(tile, SRSTD), lcol(tile, SVAR), AF.Ln).then_inc(sa, 1)
                    s.wait_ge(sa, sa_ln(tq, tile, 3))
                    s.activation(lcol(tile, SRSTD), lcol(tile, SRSTD), AF.Exp, scale=-0.5).then_inc(sa, 1)

        @blk.vector
        def _(v):
            for tq in range(2):
                for tile in range(8):
                    TT = 8 * tq + tile
                    v.wait_ge(stp, pe_b0(TT, 1))
                    v.tensor_copy(
                        x_fm[:].rearrange("p (kc t) -> p kc t", kc=8)[:, :, 128 * tile:128 * tile + 128],
                        p_tr[:].rearrange("p (kc t) -> p kc t", kc=8)).then_inc(sv, 1)
                    v.wait_ge(stp, pe_b0(TT, 2))
                    settle(v, v.reduce_max(ccol(tile, RMX), p_rg[:, 0:8], AX.X))
                    v.tensor_scalar_mul(ccol(tile, RMX), ccol(tile, RMX), -1.0)
                    settle(v, v.reduce_max(ccol(tile, GMX), p_rg[:, 8:16], AX.X))
                    v.tensor_scalar_mul(ccol(tile, GMX), ccol(tile, GMX), -10.0).then_inc(sv, 1)
                    v.wait_ge(sa, sa_b0(TT, 2))
                    settle(v, v.tensor_mul(pgb[:], eg[:, 0:8], eg[:, 8:16]))
                    settle(v, v.reciprocal(ccol(tile, CPROD), ccol(tile, RSUM)))
                    settle(v, v.reciprocal(ccol(tile, CWS), ccol(tile, GSUM)))
                    settle(v, v.tensor_mul(ccol(tile, CPROD), ccol(tile, CPROD), ccol(tile, CWS)))
                    settle(v, v.tensor_scalar_mul(pgb[:], pgb[:], ccol(tile, CPROD)))
                    settle(v, v.reduce_sum(ccol(tile, CWS), pgb[:], AX.X))
                    settle(v, v.tensor_scalar_add(ccol(tile, CWS), ccol(tile, CWS), 1e-9))
                    settle(v, v.reciprocal(ccol(tile, CRS), ccol(tile, CWS)))
                    v.tensor_scalar_mul(cw[:, 8 * tile:8 * tile + 8], pgb[:],
                                        ccol(tile, CRS)).then_inc(sv, 1)
                for e in range(8):
                    for fh in range(2):
                        G = tq * 16 + e * 2 + fh
                        for idx in range(32):
                            fc, ts = idx // 2, idx % 2
                            v.wait_ge(sa, sa_silu(G, idx))
                            v.tensor_mul(h_sl(fc, 512 * ts, 512),
                                         sgb[:, 512 * (idx % 2):512 * (idx % 2) + 512],
                                         ups[:, 512 * (idx % 2):512 * (idx % 2) + 512]).then_inc(sv, 1)
                        v.wait_ge(ly, 16 * 8 * (tq + 1))
                        for tt in range(8):
                            for ysl in range(2):
                                yidx = tt * 2 + ysl
                                v.wait_ge(stp, pe_y(G, yidx))
                                v.scalar_tensor_tensor(
                                    y_sb[:, 1024 * tt + 512 * ysl:1024 * tt + 512 * ysl + 512],
                                    yps[:, 512 * (yidx % 2):512 * (yidx % 2) + 512],
                                    cw_col(tt, e),
                                    y_sb[:, 1024 * tt + 512 * ysl:1024 * tt + 512 * ysl + 512],
                                    ALU.mult, ALU.add).then_inc(sv, 1)
                for tile in range(8):
                    v.wait_ge(sa, sa_ln(tq, tile, 2))
                    settle(v, v.tensor_scalar_mul(lcol(tile, SMU), lcol(tile, SSUM), 1.0 / 1024))
                    v.tensor_scalar_mul(lcol(tile, SVAR), lcol(tile, SSQS), 1.0 / 1024)
                    settle(v, v.tensor_mul(lcol(tile, SSQS), lcol(tile, SMU), lcol(tile, SMU)))
                    settle(v, v.tensor_sub(lcol(tile, SVAR), lcol(tile, SVAR), lcol(tile, SSQS)))
                    v.tensor_scalar_add(lcol(tile, SVAR), lcol(tile, SVAR), 1e-5).then_inc(sv, 1)
                    v.wait_ge(sa, sa_ln(tq, tile, 4))
                    if 8 * tq + tile >= 2:
                        TTs = 8 * tq + tile
                        v.wait_ge(sop[tile % 2], 16 * (4 * tq + tile // 2) if tile >= 2 else 16 * 4 * tq)
                    v.tensor_scalar(x2b[:, 1024 * (tile % 2):1024 * (tile % 2) + 1024],
                                    ysb_t(tile), lcol(tile, SMU), lcol(tile, SRSTD),
                                    ALU.subtract, ALU.mult).then_inc(sv, 1)
    return nc





D14 = [0] + [1 << j for j in range(13)]


def host_inputs_c1(x2_full, Wq, bq, Wk, bk, Wv, bv, core):
    b, q = core // 4, core % 4
    r0 = q * 2048
    return {
        "x2": np.ascontiguousarray(x2_full[b, r0:r0 + 2048], dtype=np.float32),
        "wqT": np.ascontiguousarray(Wq.T, dtype=np.float32),
        "wkT": np.ascontiguousarray(Wk.T, dtype=np.float32),
        "wvT": np.ascontiguousarray(Wv.T, dtype=np.float32),
        "bqkv": np.ascontiguousarray(np.stack([bq, bk, bv])[None], dtype=np.float32),
    }


def host_inputs_c2(x2_full, qfm_all, kfm_all, vtm_all, vsum_all, Wo, bo, core):
    b, q = core // 4, core % 4
    r0 = q * 2048
    kwin = np.zeros((1024, 6144), dtype=ml_dtypes.bfloat16)
    vg = np.zeros((6144, 1024), dtype=ml_dtypes.bfloat16)
    for j, qq in enumerate([q - 2, q - 1, q]):
        if qq >= 0:
            kwin[:, 2048 * j:2048 * (j + 1)] = kfm_all[4 * b + qq]
            vg[2048 * j:2048 * (j + 1), :] = vtm_all[4 * b + qq]
    vsum4 = np.concatenate([vsum_all[4 * b + j] for j in range(4)], axis=1)
    return {
        "qfm": np.ascontiguousarray(qfm_all[core]),
        "kwin": np.ascontiguousarray(kwin),
        "vg": np.ascontiguousarray(vg),
        "vsum4": np.ascontiguousarray(vsum4, dtype=np.float32),
        "x2": np.ascontiguousarray(x2_full[b, r0:r0 + 2048], dtype=np.float32),
        "woT": np.ascontiguousarray(Wo.T, dtype=np.float32),
        "bo": np.ascontiguousarray(bo[None], dtype=np.float32),
    }


def build_c1(nc):
    x2 = nc.dram_tensor("x2", [2048, 1024], F32, kind="ExternalInput").ap()
    wqT = nc.dram_tensor("wqT", [1024, 1024], F32, kind="ExternalInput").ap()
    wkT = nc.dram_tensor("wkT", [1024, 1024], F32, kind="ExternalInput").ap()
    wvT = nc.dram_tensor("wvT", [1024, 1024], F32, kind="ExternalInput").ap()
    bqkv = nc.dram_tensor("bqkv", [1, 3, 1024], F32, kind="ExternalInput").ap()
    qfm_o = nc.dram_tensor("qfm", [1024, 2048], BF16, kind="ExternalOutput").ap()
    kfm_o = nc.dram_tensor("kfm", [1024, 2048], BF16, kind="ExternalOutput").ap()
    vtm_o = nc.dram_tensor("vtm", [2048, 1024], BF16, kind="ExternalOutput").ap()
    vs_o = nc.dram_tensor("vsump", [1, 1024], F32, kind="ExternalOutput").ap()
    ident_c = nc.inline_tensor(np.eye(128, dtype=ml_dtypes.bfloat16), name="identc1").ap()
    ones_c = nc.inline_tensor(np.ones((1, 512), dtype=ml_dtypes.bfloat16), name="onesc1").ap()
    onescol_c = nc.inline_tensor(np.ones((128, 1), dtype=ml_dtypes.bfloat16), name="onescolc1").ap()
    onesrow_c = nc.inline_tensor(np.ones((1, 128), dtype=ml_dtypes.bfloat16), name="onesrowc1").ap()

    def pe_tile(i):
        return i + 1
    def pe_proj(j):
        return 16 + j + 1 if j < 64 else 81 + 2 * (j - 64)
    def pe_vs(j):
        return 82 + 2 * (j - 64)
    PE_END = 144
    def ve_tile(i):
        return i + 1
    def ve_proj(j):
        return 16 + j + 1
    VE_VS = 16 + 96 + 1

    with contextlib.ExitStack() as _st:
        ident = _st.enter_context(nc.sbuf_tensor("c_ident", [128, 128], BF16))
        ones_r = _st.enter_context(nc.sbuf_tensor("c_ones", [1, 512], BF16))
        ones_cl = _st.enter_context(nc.sbuf_tensor("c_onescol", [128, 1], BF16))
        ones_rw = _st.enter_context(nc.sbuf_tensor("c_onesrow", [1, 128], BF16))
        bqkv_b = _st.enter_context(nc.sbuf_tensor("c_bqkv", [1, 3 * 1024], BF16))
        wq_b = _st.enter_context(nc.sbuf_tensor("c_wq", [128, 8 * 1024], BF16))
        wk_b = _st.enter_context(nc.sbuf_tensor("c_wk", [128, 8 * 1024], BF16))
        wv_b = _st.enter_context(nc.sbuf_tensor("c_wv", [128, 8 * 1024], BF16))
        x_fm = _st.enter_context(nc.sbuf_tensor("c_xfm", [128, 8 * 2048], BF16))
        xbf = _st.enter_context(nc.sbuf_tensor("c_xbf", [128, 2 * 1024], BF16))
        ev = _st.enter_context(nc.sbuf_tensor("c_ev", [128, 2 * 512], BF16))
        vs_sb = _st.enter_context(nc.sbuf_tensor("c_vs", [1, 1024], F32))
        p_tr = _st.enter_context(nc.psum_tensor("c_ptr", [128, 1024], BF16))
        p_mm = _st.enter_context(nc.psum_tensor("c_pmm", [128, 2 * 512], F32))
        p_vs0 = _st.enter_context(nc.psum_tensor("c_pvs0", [1, 512], F32))
        p_vs1 = _st.enter_context(nc.psum_tensor("c_pvs1", [1, 512], F32))
        lc = _st.enter_context(nc.semaphore("c_lc"))
        lcg = _st.enter_context(nc.semaphore("c_lcg"))
        lx0 = _st.enter_context(nc.semaphore("c_lx0"))
        lx1 = _st.enter_context(nc.semaphore("c_lx1"))
        so0 = _st.enter_context(nc.semaphore("c_so0"))
        so1 = _st.enter_context(nc.semaphore("c_so1"))
        sv = _st.enter_context(nc.semaphore("c_sv"))
        stp = _st.enter_context(nc.semaphore("c_stp"))
        blk = _st.enter_context(nc.Block())
        lxp, sop = [lx0, lx1], [so0, so1]

        @blk.sync
        def _(sync):
            sync.dma_start(out=ident[:], in_=ident_c).then_inc(lc, 16)
            sync.dma_start(out=ones_r[:], in_=ones_c).then_inc(lc, 16)
            sync.dma_start(out=ones_cl[:], in_=onescol_c).then_inc(lc, 16)
            sync.dma_start(out=ones_rw[:], in_=onesrow_c).then_inc(lc, 16)
            for j in range(96):
                sync.wait_ge(sv, ve_proj(j))
                par = j % 2
                if j < 32:
                    oc, ts = j // 4, j % 4
                    dst = qfm_o[128 * oc:128 * oc + 128, 512 * ts:512 * ts + 512]
                elif j < 64:
                    oc, ts = (j - 32) // 4, (j - 32) % 4
                    dst = kfm_o[128 * oc:128 * oc + 128, 512 * ts:512 * ts + 512]
                else:
                    tile, os_ = (j - 64) // 2, (j - 64) % 2
                    dst = vtm_o[128 * tile:128 * tile + 128, 512 * os_:512 * os_ + 512]
                sync.dma_start(out=dst, in_=ev[:, 512 * par:512 * par + 512]).then_inc(sop[par], 16)
            sync.wait_ge(sv, VE_VS)
            sync.dma_start(out=vs_o, in_=vs_sb[:]).then_inc(so0, 16)
            sync.wait_ge(so0, 16 * 49)
            sync.wait_ge(so1, 16 * 48)

        @blk.gpsimd
        def _(g):
            g.dma_start(out=bqkv_b[:].rearrange("p (a f) -> p a f", a=3),
                        in_=bqkv).then_inc(lcg, 16)
            for w_b, w_in in ((wq_b, wqT), (wk_b, wkT), (wv_b, wvT)):
                g.dma_start(out=w_b[:].rearrange("p (ic o) -> p ic o", ic=8),
                            in_=w_in.rearrange("(ic p) o -> p ic o", p=128)).then_inc(lcg, 16)
            for i in range(16):
                if i >= 2:
                    g.wait_ge(stp, pe_tile(i - 2))
                g.dma_start(out=xbf[:, 1024 * (i % 2):1024 * (i % 2) + 1024],
                            in_=x2[128 * i:128 * i + 128, :]).then_inc(lxp[i % 2], 16)

        @blk.tensor
        def _(t):
            t.wait_ge(lc, 64)
            t.wait_ge(lcg, 64)
            for i in range(16):
                t.wait_ge(lxp[i % 2], 16 * (i // 2 + 1))
                if i >= 1:
                    t.wait_ge(sv, ve_tile(i - 1))
                for fc in range(8):
                    ins = t.transpose(p_tr[:, 128 * fc:128 * fc + 128],
                                      xbf[:, 1024 * (i % 2) + 128 * fc:1024 * (i % 2) + 128 * fc + 128],
                                      ident[:])
                    if fc == 7:
                        ins.then_inc(stp, 1)
            for j in range(96):
                par = j % 2
                if j >= 2:
                    t.wait_ge(sv, ve_proj(j - 2))
                if j < 64:
                    isq = j < 32
                    oc, ts = (j % 32) // 4, (j % 32) % 4
                    wb = wq_b if isq else wk_b
                    for ic in range(8):
                        t.matmul(p_mm[:, 512 * par:512 * par + 512],
                                 wb[:, 1024 * ic + 128 * oc:1024 * ic + 128 * oc + 128],
                                 x_fm[:, 2048 * ic + 512 * ts:2048 * ic + 512 * ts + 512],
                                 start=(ic == 0), stop=False)
                    t.matmul(p_mm[:, 512 * par:512 * par + 512],
                             bqkv_b[0:1, 1024 * (0 if isq else 1) + 128 * oc:
                                    1024 * (0 if isq else 1) + 128 * oc + 128],
                             ones_r[:], start=False, stop=True).then_inc(stp, 1)
                else:
                    tile, os_ = (j - 64) // 2, (j - 64) % 2
                    for ic in range(8):
                        t.matmul(p_mm[:, 512 * par:512 * par + 512],
                                 x_fm[:, 2048 * ic + 128 * tile:2048 * ic + 128 * tile + 128],
                                 wv_b[:, 1024 * ic + 512 * os_:1024 * ic + 512 * os_ + 512],
                                 start=(ic == 0), stop=False)
                    t.matmul(p_mm[:, 512 * par:512 * par + 512],
                             ones_rw[:],
                             bqkv_b[0:1, 2048 + 512 * os_:2048 + 512 * os_ + 512],
                             start=False, stop=True).then_inc(stp, 1)
                    t.wait_ge(sv, ve_proj(j))
                    pv = p_vs0 if os_ == 0 else p_vs1
                    t.matmul(pv[:], ones_cl[:], ev[:, 512 * par:512 * par + 512],
                             start=(tile == 0), stop=(tile == 15)).then_inc(stp, 1)

        @blk.vector
        def _(v):
            for i in range(16):
                v.wait_ge(stp, pe_tile(i))
                v.tensor_copy(
                    x_fm[:].rearrange("p (ic t) -> p ic t", ic=8)[:, :, 128 * i:128 * i + 128],
                    p_tr[:].rearrange("p (ic t) -> p ic t", ic=8)).then_inc(sv, 1)
            for j in range(96):
                par = j % 2
                v.wait_ge(stp, pe_proj(j))
                if j >= 2:
                    v.wait_ge(sop[par], 16 * (j // 2))
                    if j - 2 >= 64:
                        v.wait_ge(stp, pe_vs(j - 2))
                v.tensor_copy(ev[:, 512 * par:512 * par + 512],
                              p_mm[:, 512 * par:512 * par + 512]).then_inc(sv, 1)
            v.wait_ge(stp, PE_END)
            v.tensor_copy(vs_sb[:, 0:512], p_vs0[:])
            v.tensor_copy(vs_sb[:, 512:1024], p_vs1[:]).then_inc(sv, 1)

    return nc




def build_c2(nc):
    qfm_i = nc.dram_tensor("qfm", [1024, 2048], BF16, kind="ExternalInput").ap()
    kwin_i = nc.dram_tensor("kwin", [1024, 6144], BF16, kind="ExternalInput").ap()
    vg_i = nc.dram_tensor("vg", [6144, 1024], BF16, kind="ExternalInput").ap()
    vs4_i = nc.dram_tensor("vsum4", [1, 4096], F32, kind="ExternalInput").ap()
    x2_i = nc.dram_tensor("x2", [2048, 1024], F32, kind="ExternalInput").ap()
    woT_i = nc.dram_tensor("woT", [1024, 1024], F32, kind="ExternalInput").ap()
    bo_i = nc.dram_tensor("bo", [1, 1024], F32, kind="ExternalInput").ap()
    x3_o = nc.dram_tensor("x3", [2048, 1024], F32, kind="ExternalOutput").ap()
    ident_c = nc.inline_tensor(np.eye(128, dtype=ml_dtypes.bfloat16), name="identc2").ap()
    onesrow_c = nc.inline_tensor(np.ones((1, 128), dtype=ml_dtypes.bfloat16), name="onesrowc2").ap()
    hs = np.zeros((128, 128), dtype=ml_dtypes.bfloat16)
    for c in range(8):
        hs[0:64, 16 * c + 2 * c] = 1
        hs[64:128, 16 * c + 2 * c + 1] = 1
    hsel_c = nc.inline_tensor(np.ascontiguousarray(hs), name="hselc2").ap()

    C_ND = 14
    def pe_sc(di, c):
        return 9 * di + c + 1
    def pe_sct(di):
        return 9 * di + 9
    PE_SC = 9 * C_ND
    PE_VSB = PE_SC + 1
    def pe_op(qt, k):
        return PE_VSB + 3 * qt + k
    def ve_prod(di, c):
        return 1 + 10 * di + c + 1
    def ve_wev(di):
        return 1 + 10 * di + 9
    def ve_wtmev(di):
        return 1 + 10 * di + 10
    VE_SC = 1 + 10 * C_ND
    VE_VSB = VE_SC + 1
    def ve_qt(qt, k):
        return VE_VSB + 20 * qt + k
    def sa_exp(di, ts):
        return 4 * di + ts + 1
    SA_SC = 4 * C_ND
    def sa_qt(qt, k):
        return SA_SC + 5 * qt + k

    with contextlib.ExitStack() as _st:
        ident = _st.enter_context(nc.sbuf_tensor("d_ident", [128, 128], BF16))
        ones_rw = _st.enter_context(nc.sbuf_tensor("d_onesrow", [1, 128], BF16))
        hsel = _st.enter_context(nc.sbuf_tensor("d_hsel", [128, 128], BF16))
        qfm = _st.enter_context(nc.sbuf_tensor("d_qfm", [128, 8 * 2048], BF16))
        kwin = _st.enter_context(nc.sbuf_tensor("d_kwin", [128, 8 * 6144], BF16))
        woT = _st.enter_context(nc.sbuf_tensor("d_woT", [128, 8 * 1024], BF16))
        bo_b = _st.enter_context(nc.sbuf_tensor("d_bo", [1, 1024], BF16))
        vs4 = _st.enter_context(nc.sbuf_tensor("d_vs4", [1, 1024], F32))
        vsrow = _st.enter_context(nc.sbuf_tensor("d_vsrow", [1, 1024], BF16))
        vsb = _st.enter_context(nc.sbuf_tensor("d_vsb", [128, 1024], BF16))
        prodb = _st.enter_context(nc.sbuf_tensor("d_prod", [128, 2 * 2048], BF16))
        wscb = _st.enter_context(nc.sbuf_tensor("d_wscb", [16, 2048], BF16))
        wtm = _st.enter_context(nc.sbuf_tensor("d_wtm", [128, 16 * 224], BF16))
        zc = _st.enter_context(nc.sbuf_tensor("d_zc", [128, 16 * 16], F32))
        ztmp = _st.enter_context(nc.sbuf_tensor("d_ztmp", [128, 16], F32))
        vshb = _st.enter_context(nc.sbuf_tensor("d_vsh", [128, 2 * 1024], BF16))
        acc = _st.enter_context(nc.sbuf_tensor("d_acc", [128, 2 * 1024], F32))
        tmpb = _st.enter_context(nc.sbuf_tensor("d_tmp", [128, 1024], BF16))
        obf = _st.enter_context(nc.sbuf_tensor("d_obf", [128, 1024], BF16))
        ofm = _st.enter_context(nc.sbuf_tensor("d_ofm", [128, 1024], BF16))
        x2t = _st.enter_context(nc.sbuf_tensor("d_x2t", [128, 1024], F32))
        res = _st.enter_context(nc.sbuf_tensor("d_res", [128, 1024], F32))
        lncols = _st.enter_context(nc.sbuf_tensor("d_lncols", [128, 16 * 16], F32))
        x3b = _st.enter_context(nc.sbuf_tensor("d_x3b", [128, 1024], F32))
        p_pool = _st.enter_context(nc.psum_tensor("d_pool", [128, 2048], F32))
        p_tr = _st.enter_context(nc.psum_tensor("d_ptr", [128, 1024], BF16))
        p_s = p_pool[0:16, :]
        p_vs = p_pool[:, 0:1024]
        p_o = p_pool[:, 0:1024]
        p_scr = p_pool[:, 1024:2048]
        lc = _st.enter_context(nc.semaphore("d_lc"))
        lcg = _st.enter_context(nc.semaphore("d_lcg"))
        lv0 = _st.enter_context(nc.semaphore("d_lv0"))
        lv1 = _st.enter_context(nc.semaphore("d_lv1"))
        lx = _st.enter_context(nc.semaphore("d_lx"))
        so = _st.enter_context(nc.semaphore("d_so"))
        lvs = _st.enter_context(nc.semaphore("d_lvs"))
        svs = _st.enter_context(nc.semaphore("d_svs"))
        sa = _st.enter_context(nc.semaphore("d_sa"))
        sv = _st.enter_context(nc.semaphore("d_sv"))
        stp = _st.enter_context(nc.semaphore("d_stp"))
        sgv = _st.enter_context(nc.semaphore("d_sgv"))
        sga = _st.enter_context(nc.semaphore("d_sga"))
        blk = _st.enter_context(nc.Block())
        lvp = [lv0, lv1]
        sgc = {"v": 0, "a": 0}
        def settle(eng, ins, which="v"):
            gsem = {"v": sgv, "a": sga}[which]
            sgc[which] += 1
            ins.then_inc(gsem, 1)
            eng.wait_ge(gsem, sgc[which])

        SMU, SVAR, SSQS, SSUM, SRSTD = range(5)
        def lcol(qt, j):
            return lncols[:, 16 * qt + j:16 * qt + j + 1]
        def wtm_d(qt, di):
            return wtm[:, 224 * qt + 16 * di:224 * qt + 16 * di + 16]

        @blk.sync
        def _(sync):
            sync.dma_start(out=ident[:], in_=ident_c).then_inc(lc, 16)
            sync.dma_start(out=ones_rw[:], in_=onesrow_c).then_inc(lc, 16)
            sync.dma_start(out=hsel[:], in_=hsel_c).then_inc(lc, 16)
            sync.dma_start(out=qfm[:].rearrange("p (c t) -> p c t", c=8),
                           in_=qfm_i.rearrange("(c p) t -> p c t", p=128)).then_inc(lc, 16)
            sync.dma_start(out=kwin[:].rearrange("p (c t) -> p c t", c=8),
                           in_=kwin_i.rearrange("(c p) t -> p c t", p=128)).then_inc(lc, 16)

            for qt in range(16):
                for di, d in enumerate(D14):
                    gi = qt * C_ND + di
                    par = gi % 2
                    if gi >= 2:
                        pqt, pdi = (gi - 2) // C_ND, (gi - 2) % C_ND
                        sync.wait_ge(sv, ve_qt(pqt, 2 + pdi))
                    sync.dma_start(out=vshb[:, 1024 * par:1024 * par + 1024],
                                   in_=vg_i[4096 + 128 * qt - d:4096 + 128 * qt - d + 128, :]
                                   ).then_inc(lvp[par], 16)
                if qt >= 1:
                    sync.wait_ge(sv, ve_qt(qt - 1, 18))
                sync.dma_start(out=x2t[:],
                               in_=x2_i[128 * qt:128 * qt + 128, :]).then_inc(lx, 16)
                if qt >= 1:
                    sync.wait_ge(sv, ve_qt(qt - 1, 20))
                    sync.dma_start(out=x3_o[128 * (qt - 1):128 * qt, :],
                                   in_=x3b[:]).then_inc(so, 16)
            sync.wait_ge(sv, ve_qt(15, 20))
            sync.dma_start(out=x3_o[128 * 15:128 * 16, :], in_=x3b[:]).then_inc(so, 16)
            sync.wait_ge(so, 16 * 16)

        @blk.gpsimd
        def _(g):
            g.dma_start(out=woT[:].rearrange("p (ic o) -> p ic o", ic=8),
                        in_=woT_i.rearrange("(ic p) o -> p ic o", p=128)).then_inc(lcg, 16)
            g.dma_start(out=bo_b[:], in_=bo_i).then_inc(lcg, 16)
            for pp in range(4):
                g.wait_ge(lvs, 16 * pp)
                g.dma_start(out=vs4[:], in_=vs4_i[:, 1024 * pp:1024 * pp + 1024],
                            accum_op=(ALU.bypass if pp == 0 else ALU.add)).then_inc(lvs, 16)

        @blk.tensor
        def _(t):
            t.wait_ge(lc, 16 * 5)
            t.wait_ge(lcg, 32)
            for di, d in enumerate(D14):
                for c in range(8):
                    t.wait_ge(sv, ve_prod(di, c))
                    if c == 0 and di >= 1:
                        t.wait_ge(sv, ve_wev(di - 1))
                    for ts in range(4):
                        ins = t.matmul(p_s[:, 512 * ts:512 * ts + 512],
                                       hsel[:, 16 * c:16 * c + 16],
                                       prodb[:, 2048 * (c % 2) + 512 * ts:2048 * (c % 2) + 512 * ts + 512],
                                       start=(c == 0), stop=(c == 7))
                        if ts == 3:
                            ins.then_inc(stp, 1)
                t.wait_ge(sv, ve_wev(di))
                if di >= 1:
                    t.wait_ge(sv, ve_wtmev(di - 1))
                for qt in range(16):
                    ins = t.transpose(p_tr[:, 16 * qt:16 * qt + 16],
                                      wscb[0:16, 128 * qt:128 * qt + 128],
                                      ident[0:16, 0:16])
                    if qt == 15:
                        ins.then_inc(stp, 1)
            t.wait_ge(sv, 1)
            t.matmul(p_vs[:, 0:512], ones_rw[:], vsrow[0:1, 0:512], start=True, stop=True)
            t.matmul(p_vs[:, 512:1024], ones_rw[:], vsrow[0:1, 512:1024],
                     start=True, stop=True).then_inc(stp, 1)
            for qt in range(16):
                t.wait_ge(sa, sa_qt(qt, 1))
                if qt >= 1:
                    t.wait_ge(sv, ve_qt(qt - 1, 17))
                else:
                    t.wait_ge(sv, ve_wtmev(C_ND - 1))
                for fc in range(8):
                    ins = t.transpose(p_tr[:, 128 * fc:128 * fc + 128],
                                      obf[:, 128 * fc:128 * fc + 128], ident[:])
                    if fc == 7:
                        ins.then_inc(stp, 1)
                t.wait_ge(sv, ve_qt(qt, 17))
                if qt >= 1:
                    t.wait_ge(sv, ve_qt(qt - 1, 18))
                for os_ in range(2):
                    for ic in range(8):
                        t.matmul(p_o[:, 512 * os_:512 * os_ + 512],
                                 ofm[:, 128 * ic:128 * ic + 128],
                                 woT[:, 1024 * ic + 512 * os_:1024 * ic + 512 * os_ + 512],
                                 start=(ic == 0), stop=False)
                    ins = t.matmul(p_o[:, 512 * os_:512 * os_ + 512],
                                   ones_rw[:], bo_b[0:1, 512 * os_:512 * os_ + 512],
                                   start=False, stop=True)
                    if os_ == 1:
                        ins.then_inc(stp, 1)
                    else:
                        ins.then_inc(stp, 1)

        @blk.scalar
        def _(s):
            for di, d in enumerate(D14):
                for ts in range(4):
                    if ts == 0:
                        s.wait_ge(stp, pe_sc(di, 7))
                    s.activation(p_s[:, 512 * ts:512 * ts + 512],
                                 p_s[:, 512 * ts:512 * ts + 512], AF.Exp,
                                 scale=0.125).then_inc(sa, 1)
            for qt in range(16):
                s.wait_ge(sv, ve_qt(qt, 16))
                if qt >= 1:
                    s.wait_ge(stp, pe_op(qt - 1, 1))
                s.activation(obf[:], acc[:, 1024 * (qt % 2):1024 * (qt % 2) + 1024],
                             AF.Copy).then_inc(sa, 1)
                s.wait_ge(sv, ve_qt(qt, 18))
                s.activation(p_scr, res[:], AF.Square, accum_out=lcol(qt, SSQS)).then_inc(sa, 1)
                s.activation(p_scr, res[:], AF.Copy, accum_out=lcol(qt, SSUM)).then_inc(sa, 1)
                s.wait_ge(sv, ve_qt(qt, 19))
                s.activation(lcol(qt, SRSTD), lcol(qt, SVAR), AF.Ln).then_inc(sa, 1)
                s.wait_ge(sa, sa_qt(qt, 4))
                s.activation(lcol(qt, SRSTD), lcol(qt, SRSTD), AF.Exp, scale=-0.5).then_inc(sa, 1)

        @blk.vector
        def _(v):
            v.wait_ge(lc, 16 * 5)
            v.wait_ge(lvs, 64)
            v.tensor_copy(vsrow[:], vs4[:]).then_inc(sv, 1)
            for di, d in enumerate(D14):
                for c in range(8):
                    if di * 8 + c >= 2:
                        gi = di * 8 + c - 2
                        v.wait_ge(stp, pe_sc(gi // 8, gi % 8))
                    v.tensor_mul(prodb[:, 2048 * (c % 2):2048 * (c % 2) + 2048],
                                 qfm[:, 2048 * c:2048 * c + 2048],
                                 kwin[:, 6144 * c + 4096 - d:6144 * c + 4096 - d + 2048]
                                 ).then_inc(sv, 1)
                v.wait_ge(sa, sa_exp(di, 3))
                if di >= 1:
                    v.wait_ge(stp, pe_sct(di - 1))
                v.tensor_scalar_add(wscb[0:16, 0:2048],
                                    p_s.rearrange("p (ts f) -> p ts f", ts=4).rearrange("p ts f -> p (ts f)"),
                                    -1.0).then_inc(sv, 1)
                v.wait_ge(stp, pe_sct(di))
                v.tensor_copy(
                    wtm[:].rearrange("p (qt x) -> p qt x", qt=16)[:, :, 16 * di:16 * di + 16],
                    p_tr[:, 0:256].rearrange("p (qt x) -> p qt x", qt=16)).then_inc(sv, 1)
            v.wait_ge(stp, PE_VSB)
            v.tensor_copy(vsb[:], p_vs).then_inc(sv, 1)
            for qt in range(16):
                settle(v, v.tensor_reduce(
                    ztmp[:].unsqueeze(2),
                    wtm[:, 224 * qt:224 * qt + 224].rearrange("p (dd h) -> p h dd", dd=C_ND),
                    AX.X, ALU.add))
                settle(v, v.tensor_scalar_add(ztmp[:], ztmp[:], 8192.0))
                settle(v, v.reciprocal(ztmp[:], ztmp[:]))
                v.tensor_copy(zc[:, 16 * qt:16 * qt + 16], ztmp[:]).then_inc(sv, 1)
                for di, d in enumerate(D14):
                    gi = qt * C_ND + di
                    par = gi % 2
                    v.wait_ge(lvp[par], 16 * (gi // 2 + 1))
                    if di == 0 and qt >= 2:
                        v.wait_ge(sa, sa_qt(qt - 2, 1))
                    v.tensor_mul(tmpb[:].rearrange("p (h f) -> p h f", h=16),
                                 vshb[:, 1024 * par:1024 * par + 1024].rearrange("p (h f) -> p h f", h=16),
                                 wtm_d(qt, di).unsqueeze(2).broadcast_to([128, 16, 64]))
                    if di == 0:
                        v.tensor_add(acc[:, 1024 * (qt % 2):1024 * (qt % 2) + 1024],
                                     vsb[:], tmpb[:]).then_inc(sv, 1)
                    else:
                        v.tensor_add(acc[:, 1024 * (qt % 2):1024 * (qt % 2) + 1024],
                                     acc[:, 1024 * (qt % 2):1024 * (qt % 2) + 1024],
                                     tmpb[:]).then_inc(sv, 1)
                v.tensor_mul(acc[:, 1024 * (qt % 2):1024 * (qt % 2) + 1024].rearrange("p (h f) -> p h f", h=16),
                             acc[:, 1024 * (qt % 2):1024 * (qt % 2) + 1024].rearrange("p (h f) -> p h f", h=16),
                             zc[:, 16 * qt:16 * qt + 16].unsqueeze(2).broadcast_to([128, 16, 64])
                             ).then_inc(sv, 1)
                v.wait_ge(stp, pe_op(qt, 1))
                v.tensor_copy(ofm[:], p_tr[:]).then_inc(sv, 1)
                v.wait_ge(stp, pe_op(qt, 3))
                v.wait_ge(lx, 16 * (qt + 1))
                if qt >= 1:
                    v.wait_ge(sa, sa_qt(qt - 1, 3))
                v.tensor_add(res[:], p_o, x2t[:]).then_inc(sv, 1)
                v.wait_ge(sa, sa_qt(qt, 3))
                settle(v, v.tensor_scalar_mul(lcol(qt, SMU), lcol(qt, SSUM), 1.0 / 1024))
                v.tensor_scalar_mul(lcol(qt, SVAR), lcol(qt, SSQS), 1.0 / 1024)
                settle(v, v.tensor_mul(lcol(qt, SSQS), lcol(qt, SMU), lcol(qt, SMU)))
                settle(v, v.tensor_sub(lcol(qt, SVAR), lcol(qt, SVAR), lcol(qt, SSQS)))
                v.tensor_scalar_add(lcol(qt, SVAR), lcol(qt, SVAR), 1e-5).then_inc(sv, 1)
                v.wait_ge(sa, sa_qt(qt, 5))
                if qt >= 1:
                    v.wait_ge(so, 16 * qt)
                v.tensor_scalar(x3b[:],
                                res[:], lcol(qt, SMU), lcol(qt, SRSTD),
                                ALU.subtract, ALU.mult).then_inc(sv, 1)

    return nc





D_NE, D_NA, D_NO = 24, 17, 16


def rotary_tables(r0):
    t = np.arange(r0 - 1024, r0 + 2048, dtype=np.float64)
    half = np.arange(0, 64, 2, dtype=np.float64)
    inv_freq = 1.0 / 10000.0 ** (half / 64.0)
    fr = t[:, None] * inv_freq[None, :]
    freqs = np.concatenate([fr, fr], axis=1)
    base = (half + 0.4 * 64) / (1.4 * 64)
    power = (t - 4096.0) / 256.0
    sc = base[None, :] ** power[:, None]
    sc = np.concatenate([sc, sc], axis=1)
    cos, sin = np.cos(freqs), np.sin(freqs)
    sgn = np.concatenate([-np.ones(32), np.ones(32)])
    T1q = 8.0 * cos * sc
    T2q = 8.0 * sin * sc * sgn[None, :]
    T1k = cos / sc
    T2k = (sin / sc) * sgn[None, :]
    def rep(a):
        return np.ascontiguousarray(np.tile(a, (1, 8)), dtype=np.float32)
    return rep(T1q), rep(T2q), rep(T1k), rep(T2k)


def make_masks(r0):
    m = np.zeros((17, 128, 640), dtype=np.float64)
    i_idx = r0 - 128 + 128 * np.arange(17)[:, None, None] + np.arange(128)[None, :, None]
    j_idx = r0 - 640 + 128 * np.arange(17)[:, None, None] + np.arange(640)[None, None, :]
    bad = (j_idx < i_idx - 512) | (j_idx > i_idx) | (j_idx < 0) | (i_idx < 0)
    m[bad] = -1e30
    return m.astype(ml_dtypes.bfloat16)


def host_inputs_d(xin_full, z, l, core, x3own_full=None):
    b, q = core // 4, core % 4
    r0 = q * 2048
    lo = r0 - 1152
    sl = xin_full[b, max(0, lo):r0 + 2048]
    xin = np.pad(sl, ((max(0, -lo), 0), (0, 0)))
    T1q, T2q, T1k, T2k = rotary_tables(r0)
    out = {
        "xin": np.ascontiguousarray(xin, dtype=np.float32),
        "wqkv": np.ascontiguousarray(z["L1_ffn_wqkv"][l], dtype=np.float32),
        "wo": np.ascontiguousarray(z["L1_ffn_wo"][l], dtype=np.float32),
        "lng": np.ascontiguousarray(z["L1_ffn_ln_g"][l][None], dtype=np.float32),
        "lnb": np.ascontiguousarray(z["L1_ffn_ln_b"][l][None], dtype=np.float32),
        "w1": np.ascontiguousarray(z["L1_ffn_w1"][l], dtype=np.float32),
        "b1": np.ascontiguousarray(z["L1_ffn_b1"][l][None], dtype=np.float32),
        "w2": np.ascontiguousarray(z["L1_ffn_w2"][l], dtype=np.float32),
        "b2": np.ascontiguousarray(z["L1_ffn_b2"][l][None], dtype=np.float32),
        "qs512": np.ascontiguousarray(np.tile(z["L1_ffn_q_scale"][l], 8)[None], dtype=np.float32),
        "ks512": np.ascontiguousarray(np.tile(z["L1_ffn_k_scale"][l], 8)[None], dtype=np.float32),
        "T1q": T1q, "T2q": T2q, "T1k": T1k, "T2k": T2k,
        "masks": make_masks(r0),
    }
    if x3own_full is not None:
        out["x3own"] = np.ascontiguousarray(x3own_full[b, r0:r0 + 2048], dtype=np.float32)
    return out


def w1_slice_parts(s):
    ns = 340 if s == 10 else 512
    c0, c1 = 512 * s, 512 * s + ns
    a = (0, min(c1, 2730) - c0) if c0 < 2730 else None
    if a is not None and a[1] <= a[0]:
        a = None
    g = (max(c0, 2730) - c0, c1 - c0) if c1 > 2730 else None
    if g is not None and g[1] <= g[0]:
        g = None
    return ns, a, g


def build_d(nc, final):
    xin = nc.dram_tensor("xin", [3200, 1024], F32, kind="ExternalInput").ap()
    wqkv_i = nc.dram_tensor("wqkv", [1024, 1536], F32, kind="ExternalInput").ap()
    wo_i = nc.dram_tensor("wo", [512, 1024], F32, kind="ExternalInput").ap()
    lng_i = nc.dram_tensor("lng", [1, 1024], F32, kind="ExternalInput").ap()
    lnb_i = nc.dram_tensor("lnb", [1, 1024], F32, kind="ExternalInput").ap()
    w1_i = nc.dram_tensor("w1", [1024, 5460], F32, kind="ExternalInput").ap()
    b1_i = nc.dram_tensor("b1", [1, 5460], F32, kind="ExternalInput").ap()
    w2_i = nc.dram_tensor("w2", [2730, 1024], F32, kind="ExternalInput").ap()
    b2_i = nc.dram_tensor("b2", [1, 1024], F32, kind="ExternalInput").ap()
    qs_i = nc.dram_tensor("qs512", [1, 512], F32, kind="ExternalInput").ap()
    ks_i = nc.dram_tensor("ks512", [1, 512], F32, kind="ExternalInput").ap()
    t1q_i = nc.dram_tensor("T1q", [3072, 512], F32, kind="ExternalInput").ap()
    t2q_i = nc.dram_tensor("T2q", [3072, 512], F32, kind="ExternalInput").ap()
    t1k_i = nc.dram_tensor("T1k", [3072, 512], F32, kind="ExternalInput").ap()
    t2k_i = nc.dram_tensor("T2k", [3072, 512], F32, kind="ExternalInput").ap()
    masks_i = nc.dram_tensor("masks", [17, 128, 640], BF16, kind="ExternalInput").ap()
    x3own_i = (nc.dram_tensor("x3own", [2048, 1024], F32, kind="ExternalInput").ap()
               if final else None)
    y_o = nc.dram_tensor("y", [2048, 1024], F32, kind="ExternalOutput").ap()
    x4scr = nc.dram_tensor("x4scr", [17 * 128, 1024], F32, kind="Internal").ap()
    ident_c = nc.inline_tensor(np.eye(128, dtype=ml_dtypes.bfloat16), name="identd").ap()
    onesrow_c = nc.inline_tensor(np.ones((1, 128), dtype=ml_dtypes.bfloat16), name="onesrowd").ap()

    def pe_xt(i): return i + 1
    PE_QSB = 26
    def pe_mm(j, sl): return 26 + 4 * j + sl + 1
    def pe_tr2(j): return 26 + 4 * j + 4
    PE_D2 = 26 + 96
    def ve_xt(i): return i + 1
    VE_QSB = 26
    def ve_vev(j): return 26 + 4 * j + 1
    def ve_qproc(j): return 26 + 4 * j + 2
    def ve_kproc(j): return 26 + 4 * j + 3
    def ve_trev(j): return 26 + 4 * j + 4
    VE_D2 = 26 + 96
    def sa_d2(j, k): return 6 * j + k
    SA_D2 = 6 * D_NE

    def pe_sim(qt, h): return PE_D2 + 26 * qt + 3 * h + 1
    def pe_antr(qt, h): return PE_D2 + 26 * qt + 3 * h + 2
    def pe_ov(qt, h): return PE_D2 + 26 * qt + 3 * h + 3
    def pe_otr(qt): return PE_D2 + 26 * qt + 25
    def pe_wo(qt): return PE_D2 + 26 * qt + 26
    PE_D3 = PE_D2 + 26 * D_NA
    def ve_smb(qt, h): return VE_D2 + 27 * qt + 3 * h + 1
    def ve_an(qt, h): return VE_D2 + 27 * qt + 3 * h + 2
    def ve_anT(qt, h): return VE_D2 + 27 * qt + 3 * h + 3
    def ve_oev(qt): return VE_D2 + 27 * qt + 25
    def ve_ofm(qt): return VE_D2 + 27 * qt + 26
    def ve_res(qt): return VE_D2 + 27 * qt + 27
    VE_D3 = VE_D2 + 27 * D_NA
    def sa_eb(qt, h): return SA_D2 + 8 * qt + h + 1
    SA_D3 = SA_D2 + 8 * D_NA

    PE_GB = PE_D3 + 1
    PE_BB = PE_D3 + 2
    def pe4(i, k): return PE_D3 + 2 + 17 * i + k
    VT = 16 if final else 15
    VE_GB = VE_D3 + 1
    VE_BB = VE_D3 + 2
    def ve4(i, k): return VE_D3 + 2 + VT * i + k
    AT = 18 if final else 12
    def sa4(i, k): return SA_D3 + AT * i + k

    with contextlib.ExitStack() as _st:
        blk = _st.enter_context(nc.Block())
        ident = _st.enter_context(nc.sbuf_tensor("e_ident", [128, 128], BF16))
        ones_rw = _st.enter_context(nc.sbuf_tensor("e_onesrow", [1, 128], BF16))
        lc = _st.enter_context(nc.semaphore("e_lc"))
        lcg = _st.enter_context(nc.semaphore("e_lcg"))
        lt0 = _st.enter_context(nc.semaphore("e_lt0"))
        lt1 = _st.enter_context(nc.semaphore("e_lt1"))
        lx0 = _st.enter_context(nc.semaphore("e_lx0"))
        lx1 = _st.enter_context(nc.semaphore("e_lx1"))
        lm0 = _st.enter_context(nc.semaphore("e_lm0"))
        lm1 = _st.enter_context(nc.semaphore("e_lm1"))
        so = _st.enter_context(nc.semaphore("e_so"))
        sx4 = _st.enter_context(nc.semaphore("e_sx4"))
        sa = _st.enter_context(nc.semaphore("e_sa"))
        sv = _st.enter_context(nc.semaphore("e_sv"))
        stp = _st.enter_context(nc.semaphore("e_stp"))
        sgv = _st.enter_context(nc.semaphore("e_sgv"))
        sga = _st.enter_context(nc.semaphore("e_sga"))
        ltp, lxp, lmp = [lt0, lt1], [lx0, lx1], [lm0, lm1]
        p_pool = _st.enter_context(nc.psum_tensor("e_pool", [128, 2048], F32))
        p_tr = _st.enter_context(nc.psum_tensor("e_ptr", [128, 1024], BF16))
        sgc = {"v": 0, "a": 0}
        def settle(eng, ins, which="v"):
            gsem = {"v": sgv, "a": sga}[which]
            sgc[which] += 1
            ins.then_inc(gsem, 1)
            eng.wait_ge(gsem, sgc[which])

        cap = {}

        with contextlib.ExitStack() as stA:
            qfm = stA.enter_context(nc.sbuf_tensor("e_qfm", [128, 4 * 3072], BF16))
            kfm = stA.enter_context(nc.sbuf_tensor("e_kfm", [128, 4 * 3072], BF16))
            vtm = stA.enter_context(nc.sbuf_tensor("e_vtm", [128, 24 * 512], BF16))

            with contextlib.ExitStack() as st2:
                xe_fm = st2.enter_context(nc.sbuf_tensor("e_xefm", [128, 8 * 3200], BF16))
                wqkvb = st2.enter_context(nc.sbuf_tensor("e_wqkv", [128, 8 * 1536], BF16))
                qsb = st2.enter_context(nc.sbuf_tensor("e_qsb", [128, 1024], F32))
                qsrow = st2.enter_context(nc.sbuf_tensor("e_qsrow", [1, 1024], BF16))
                tbl = st2.enter_context(nc.sbuf_tensor("e_tbl", [128, 2 * 2048], F32))
                xbf = st2.enter_context(nc.sbuf_tensor("e_xbf", [128, 2 * 1024], BF16))
                sqb = st2.enter_context(nc.sbuf_tensor("e_sqb", [128, 1024], F32))
                ncol = st2.enter_context(nc.sbuf_tensor("e_ncol", [128, 16], F32))
                qn = st2.enter_context(nc.sbuf_tensor("e_qn", [128, 512], F32))
                tr1 = st2.enter_context(nc.sbuf_tensor("e_tr1", [128, 512], F32))
                tr2 = st2.enter_context(nc.sbuf_tensor("e_tr2", [128, 512], F32))
                qrb = st2.enter_context(nc.sbuf_tensor("e_qrb", [128, 1024], BF16))
                p_q = p_pool[:, 0:1024]
                p_v = p_pool[:, 1024:1536]
                p_qs = p_pool[:, 1024:2048]

                @blk.sync
                def _(sync):
                    sync.dma_start(out=ident[:], in_=ident_c).then_inc(lc, 16)
                    sync.dma_start(out=ones_rw[:], in_=onesrow_c).then_inc(lc, 16)

                    for j in range(D_NE):
                        par = j % 2
                        if j >= 2:
                            sync.wait_ge(sv, ve_kproc(j - 2))
                        sync.dma_start(out=tbl[:, 2048 * par + 0:2048 * par + 512],
                                       in_=t1q_i[128 * j:128 * j + 128, :]).then_inc(ltp[par], 16)
                        sync.dma_start(out=tbl[:, 2048 * par + 512:2048 * par + 1024],
                                       in_=t2q_i[128 * j:128 * j + 128, :]).then_inc(ltp[par], 16)
                        sync.dma_start(out=tbl[:, 2048 * par + 1024:2048 * par + 1536],
                                       in_=t1k_i[128 * j:128 * j + 128, :]).then_inc(ltp[par], 16)
                        sync.dma_start(out=tbl[:, 2048 * par + 1536:2048 * par + 2048],
                                       in_=t2k_i[128 * j:128 * j + 128, :]).then_inc(ltp[par], 16)

                @blk.gpsimd
                def _(g):
                    g.dma_start(out=wqkvb[:].rearrange("p (c o) -> p c o", c=8),
                                in_=wqkv_i.rearrange("(c p) o -> p c o", p=128)).then_inc(lcg, 16)
                    g.dma_start(out=qsrow[0:1, 0:512], in_=qs_i).then_inc(lcg, 16)
                    g.dma_start(out=qsrow[0:1, 512:1024], in_=ks_i).then_inc(lcg, 16)
                    for i in range(25):
                        if i >= 2:
                            g.wait_ge(stp, pe_xt(i - 2))
                        g.dma_start(out=xbf[:, 1024 * (i % 2):1024 * (i % 2) + 1024],
                                    in_=xin[128 * i:128 * i + 128, :]).then_inc(lxp[i % 2], 16)

                @blk.vector
                def _(v):
                    for i in range(25):
                        v.wait_ge(stp, pe_xt(i))
                        v.tensor_copy(
                            xe_fm[:].rearrange("p (c t) -> p c t", c=8)[:, :, 128 * i:128 * i + 128],
                            p_tr[:].rearrange("p (c t) -> p c t", c=8)).then_inc(sv, 1)
                    v.wait_ge(stp, PE_QSB)
                    v.tensor_copy(qsb[:], p_qs).then_inc(sv, 1)
                    for j in range(D_NE):
                        v.wait_ge(stp, pe_mm(j, 2))
                        v.tensor_copy(vtm[:, 512 * j:512 * j + 512], p_v).then_inc(sv, 1)
                        v.wait_ge(ltp[j % 2], 64 * (j // 2 + 1))
                        for isq in (True, False):
                            half = 0 if isq else 512
                            v.wait_ge(sa, sa_d2(j, 1 if isq else 2))
                            settle(v, v.tensor_reduce(
                                ncol[:, half // 64:half // 64 + 8].unsqueeze(2),
                                sqb[:, half:half + 512].rearrange("p (h f) -> p h f", h=8),
                                AX.X, ALU.add))
                            settle(v, v.tensor_scalar_max(
                                ncol[:, half // 64:half // 64 + 8],
                                ncol[:, half // 64:half // 64 + 8], 1e-30))
                            cap[("n2", j, isq)] = sgc["v"]
                            v.wait_ge(sa, sa_d2(j, 4 if isq else 6))
                            v.tensor_scalar_max(ncol[:, half // 64:half // 64 + 8],
                                                ncol[:, half // 64:half // 64 + 8], 1e-12)
                            settle(v, v.reciprocal(ncol[:, half // 64:half // 64 + 8],
                                                   ncol[:, half // 64:half // 64 + 8]))
                            v.tensor_mul(qn[:].rearrange("p (h f) -> p h f", h=8),
                                         p_q[:, half:half + 512].rearrange("p (h f) -> p h f", h=8),
                                         ncol[:, half // 64:half // 64 + 8].unsqueeze(2)
                                         .broadcast_to([128, 8, 64]))
                            v.tensor_mul(qn[:], qn[:], qsb[:, half:half + 512])
                            toff = 2048 * (j % 2) + (0 if isq else 1024)
                            v.tensor_mul(tr1[:], qn[:], tbl[:, toff:toff + 512])
                            qv = qn[:].rearrange("p (h s f) -> p h s f", h=8, s=2)
                            t2v = tbl[:, toff + 512:toff + 1024].rearrange("p (h s f) -> p h s f", h=8, s=2)
                            o2 = tr2[:].rearrange("p (h s f) -> p h s f", h=8, s=2)
                            v.tensor_mul(o2[:, :, 0:1, :], qv[:, :, 1:2, :], t2v[:, :, 0:1, :])
                            v.tensor_mul(o2[:, :, 1:2, :], qv[:, :, 0:1, :], t2v[:, :, 1:2, :])
                            v.tensor_add(qrb[:, half:half + 512], tr1[:], tr2[:]).then_inc(sv, 1)
                        v.wait_ge(stp, pe_tr2(j))
                        v.tensor_copy(
                            qfm[:].rearrange("p (hp t) -> p hp t", hp=4)[:, :, 128 * j:128 * j + 128],
                            p_tr[:, 0:512].rearrange("p (hp t) -> p hp t", hp=4))
                        v.tensor_copy(
                            kfm[:].rearrange("p (hp t) -> p hp t", hp=4)[:, :, 128 * j:128 * j + 128],
                            p_tr[:, 512:1024].rearrange("p (hp t) -> p hp t", hp=4)).then_inc(sv, 1)

                @blk.tensor
                def _(t):
                    t.wait_ge(lc, 32)
                    t.wait_ge(lcg, 48)
                    for i in range(25):
                        t.wait_ge(lxp[i % 2], 16 * (i // 2 + 1))
                        if i >= 1:
                            t.wait_ge(sv, ve_xt(i - 1))
                        for fc in range(8):
                            ins = t.transpose(p_tr[:, 128 * fc:128 * fc + 128],
                                              xbf[:, 1024 * (i % 2) + 128 * fc:1024 * (i % 2) + 128 * fc + 128],
                                              ident[:])
                            if fc == 7:
                                ins.then_inc(stp, 1)
                    t.wait_ge(sv, 25)
                    t.matmul(p_qs[:, 0:512], ones_rw[:], qsrow[0:1, 0:512], start=True, stop=True)
                    t.matmul(p_qs[:, 512:1024], ones_rw[:], qsrow[0:1, 512:1024],
                             start=True, stop=True).then_inc(stp, 1)
                    for j in range(D_NE):
                        for sl in range(3):
                            if sl == 0:
                                t.wait_ge(sv, ve_qproc(j - 1) if j >= 1 else VE_QSB)
                            elif sl == 1:
                                if j >= 1:
                                    t.wait_ge(sv, ve_kproc(j - 1))
                            else:
                                t.wait_ge(sv, ve_vev(j - 1) if j >= 1 else VE_QSB)
                            dst = (p_q[:, 0:512] if sl == 0 else
                                   p_q[:, 512:1024] if sl == 1 else p_v)
                            for c in range(8):
                                off = 128 if c < 4 else 127
                                ins = t.matmul(dst,
                                               xe_fm[:, 3200 * c + off + 128 * j:3200 * c + off + 128 * j + 128],
                                               wqkvb[:, 1536 * c + 512 * sl:1536 * c + 512 * sl + 512],
                                               start=(c == 0), stop=(c == 7))
                                if c == 7:
                                    ins.then_inc(stp, 1)
                        t.wait_ge(sv, ve_kproc(j))
                        if j >= 1:
                            t.wait_ge(sv, ve_trev(j - 1))
                        for hp in range(4):
                            t.transpose(p_tr[:, 128 * hp:128 * hp + 128],
                                        qrb[:, 128 * hp:128 * hp + 128], ident[:])
                        for hp in range(4):
                            ins = t.transpose(p_tr[:, 512 + 128 * hp:512 + 128 * hp + 128],
                                              qrb[:, 512 + 128 * hp:512 + 128 * hp + 128], ident[:])
                            if hp == 3:
                                ins.then_inc(stp, 1)

                @blk.scalar
                def _(s):
                    for j in range(D_NE):
                        s.wait_ge(stp, pe_mm(j, 0))
                        if j >= 1:
                            s.wait_ge(sgv, cap[("n2", j - 1, False)])
                        s.activation(sqb[:, 0:512], p_q[:, 0:512], AF.Square).then_inc(sa, 1)
                        s.wait_ge(stp, pe_mm(j, 1))
                        s.activation(sqb[:, 512:1024], p_q[:, 512:1024], AF.Square).then_inc(sa, 1)
                        s.wait_ge(sgv, cap[("n2", j, True)])
                        s.activation(ncol[:, 0:8], ncol[:, 0:8], AF.Ln).then_inc(sa, 1)
                        s.wait_ge(sa, sa_d2(j, 3))
                        s.activation(ncol[:, 0:8], ncol[:, 0:8], AF.Exp, scale=0.5).then_inc(sa, 1)
                        s.wait_ge(sgv, cap[("n2", j, False)])
                        s.activation(ncol[:, 8:16], ncol[:, 8:16], AF.Ln).then_inc(sa, 1)
                        s.wait_ge(sa, sa_d2(j, 5))
                        s.activation(ncol[:, 8:16], ncol[:, 8:16], AF.Exp, scale=0.5).then_inc(sa, 1)

            with contextlib.ExitStack() as st3:
                maskb = st3.enter_context(nc.sbuf_tensor("e_maskb", [128, 2 * 640], BF16))
                smb = st3.enter_context(nc.sbuf_tensor("e_smb", [128, 640], F32))
                eb = st3.enter_context(nc.sbuf_tensor("e_eb", [128, 640], BF16))
                anb = st3.enter_context(nc.sbuf_tensor("e_anb", [128, 640], BF16))
                anT = st3.enter_context(nc.sbuf_tensor("e_anT", [128, 640], BF16))
                otm = st3.enter_context(nc.sbuf_tensor("e_otm", [128, 512], BF16))
                ofm = st3.enter_context(nc.sbuf_tensor("e_ofm", [128, 512], BF16))
                wob = st3.enter_context(nc.sbuf_tensor("e_wob", [128, 4 * 1024], BF16))
                xres = st3.enter_context(nc.sbuf_tensor("e_xres", [128, 2 * 1024], F32))
                x4b = st3.enter_context(nc.sbuf_tensor("e_x4b", [128, 1024], F32))
                accol = st3.enter_context(nc.sbuf_tensor("e_accol", [128, 2 * 17], F32))
                p_s = p_pool[:, 0:640]
                p_o3 = p_pool[:, 1024:1536]
                p_wo = p_pool[:, 0:1024]

                @blk.sync
                def _(sync):
                    sync.wait_ge(stp, PE_D2)
                    sync.wait_ge(sv, VE_D2)
                    sync.wait_ge(sa, SA_D2)
                    for qt in range(D_NA):
                        par = qt % 2
                        if qt >= 2:
                            sync.wait_ge(sv, ve_res(qt - 2))
                        sync.dma_start(out=maskb[:, 640 * par:640 * par + 640],
                                       in_=masks_i[qt]).then_inc(lmp[par], 16)
                        sync.dma_start(out=xres[:, 1024 * par:1024 * par + 1024],
                                       in_=xin[1024 + 128 * qt:1024 + 128 * qt + 128, :]
                                       ).then_inc(lmp[par], 16)
                        sync.wait_ge(sv, ve_res(qt))
                        sync.dma_start(out=x4scr[128 * qt:128 * qt + 128, :],
                                       in_=x4b[:]).then_inc(sx4, 16)

                @blk.gpsimd
                def _(g):
                    g.wait_ge(stp, PE_D2)
                    g.wait_ge(sv, VE_D2)
                    g.wait_ge(sa, SA_D2)
                    g.dma_start(out=wob[:].rearrange("p (c o) -> p c o", c=4),
                                in_=wo_i.rearrange("(c p) o -> p c o", p=128)).then_inc(lcg, 16)

                @blk.vector
                def _(v):
                    for qt in range(D_NA):
                        par = qt % 2
                        for h in range(8):
                            v.wait_ge(stp, pe_sim(qt, h))
                            if h == 0:
                                v.wait_ge(lmp[par], 32 * (qt // 2) + 16)
                            v.tensor_add(smb[:], p_s, maskb[:, 640 * par:640 * par + 640]
                                         ).then_inc(sv, 1)
                            settle(v, v.reduce_max(accol[:, 17 * 0 + qt:17 * 0 + qt + 1], smb[:], AX.X))
                            settle(v, v.tensor_scalar_mul(accol[:, qt:qt + 1],
                                                          accol[:, qt:qt + 1], -1.0))
                            cap[("nm", qt, h)] = sgc["v"]
                            v.wait_ge(sa, sa_eb(qt, h))
                            settle(v, v.reciprocal(accol[:, 17 + qt:17 + qt + 1],
                                                   accol[:, 17 + qt:17 + qt + 1]))
                            v.tensor_scalar_mul(anb[:], eb[:], accol[:, 17 + qt:17 + qt + 1]
                                                ).then_inc(sv, 1)
                            v.wait_ge(stp, pe_antr(qt, h))
                            v.tensor_copy(anT[:], p_tr[:, 0:640]).then_inc(sv, 1)
                        v.wait_ge(stp, pe_ov(qt, 7))
                        v.tensor_copy(otm[:], p_o3).then_inc(sv, 1)
                        v.wait_ge(stp, pe_otr(qt))
                        v.tensor_copy(ofm[:], p_tr[:, 0:512]).then_inc(sv, 1)
                        v.wait_ge(stp, pe_wo(qt))
                        v.wait_ge(lmp[par], 32 * (qt // 2 + 1))
                        v.tensor_add(x4b[:], p_wo, xres[:, 1024 * par:1024 * par + 1024]
                                     ).then_inc(sv, 1)

                @blk.tensor
                def _(t):
                    t.wait_ge(lcg, 64)
                    for qt in range(D_NA):
                        for h in range(8):
                            hp, hi = h // 2, h % 2
                            if h >= 1:
                                t.wait_ge(sv, ve_smb(qt, h - 1))
                            elif qt >= 1:
                                t.wait_ge(sv, ve_res(qt - 1))
                            for kb in range(5):
                                kt = qt + 3 + kb
                                ins = t.matmul(
                                    p_s[:, 128 * kb:128 * kb + 128],
                                    qfm[64 * hi:64 * hi + 64,
                                        3072 * hp + 128 * (qt + 7):3072 * hp + 128 * (qt + 7) + 128],
                                    kfm[64 * hi:64 * hi + 64,
                                        3072 * hp + 128 * kt:3072 * hp + 128 * kt + 128],
                                    start=True, stop=True)
                                if kb == 4:
                                    ins.then_inc(stp, 1)
                            t.wait_ge(sv, ve_an(qt, h))
                            if h >= 1:
                                t.wait_ge(sv, ve_anT(qt, h - 1))
                            elif qt >= 1:
                                t.wait_ge(sv, ve_ofm(qt - 1))
                            for kb in range(5):
                                ins = t.transpose(p_tr[:, 128 * kb:128 * kb + 128],
                                                  anb[:, 128 * kb:128 * kb + 128], ident[:])
                                if kb == 4:
                                    ins.then_inc(stp, 1)
                            t.wait_ge(sv, ve_anT(qt, h))
                            for kb in range(5):
                                kt = qt + 3 + kb
                                ins = t.matmul(p_o3[:, 64 * h:64 * h + 64],
                                               anT[:, 128 * kb:128 * kb + 128],
                                               vtm[:, 512 * kt + 64 * h:512 * kt + 64 * h + 64],
                                               start=(kb == 0), stop=(kb == 4))
                                if kb == 4:
                                    ins.then_inc(stp, 1)
                        t.wait_ge(sv, ve_oev(qt))
                        for ic in range(4):
                            ins = t.transpose(p_tr[:, 128 * ic:128 * ic + 128],
                                              otm[:, 128 * ic:128 * ic + 128], ident[:])
                            if ic == 3:
                                ins.then_inc(stp, 1)
                        t.wait_ge(sv, ve_ofm(qt))
                        for os_ in range(2):
                            for ic in range(4):
                                ins = t.matmul(p_wo[:, 512 * os_:512 * os_ + 512],
                                               ofm[:, 128 * ic:128 * ic + 128],
                                               wob[:, 1024 * ic + 512 * os_:1024 * ic + 512 * os_ + 512],
                                               start=(ic == 0), stop=(ic == 3))
                                if os_ == 1 and ic == 3:
                                    ins.then_inc(stp, 1)

                @blk.scalar
                def _(s):
                    for qt in range(D_NA):
                        for h in range(8):
                            s.wait_ge(sgv, cap[("nm", qt, h)])
                            if qt * 8 + h >= 1:
                                ph = (qt * 8 + h - 1)
                                s.wait_ge(sv, ve_an(ph // 8, ph % 8))
                            s.activation(eb[:], smb[:], AF.Exp, bias=accol[:, qt:qt + 1],
                                         accum_out=accol[:, 17 + qt:17 + qt + 1]).then_inc(sa, 1)


        with contextlib.ExitStack() as st4:
            w1b = st4.enter_context(nc.sbuf_tensor("e_w1b", [128, 8 * 5460], BF16))
            w2b = st4.enter_context(nc.sbuf_tensor("e_w2b", [128, 22 * 1024], BF16))
            lngb = st4.enter_context(nc.sbuf_tensor("e_lngb", [1, 2048], BF16))
            b1b = st4.enter_context(nc.sbuf_tensor("e_b1b", [1, 5460], BF16))
            b2b = st4.enter_context(nc.sbuf_tensor("e_b2b", [1, 1024], BF16))
            gb = st4.enter_context(nc.sbuf_tensor("e_gb", [128, 1024], F32))
            bb = st4.enter_context(nc.sbuf_tensor("e_bb", [128, 1024], F32))
            x4c = st4.enter_context(nc.sbuf_tensor("e_x4c", [128, 1024], F32))
            x4s = st4.enter_context(nc.sbuf_tensor("e_x4s", [128, 1024], F32))
            h2n = st4.enter_context(nc.sbuf_tensor("e_h2n", [128, 1024], F32))
            h2nb = st4.enter_context(nc.sbuf_tensor("e_h2nb", [128, 1024], BF16))
            h2fm = st4.enter_context(nc.sbuf_tensor("e_h2fm", [128, 1024], BF16))
            w1a = st4.enter_context(nc.sbuf_tensor("e_w1a", [128, 2730], BF16))
            w1g = st4.enter_context(nc.sbuf_tensor("e_w1g", [128, 2730], BF16))
            agb = st4.enter_context(nc.sbuf_tensor("e_agb", [128, 2730], BF16))
            agfm = st4.enter_context(nc.sbuf_tensor("e_agfm", [128, 2816], BF16))
            ybuf = st4.enter_context(nc.sbuf_tensor("e_ybuf", [128, 1024], F32))
            res2 = st4.enter_context(nc.sbuf_tensor("e_res2", [128, 1024], F32)) if final else None
            x3ob = st4.enter_context(nc.sbuf_tensor("e_x3ob", [128, 1024], F32)) if final else None
            sscr = st4.enter_context(nc.sbuf_tensor("e_sscr", [128, 512], F32))
            ln4 = st4.enter_context(nc.sbuf_tensor("e_ln4", [128, 16 * 16], F32))
            SMU, SVAR, SSQS, SSUM, SRSTD, SQ2, SM2, SV2, SR2 = range(9)
            def l4(i, k):
                return ln4[:, 16 * i + k:16 * i + k + 1]

            NLD = 3 if final else 2

            @blk.sync
            def _(sync):
                sync.wait_ge(stp, PE_D3)
                sync.wait_ge(sv, VE_D3)
                sync.wait_ge(sa, SA_D3)
                sync.wait_ge(sx4, 16 * D_NA)
                for i in range(D_NO):
                    par = i % 2
                    if i >= 1:
                        sync.wait_ge(sv, ve4(i - 1, 14))
                    r = 128 * (i + 1)
                    sync.dma_start(out=x4c[:], in_=x4scr[r:r + 128, :]).then_inc(ltp[par], 16)
                    sync.dma_start(out=x4s[:], in_=x4scr[r - 1:r + 127, :]).then_inc(ltp[par], 16)
                    if final:
                        sync.dma_start(out=x3ob[:], in_=x3own_i[128 * i:128 * i + 128, :]
                                       ).then_inc(ltp[par], 16)
                    sync.wait_ge(sv, ve4(i, VT))
                    sync.dma_start(out=y_o[128 * i:128 * i + 128, :],
                                   in_=ybuf[:] if not final else res2[:]).then_inc(so, 16)
                sync.wait_ge(so, 16 * D_NO)

            @blk.gpsimd
            def _(g):
                g.wait_ge(stp, PE_D3)
                g.wait_ge(sv, VE_D3)
                g.wait_ge(sa, SA_D3)
                g.dma_start(out=lngb[0:1, 0:1024], in_=lng_i).then_inc(lcg, 16)
                g.dma_start(out=lngb[0:1, 1024:2048], in_=lnb_i).then_inc(lcg, 16)
                g.dma_start(out=b1b[:], in_=b1_i).then_inc(lcg, 16)
                g.dma_start(out=b2b[:], in_=b2_i).then_inc(lcg, 16)
                g.dma_start(out=w1b[:].rearrange("p (c f) -> p c f", c=8),
                            in_=w1_i.rearrange("(c p) f -> p c f", p=128)).then_inc(lcg, 16)
                g.dma_start(out=w2b[:, 0:21 * 1024].rearrange("p (c y) -> p c y", c=21),
                            in_=w2_i[0:2688, :].rearrange("(c p) y -> p c y", p=128)
                            ).then_inc(lcg, 16)
                g.dma_start(out=w2b[0:42, 21 * 1024:22 * 1024],
                            in_=w2_i[2688:2730, :]).then_inc(lcg, 16)

            @blk.vector
            def _(v):
                v.wait_ge(stp, PE_GB)
                v.tensor_copy(gb[:], p_pool[:, 0:1024]).then_inc(sv, 1)
                v.wait_ge(stp, PE_BB)
                v.tensor_copy(bb[:], p_pool[:, 0:1024]).then_inc(sv, 1)
                for i in range(D_NO):
                    par = i % 2
                    v.wait_ge(sa, sa4(i, 4))
                    settle(v, v.tensor_add(l4(i, SSQS), l4(i, SSQS), l4(i, SQ2)))
                    settle(v, v.tensor_add(l4(i, SSUM), l4(i, SSUM), l4(i, SM2)))
                    settle(v, v.tensor_scalar_mul(l4(i, SMU), l4(i, SSUM), 1.0 / 1024))
                    v.tensor_scalar_mul(l4(i, SVAR), l4(i, SSQS), 1.0 / 1024)
                    settle(v, v.tensor_mul(l4(i, SQ2), l4(i, SMU), l4(i, SMU)))
                    settle(v, v.tensor_sub(l4(i, SVAR), l4(i, SVAR), l4(i, SQ2)))
                    settle(v, v.tensor_scalar_add(l4(i, SVAR), l4(i, SVAR), 1e-5))
                    cap[("var", i)] = sgc["v"]
                    v.wait_ge(sa, sa4(i, 6))
                    v.tensor_scalar(h2n[:, 0:512], x4c[:, 0:512],
                                    l4(i, SMU), l4(i, SRSTD), ALU.subtract, ALU.mult)
                    v.tensor_scalar(h2n[:, 512:1024], x4s[:, 512:1024],
                                    l4(i, SMU), l4(i, SRSTD), ALU.subtract, ALU.mult)
                    v.tensor_mul(h2n[:], h2n[:], gb[:])
                    v.tensor_add(h2nb[:], h2n[:], bb[:]).then_inc(sv, 1)
                    v.wait_ge(stp, pe4(i, 1))
                    v.tensor_copy(h2fm[:], p_tr[:]).then_inc(sv, 1)
                    for s_ in range(6):
                        ns, a, gpart = w1_slice_parts(s_)
                        v.wait_ge(stp, pe4(i, 2 + s_))
                        pa = p_pool[:, 512 * (s_ % 2):512 * (s_ % 2) + 512]
                        v.tensor_copy(w1a[:, 512 * s_ + a[0]:512 * s_ + a[1]],
                                      pa[:, a[0]:a[1]]).then_inc(sv, 1)
                    v.wait_ge(sa, sa4(i, 12))
                    v.tensor_mul(agb[:], w1a[:], w1g[:]).then_inc(sv, 1)
                    for g3 in range(3):
                        v.wait_ge(stp, pe4(i, 13 + g3))
                        if g3 < 2:
                            v.tensor_copy(agfm[:, 1024 * g3:1024 * g3 + 1024],
                                          p_tr[:]).then_inc(sv, 1)
                        else:
                            v.tensor_copy(agfm[:, 2048:2048 + 640], p_tr[:, 0:640])
                            v.tensor_copy(agfm[0:42, 2688:2816],
                                          p_tr[0:42, 640:768]).then_inc(sv, 1)
                    for os_ in range(2):
                        v.wait_ge(stp, pe4(i, 16 + os_))
                        if i >= 1 and os_ == 0:
                            v.wait_ge(so, 16 * i)
                        v.tensor_add(ybuf[:, 512 * os_:512 * os_ + 512],
                                     p_pool[:, 512 * os_:512 * os_ + 512],
                                     x4c[:, 512 * os_:512 * os_ + 512]).then_inc(sv, 1)
                    if final:
                        v.tensor_add(res2[:], ybuf[:], x3ob[:]).then_inc(sv, 1)
                        v.wait_ge(sa, sa4(i, 16))
                        settle(v, v.tensor_add(l4(i, SQ2), l4(i, SQ2), l4(i, SV2)))
                        settle(v, v.tensor_add(l4(i, SM2), l4(i, SM2), l4(i, 9)))
                        settle(v, v.tensor_scalar_mul(l4(i, SM2), l4(i, SM2), 1.0 / 1024))
                        v.tensor_scalar_mul(l4(i, SV2), l4(i, SQ2), 1.0 / 1024)
                        settle(v, v.tensor_mul(l4(i, SQ2), l4(i, SM2), l4(i, SM2)))
                        settle(v, v.tensor_sub(l4(i, SV2), l4(i, SV2), l4(i, SQ2)))
                        settle(v, v.tensor_scalar_add(l4(i, SV2), l4(i, SV2), 1e-5))
                        cap[("var2", i)] = sgc["v"]
                        v.wait_ge(sa, sa4(i, 18))
                        v.tensor_scalar(res2[:], res2[:], l4(i, SM2), l4(i, SR2),
                                        ALU.subtract, ALU.mult).then_inc(sv, 1)
                    else:
                        v.engine_nop().then_inc(sv, 1)

            @blk.tensor
            def _(t):
                t.wait_ge(lcg, 16 * 11)
                t.wait_ge(sv, ve_res(D_NA - 1))
                for half in range(2):
                    ins = t.matmul(p_pool[:, 512 * half:512 * half + 512], ones_rw[:],
                                   lngb[0:1, 512 * half:512 * half + 512], start=True, stop=True)
                    if half == 1:
                        ins.then_inc(stp, 1)
                t.wait_ge(sv, VE_GB)
                for half in range(2):
                    ins = t.matmul(p_pool[:, 512 * half:512 * half + 512], ones_rw[:],
                                   lngb[0:1, 1024 + 512 * half:1024 + 512 * half + 512],
                                   start=True, stop=True)
                    if half == 1:
                        ins.then_inc(stp, 1)
                for i in range(D_NO):
                    t.wait_ge(sv, ve4(i, 1))
                    if i >= 1:
                        t.wait_ge(sv, ve4(i - 1, 12))
                    else:
                        t.wait_ge(sv, VE_BB)
                    for fc in range(8):
                        ins = t.transpose(p_tr[:, 128 * fc:128 * fc + 128],
                                          h2nb[:, 128 * fc:128 * fc + 128], ident[:])
                        if fc == 7:
                            ins.then_inc(stp, 1)
                    t.wait_ge(sv, ve4(i, 2))
                    for s_ in range(11):
                        ns, a, gpart = w1_slice_parts(s_)
                        prev = s_ - 2
                        if prev >= 0:
                            _, pa_, pg_ = w1_slice_parts(prev)
                            if pa_ is not None:
                                t.wait_ge(sv, ve4(i, 3 + prev))
                            if pg_ is not None:
                                t.wait_ge(sa, sa4(i, 7 + (prev - 5)))
                        elif i >= 1:
                            t.wait_ge(sv, ve4(i - 1, 13 + s_))
                            pg_ = w1_slice_parts(prev + 11)[2]
                            if pg_ is not None:
                                t.wait_ge(sa, sa4(i - 1, 7 + (prev + 11 - 5)))
                        pd = p_pool[:, 512 * (s_ % 2):512 * (s_ % 2) + ns]
                        for c in range(8):
                            t.matmul(pd, h2fm[:, 128 * c:128 * c + 128],
                                     w1b[:, 5460 * c + 512 * s_:5460 * c + 512 * s_ + ns],
                                     start=(c == 0), stop=False)
                        t.matmul(pd, ones_rw[:], b1b[0:1, 512 * s_:512 * s_ + ns],
                                 start=False, stop=True).then_inc(stp, 1)
                    t.wait_ge(sv, ve4(i, 9))
                    for g3 in range(3):
                        ncc = 8 if g3 < 2 else 6
                        if g3 >= 1:
                            t.wait_ge(sv, ve4(i, 10 + g3 - 1))
                        for cc in range(ncc):
                            gcc = 8 * g3 + cc
                            nc_ = 42 if gcc == 21 else 128
                            ins = t.transpose(p_tr[0:nc_, 128 * cc:128 * cc + 128],
                                              agb[:, 128 * gcc:128 * gcc + nc_],
                                              ident[:])
                            if cc == ncc - 1:
                                ins.then_inc(stp, 1)
                    t.wait_ge(sv, ve4(i, 12))
                    for os_ in range(2):
                        pd = p_pool[:, 512 * os_:512 * os_ + 512]
                        for cc in range(22):
                            nc_ = 42 if cc == 21 else 128
                            t.matmul(pd, agfm[0:nc_, 128 * cc:128 * cc + 128],
                                     w2b[0:nc_, 1024 * cc + 512 * os_:1024 * cc + 512 * os_ + 512],
                                     start=(cc == 0), stop=False)
                        t.matmul(pd, ones_rw[:], b2b[0:1, 512 * os_:512 * os_ + 512],
                                 start=False, stop=True).then_inc(stp, 1)

            @blk.scalar
            def _(s):
                for i in range(D_NO):
                    par = i % 2
                    s.wait_ge(ltp[par], 16 * (48 + NLD * (i // 2 + 1)))
                    if i >= 1:
                        s.wait_ge(sv, ve4(i - 1, 1))
                    s.activation(sscr[:, 0:512], x4c[:, 0:512], AF.Square,
                                 accum_out=l4(i, SSQS)).then_inc(sa, 1)
                    s.activation(sscr[:, 0:512], x4s[:, 512:1024], AF.Square,
                                 accum_out=l4(i, SQ2)).then_inc(sa, 1)
                    s.activation(sscr[:, 0:512], x4c[:, 0:512], AF.Copy,
                                 accum_out=l4(i, SSUM)).then_inc(sa, 1)
                    s.activation(sscr[:, 0:512], x4s[:, 512:1024], AF.Copy,
                                 accum_out=l4(i, SM2)).then_inc(sa, 1)
                    s.wait_ge(sgv, cap[("var", i)])
                    s.activation(l4(i, SRSTD), l4(i, SVAR), AF.Ln).then_inc(sa, 1)
                    s.wait_ge(sa, sa4(i, 5))
                    s.activation(l4(i, SRSTD), l4(i, SRSTD), AF.Exp, scale=-0.5).then_inc(sa, 1)
                    for s_ in range(5, 11):
                        ns, a, gpart = w1_slice_parts(s_)
                        s.wait_ge(stp, pe4(i, 2 + s_))
                        pa = p_pool[:, 512 * (s_ % 2):512 * (s_ % 2) + ns]
                        gcol0 = 512 * s_ + gpart[0] - 2730
                        s.activation(w1g[:, gcol0:gcol0 + (gpart[1] - gpart[0])],
                                     pa[:, gpart[0]:gpart[1]], AF.Gelu).then_inc(sa, 1)
                    if final:
                        s.wait_ge(sv, ve4(i, 15))
                        s.activation(sscr[:], res2[:, 0:512], AF.Square,
                                     accum_out=l4(i, SQ2)).then_inc(sa, 1)
                        s.activation(sscr[:], res2[:, 512:1024], AF.Square,
                                     accum_out=l4(i, SV2)).then_inc(sa, 1)
                        s.activation(sscr[:], res2[:, 0:512], AF.Copy,
                                     accum_out=l4(i, SM2)).then_inc(sa, 1)
                        s.activation(sscr[:], res2[:, 512:1024], AF.Copy,
                                     accum_out=l4(i, 9)).then_inc(sa, 1)
                        s.wait_ge(sgv, cap[("var2", i)])
                        s.activation(l4(i, SR2), l4(i, SV2), AF.Ln).then_inc(sa, 1)
                        s.wait_ge(sa, sa4(i, 17))
                        s.activation(l4(i, SR2), l4(i, SR2), AF.Exp, scale=-0.5).then_inc(sa, 1)

    return nc


def _nc():
    return bass.Bass("TRN2", target_bir_lowering=False, detect_race_conditions=False)


def kernel(**inputs):
    z = inputs
    x0 = np.asarray(z["L0_attn_embeddings"])
    g = np.asarray(z["L0_attn_guidance_trajectories"])
    cores = list(range(8))

    nc = _nc()
    build_phase_a(nc)
    res = run_bass_kernel_spmd(nc, [host_inputs_a(x0, g, c) for c in cores], core_ids=cores)
    x1 = np.zeros((2, 8192, 1024), np.float32)
    for c in cores:
        x1[c // 4, (c % 4) * 2048:(c % 4 + 1) * 2048] = res.results[c]["x1"]

    nc = _nc()
    build_phase_b(nc)
    res = run_bass_kernel_spmd(
        nc, [host_inputs_b(x1, z["L0_ffn_w_router"], z["L0_ffn_w_gating"], z["L0_ffn_w_gate"],
                           z["L0_ffn_w_up"], z["L0_ffn_w_down"], c) for c in cores],
        core_ids=cores)
    x2 = np.zeros((2, 8192, 1024), np.float32)
    for c in cores:
        x2[c // 4, (c % 4) * 2048:(c % 4 + 1) * 2048] = res.results[c]["x2"]

    nc = _nc()
    build_c1(nc)
    res = run_bass_kernel_spmd(
        nc, [host_inputs_c1(x2, z["L1_attn_Wq"], z["L1_attn_bq"], z["L1_attn_Wk"], z["L1_attn_bk"],
                            z["L1_attn_Wv"], z["L1_attn_bv"], c) for c in cores],
        core_ids=cores)
    qfm_all = [res.results[c]["qfm"] for c in cores]
    kfm_all = [res.results[c]["kfm"] for c in cores]
    vtm_all = [res.results[c]["vtm"] for c in cores]
    vsum_all = [res.results[c]["vsump"] for c in cores]

    nc = _nc()
    build_c2(nc)
    res = run_bass_kernel_spmd(
        nc, [host_inputs_c2(x2, qfm_all, kfm_all, vtm_all, vsum_all,
                            z["L1_attn_Wo"], z["L1_attn_bo"], c) for c in cores],
        core_ids=cores)
    x3 = np.zeros((2, 8192, 1024), np.float32)
    for c in cores:
        x3[c // 4, (c % 4) * 2048:(c % 4 + 1) * 2048] = res.results[c]["x3"]

    nc = _nc()
    build_d(nc, final=False)
    res = run_bass_kernel_spmd(nc, [host_inputs_d(x3, z, 0, c) for c in cores], core_ids=cores)
    y1 = np.zeros((2, 8192, 1024), np.float32)
    for c in cores:
        y1[c // 4, (c % 4) * 2048:(c % 4 + 1) * 2048] = res.results[c]["y"]

    nc = _nc()
    build_d(nc, final=True)
    res = run_bass_kernel_spmd(nc, [host_inputs_d(y1, z, 1, c, x3own_full=x3) for c in cores],
                               core_ids=cores)
    out = np.zeros((2, 8192, 1024), np.float32)
    for c in cores:
        out[c // 4, (c % 4) * 2048:(c % 4 + 1) * 2048] = res.results[c]["y"]
    return out


# version 04 (final)
# speedup vs baseline: 1.0819x; 1.0819x over previous
import contextlib
import numpy as np
import ml_dtypes
import concourse.bass as bass
import concourse.mybir as mybir
from concourse.bass_utils import run_bass_kernel_spmd

F32 = mybir.dt.float32
BF16 = mybir.dt.bfloat16
AF = mybir.ActivationFunctionType
ALU = mybir.AluOpType
AX = mybir.AxisListType





A_NT = 17
A_NS = 16
A_W = 8


def pos_weights_row(s):
    out = np.zeros(8, dtype=np.float64)
    w = min(s, A_W)
    for k in range(8):
        idx = s - A_W + k
        valid = (idx >= 0) and (s >= 1)
        j = k - (A_W - w)
        lin = (0.1 + 0.9 * j / max(w - 1, 1)) if w > 1 else 0.1
        out[k] = lin if valid else 0.0
    return out


def band_pair(s0):
    Bc = np.zeros((128, 128), dtype=np.float64)
    Bm = np.zeros((128, 128), dtype=np.float64)
    for ss in range(128):
        s = s0 + ss
        pw = pos_weights_row(s)
        for k in range(8):
            t = s - A_W + k
            tt = t - (s0 - 128)
            if 0 <= tt < 128:
                Bc[tt, ss] = pw[k]
            tt2 = t - s0
            if 0 <= tt2 < 128:
                Bm[tt2, ss] = pw[k]
    return Bc, Bm


def host_inputs_a(x0_full, g_full, core):
    b, q = core // 4, core % 4
    r0 = q * 2048
    lo, hi = r0 - 128, r0 + 2049
    pad_lo, pad_hi = max(0, -lo), max(0, hi - 8192)
    sl = x0_full[b, max(0, lo):min(8192, hi)]
    x0s = np.pad(sl, ((pad_lo, pad_hi), (0, 0)))
    if r0 == 0:
        Bc0, Bm0 = band_pair(0)
    else:
        Bc0, Bm0 = band_pair(8 * 128)
    Bc0_f, Bm0_f = (band_pair(0) if r0 == 0 else band_pair(8 * 128))
    BcS_f, BmS_f = band_pair(8 * 128)
    pwsum = np.stack([Bc0_f.sum(0) + Bm0_f.sum(0), BcS_f.sum(0) + BmS_f.sum(0)], axis=1)
    return {
        "pwsum": np.ascontiguousarray(pwsum, dtype=np.float32),
        "x0s": np.ascontiguousarray(x0s, dtype=np.float32),
        "g": np.ascontiguousarray(g_full, dtype=np.float32),
        "band0c": Bc0.astype(ml_dtypes.bfloat16),
        "band0m": Bm0.astype(ml_dtypes.bfloat16),
    }


def build_phase_a(nc, sem_st=None):
    x0s = nc.dram_tensor("x0s", [2177, 1024], F32, kind="ExternalInput").ap()
    g_in = nc.dram_tensor("g", [512, 1024], F32, kind="ExternalInput").ap()
    b0c_in = nc.dram_tensor("band0c", [128, 128], BF16, kind="ExternalInput").ap()
    b0m_in = nc.dram_tensor("band0m", [128, 128], BF16, kind="ExternalInput").ap()
    pws_in = nc.dram_tensor("pwsum", [128, 2], F32, kind="ExternalInput").ap()
    x1_out = nc.dram_tensor("x1", [2048, 1024], F32, kind="ExternalOutput").ap()

    BcS, BmS = band_pair(8 * 128)
    bstd_np = np.concatenate([BcS, BmS], axis=1).astype(ml_dtypes.bfloat16)
    bstd_c = nc.inline_tensor(np.ascontiguousarray(bstd_np), name="bstd").ap()
    ident_c = nc.inline_tensor(np.eye(128, dtype=np.float32), name="identf32").ap()

    SA0, SV0, ST0 = 8, 8, 4
    SA1, SV1, ST1 = SA0 + 6 * A_NT, SV0 + 9 * A_NT, ST0 + 4 * A_NT

    import contextlib
    with contextlib.ExitStack() as _st:
        _ss = sem_st if sem_st is not None else _st
        ident_sb = _st.enter_context(nc.sbuf_tensor("ident_sb", [128, 128], F32))
        g_f32 = _st.enter_context(nc.sbuf_tensor("g_f32", [128, 4 * 1024], F32))
        ghat_fm = _st.enter_context(nc.sbuf_tensor("ghat_fm", [128, 8 * 512], F32))
        g_gm = _st.enter_context(nc.sbuf_tensor("g_gm", [128, 4 * 1024], F32))
        bands = _st.enter_context(nc.sbuf_tensor("bands", [128, 4 * 128], BF16))
        gcols = _st.enter_context(nc.sbuf_tensor("gcols", [128, 16], F32))
        pws = _st.enter_context(nc.sbuf_tensor("pws", [128, 2], F32))
        xa = _st.enter_context(nc.sbuf_tensor("xa", [128, 2 * 1024], F32))
        xbb = _st.enter_context(nc.sbuf_tensor("xbb", [128, 2 * 1024], F32))
        diff = _st.enter_context(nc.sbuf_tensor("diff", [128, 2 * 1024], F32))
        sqscr = _st.enter_context(nc.sbuf_tensor("sqscr", [128, 1024], F32))
        normedf = _st.enter_context(nc.sbuf_tensor("normedf", [128, 2 * 1024], F32))
        nfm = _st.enter_context(nc.sbuf_tensor("nfm", [128, 1024], F32))
        Etile = _st.enter_context(nc.sbuf_tensor("Etile", [128, 512], F32))
        gw04 = _st.enter_context(nc.sbuf_tensor("gw04", [128, 512], F32))
        gwT = _st.enter_context(nc.sbuf_tensor("gwT", [128, 512], F32))
        inflmw = _st.enter_context(nc.sbuf_tensor("inflmw", [128, 1024], F32))
        mbl = _st.enter_context(nc.sbuf_tensor("mbl", [128, A_NT * 1024], BF16))
        mwb = _st.enter_context(nc.sbuf_tensor("mwb", [128, A_NT], BF16))
        cols = _st.enter_context(nc.sbuf_tensor("cols", [128, 8 * A_NT], F32))
        xres = _st.enter_context(nc.sbuf_tensor("xres", [128, 2 * 1024], F32))
        res = _st.enter_context(nc.sbuf_tensor("res", [128, 1024], F32))
        scols = _st.enter_context(nc.sbuf_tensor("scols", [128, 8 * A_NS], F32))
        x1b = _st.enter_context(nc.sbuf_tensor("x1b", [128, 2 * 1024], F32))
        p_tr = _st.enter_context(nc.psum_tensor("p_tr", [128, 1024], F32))
        p_sims = _st.enter_context(nc.psum_tensor("p_sims", [128, 512], F32))
        p_infl = _st.enter_context(nc.psum_tensor("p_infl", [128, 1024], F32))
        p_num = _st.enter_context(nc.psum_tensor("p_num", [128, 1024], F32))
        p_den = _st.enter_context(nc.psum_tensor("p_den", [128, 8], F32))
        lc = _ss.enter_context(nc.semaphore("lc"))
        ld0 = _ss.enter_context(nc.semaphore("ld0"))
        ld1 = _ss.enter_context(nc.semaphore("ld1"))
        lr0 = _ss.enter_context(nc.semaphore("lr0"))
        lr1 = _ss.enter_context(nc.semaphore("lr1"))
        so0 = _ss.enter_context(nc.semaphore("so0"))
        so1 = _ss.enter_context(nc.semaphore("so1"))
        sgv = _ss.enter_context(nc.semaphore("sgv"))
        sga = _ss.enter_context(nc.semaphore("sga"))
        sp = _ss.enter_context(nc.semaphore("sp"))
        sa = _ss.enter_context(nc.semaphore("sa"))
        sv = _ss.enter_context(nc.semaphore("sv"))
        stt = _ss.enter_context(nc.semaphore("stt"))
        block = _st.enter_context(nc.Block())
        MS, MSC, INVM, MAG, E4, MW06, MW04, SUME = range(8)
        def col(i, j):
            return cols[:, 8 * i + j:8 * i + j + 1]
        SDEN, SRD, SMU, SVAR, SSQS, SSUM, SRSTD = range(7)
        def scol(m, j):
            return scols[:, 8 * m + j:8 * m + j + 1]

        def xa_s(i):
            return xa[:, 1024 * (i % 2):1024 * (i % 2) + 1024]
        def xbb_s(i):
            return xbb[:, 1024 * (i % 2):1024 * (i % 2) + 1024]
        def diff_s(i):
            return diff[:, 1024 * (i % 2):1024 * (i % 2) + 1024]
        def normed_s(i):
            return normedf[:, 1024 * (i % 2):1024 * (i % 2) + 1024]
        def mbl_t(i, half=None):
            if half is None:
                return mbl[:, 1024 * i:1024 * i + 1024]
            return mbl[:, 1024 * i + 512 * half:1024 * i + 512 * half + 512]
        def xres_s(m):
            return xres[:, 1024 * (m % 2):1024 * (m % 2) + 1024]
        def x1b_s(m):
            return x1b[:, 1024 * (m % 2):1024 * (m % 2) + 1024]
        def bandsel(m):
            if m == 0:
                return bands[:, 0:128], bands[:, 128:256]
            return bands[:, 256:384], bands[:, 384:512]
        ldp = [ld0, ld1]
        lrp = [lr0, lr1]
        sop = [so0, so1]
        sgc = {"v": 0, "a": 0}
        gsem = {}
        def settle(eng, ins, which="v"):
            gsem = {"v": sgv, "a": sga}[which]
            sgc[which] += 1
            ins.then_inc(gsem, 1)
            eng.wait_ge(gsem, sgc[which])

        @block.sync
        def _(sync):
            sync.dma_start(out=ident_sb[:], in_=ident_c).then_inc(lc, 16)
            for gt in range(4):
                sync.dma_start(out=g_f32[:, 1024 * gt:1024 * gt + 1024],
                               in_=g_in[128 * gt:128 * gt + 128, :]).then_inc(lc, 16)
            sync.dma_start(out=bands[:, 256:512].rearrange("p (a x) -> p a x", a=2),
                           in_=bstd_c.rearrange("p (a x) -> p a x", a=2)).then_inc(lc, 16)
            sync.dma_start(out=bands[:, 0:128], in_=b0c_in).then_inc(lc, 16)
            sync.dma_start(out=bands[:, 128:256], in_=b0m_in).then_inc(lc, 16)
            sync.dma_start(out=pws[:], in_=pws_in).then_inc(lc, 16)
            for i in range(A_NT):
                if i >= 2:
                    sync.wait_ge(sp, i - 1)
                sync.dma_start(out=xa_s(i), in_=x0s[128 * i:128 * i + 128, :]).then_inc(ldp[i % 2], 16)
                sync.dma_start(out=xbb_s(i), in_=x0s[128 * i + 1:128 * i + 129, :]).then_inc(ldp[i % 2], 16)
            for m in range(A_NS):
                if m >= 2:
                    sync.wait_ge(sv, SV1 + 4 * (m - 2) + 4)
                sync.dma_start(out=xres_s(m), in_=x0s[128 + 128 * m:256 + 128 * m, :]).then_inc(lrp[m % 2], 16)
                if m >= 1:
                    sync.wait_ge(sv, SV1 + 4 * (m - 1) + 4)
                    sync.dma_start(out=x1_out[128 * (m - 1):128 * m, :], in_=x1b_s(m - 1)).then_inc(sop[(m - 1) % 2], 16)
            sync.wait_ge(sv, SV1 + 4 * (A_NS - 1) + 4)
            sync.dma_start(out=x1_out[128 * (A_NS - 1):128 * A_NS, :], in_=x1b_s(A_NS - 1)).then_inc(sop[(A_NS - 1) % 2], 16)
            sync.wait_ge(so0, 16 * 8)
            sync.wait_ge(so1, 16 * 8)

        @block.gpsimd
        def _(g):
            for i in range(A_NT):
                g.wait_ge(ldp[i % 2], 32 * (i // 2 + 1))
                if i >= 2:
                    g.wait_ge(sa, SA0 + 6 * (i - 2) + 4)
                g.tensor_sub(diff_s(i), xbb_s(i), xa_s(i)).then_inc(sp, 1)

        @block.scalar
        def _(s):
            s.wait_ge(lc, 144)
            for gt in range(4):
                s.activation(sqscr[:], g_f32[:, 1024 * gt:1024 * gt + 1024], AF.Square,
                             accum_out=gcols[:, 4 * gt:4 * gt + 1]).then_inc(sa, 1)
                s.wait_ge(sa, 2 * gt + 1)
                settle(s, s.activation(gcols[:, 4 * gt + 1:4 * gt + 2], gcols[:, 4 * gt:4 * gt + 1], AF.Ln), "a")
                s.activation(gcols[:, 4 * gt + 1:4 * gt + 2], gcols[:, 4 * gt + 1:4 * gt + 2],
                             AF.Exp, scale=0.5).then_inc(sa, 1)
            for i in range(A_NT):
                s.wait_ge(sp, i + 1)
                s.activation(sqscr[:], diff_s(i), AF.Square, accum_out=col(i, MS)).then_inc(sa, 1)
                s.wait_ge(sv, SV0 + 9 * i + 1)
                s.activation(col(i, INVM), col(i, MSC), AF.Ln).then_inc(sa, 1)
                s.wait_ge(sa, SA0 + 6 * i + 2)
                s.activation(col(i, INVM), col(i, INVM), AF.Exp, scale=-0.5).then_inc(sa, 1)
                if i >= 2:
                    s.wait_ge(sv, SV0 + 9 * (i - 2) + 9)
                s.wait_ge(sa, SA0 + 6 * i + 3)
                s.activation(normed_s(i), diff_s(i), AF.Copy, scale=col(i, INVM)).then_inc(sa, 1)
                s.wait_ge(sv, SV0 + 9 * i + 2)
                s.activation(col(i, E4), col(i, MAG), AF.Exp, scale=-4.0).then_inc(sa, 1)
                s.wait_ge(sv, SV0 + 9 * i + 5)
                s.wait_ge(stt, ST0 + 4 * i + 2)
                if i >= 1:
                    s.wait_ge(sv, SV0 + 9 * (i - 1) + 6)
                s.activation(Etile[:], p_sims[:], AF.Exp, scale=2.0, bias=col(i, E4),
                             accum_out=col(i, SUME)).then_inc(sa, 1)
            for m in range(A_NS):
                s.wait_ge(sv, SV1 + 4 * m + 2)
                s.activation(sqscr[:], res[:], AF.Square, accum_out=scol(m, SSQS)).then_inc(sa, 1)
                s.activation(sqscr[:], res[:], AF.Copy, accum_out=scol(m, SSUM)).then_inc(sa, 1)
                s.wait_ge(sv, SV1 + 4 * m + 3)
                s.activation(scol(m, SRSTD), scol(m, SVAR), AF.Ln).then_inc(sa, 1)
                s.wait_ge(sa, SA1 + 4 * m + 3)
                s.activation(scol(m, SRSTD), scol(m, SRSTD), AF.Exp, scale=-0.5).then_inc(sa, 1)

        @block.vector
        def _(v):
            for gt in range(4):
                v.wait_ge(sa, 2 * (gt + 1))
                c1 = gcols[:, 4 * gt + 1:4 * gt + 2]
                c2 = gcols[:, 4 * gt + 2:4 * gt + 3]
                settle(v, v.tensor_scalar_max(c1, c1, 1e-8))
                settle(v, v.reciprocal(c2, c1))
                if gt >= 2:
                    v.wait_ge(stt, gt - 1)
                v.tensor_scalar_mul(normed_s(gt), g_f32[:, 1024 * gt:1024 * gt + 1024], c2)
                v.tensor_copy(g_gm[:, 1024 * gt:1024 * gt + 1024],
                              g_f32[:, 1024 * gt:1024 * gt + 1024]).then_inc(sv, 1)
                v.wait_ge(stt, gt + 1)
                v.tensor_copy(
                    ghat_fm.rearrange("p (fc g) -> p fc g", fc=8)[:, :, 128 * gt:128 * gt + 128],
                    p_tr[:].rearrange("p (fc g) -> p fc g", fc=8)).then_inc(sv, 1)
            for i in range(A_NT):
                v.wait_ge(sa, SA0 + 6 * i + 1)
                v.tensor_scalar_max(col(i, MSC), col(i, MS), 1e-30).then_inc(sv, 1)
                v.wait_ge(sa, SA0 + 6 * i + 3)
                v.tensor_mul(col(i, MAG), col(i, MSC), col(i, INVM)).then_inc(sv, 1)
                v.wait_ge(sa, SA0 + 6 * i + 5)
                settle(v, v.tensor_scalar(col(i, MSC), col(i, E4), -1.0, 1.0, ALU.mult, ALU.add))
                settle(v, v.tensor_scalar_add(col(i, MAG), col(i, E4), 1.0))
                settle(v, v.reciprocal(col(i, MAG), col(i, MAG)))
                settle(v, v.tensor_mul(col(i, MW04), col(i, MSC), col(i, MAG)))
                v.tensor_copy(mwb[:, i:i + 1], col(i, MW04))
                v.tensor_scalar_mul(col(i, MW06), col(i, MW04), 0.6)
                v.tensor_scalar_mul(col(i, MW04), col(i, MW04), 1.0).then_inc(sv, 1)
                v.wait_ge(stt, ST0 + 4 * i + 1)
                v.tensor_copy(nfm[:], p_tr[:]).then_inc(sv, 1)
                v.wait_ge(stt, ST0 + 4 * i + 2)
                settle(v, v.reduce_max(col(i, E4), p_sims[:], AX.X))
                v.tensor_scalar_mul(col(i, E4), col(i, E4), -2.0).then_inc(sv, 1)
                v.wait_ge(sa, SA0 + 6 * i + 6)
                if i >= 1:
                    v.wait_ge(stt, ST0 + 4 * (i - 1) + 3)
                settle(v, v.reciprocal(col(i, MSC), col(i, SUME)))
                settle(v, v.tensor_scalar_mul(col(i, MSC), col(i, MSC), 0.4))
                v.tensor_scalar_mul(gw04[:], Etile[:], col(i, MSC)).then_inc(sv, 1)
                v.wait_ge(stt, ST0 + 4 * i + 3)
                v.tensor_copy(gwT[:], p_tr[:, 0:512]).then_inc(sv, 1)
                v.wait_ge(stt, ST0 + 4 * i + 4)
                v.tensor_scalar_mul(inflmw[:], p_infl[:], col(i, MW04)).then_inc(sv, 1)
                v.scalar_tensor_tensor(mbl_t(i), normed_s(i), col(i, MW06), inflmw[:],
                                       ALU.mult, ALU.add).then_inc(sv, 1)
            for m in range(A_NS):
                v.wait_ge(stt, ST1 + 2 * m + 1)
                settle(v, v.tensor_scalar_max(scol(m, SDEN), p_den[:, 0:1], 1e-8))
                v.reciprocal(scol(m, SRD), scol(m, SDEN)).then_inc(sv, 1)
                v.wait_ge(sv, SV1 + 4 * m + 1)
                v.wait_ge(stt, ST1 + 2 * m + 2)
                v.wait_ge(lrp[m % 2], 16 * (m // 2 + 1))
                if m >= 1:
                    v.wait_ge(sa, SA1 + 4 * (m - 1) + 2)
                v.scalar_tensor_tensor(res[:], p_num[:], scol(m, SRD), xres_s(m),
                                       ALU.mult, ALU.add).then_inc(sv, 1)
                v.wait_ge(sa, SA1 + 4 * m + 2)
                settle(v, v.tensor_scalar_mul(scol(m, SMU), scol(m, SSUM), 1.0 / 1024))
                v.tensor_scalar_mul(scol(m, SVAR), scol(m, SSQS), 1.0 / 1024)
                settle(v, v.tensor_mul(scol(m, SSQS), scol(m, SMU), scol(m, SMU)))
                settle(v, v.tensor_sub(scol(m, SVAR), scol(m, SVAR), scol(m, SSQS)))
                v.tensor_scalar_add(scol(m, SVAR), scol(m, SVAR), 1e-5).then_inc(sv, 1)
                v.wait_ge(sa, SA1 + 4 * m + 4)
                if m >= 2:
                    v.wait_ge(sop[m % 2], 16 * (m // 2))
                v.tensor_scalar(x1b_s(m), res[:], scol(m, SMU), scol(m, SRSTD),
                                ALU.subtract, ALU.mult).then_inc(sv, 1)

        @block.tensor
        def _(t):
            t.wait_ge(lc, 144)
            for gt in range(4):
                t.wait_ge(sv, 2 * gt + 1)
                if gt >= 1:
                    t.wait_ge(sv, 2 * (gt - 1) + 2)
                for fc in range(8):
                    ins_ = t.transpose(p_tr[:, 128 * fc:128 * fc + 128],
                                       normed_s(gt)[:, 128 * fc:128 * fc + 128], ident_sb[:])
                    if fc == 7:
                        ins_.then_inc(stt, 1)
            for i in range(A_NT):
                t.wait_ge(sa, SA0 + 6 * i + 4)
                pv = SV0 + 9 * (i - 1) + 7 if i >= 1 else 8
                t.wait_ge(sv, pv)
                for fc in range(8):
                    ins_ = t.transpose(p_tr[:, 128 * fc:128 * fc + 128],
                                       normed_s(i)[:, 128 * fc:128 * fc + 128], ident_sb[:])
                    if fc == 7:
                        ins_.then_inc(stt, 1)
                t.wait_ge(sv, SV0 + 9 * i + 4)
                if i >= 1:
                    t.wait_ge(sa, SA0 + 6 * (i - 1) + 6)
                for fc in range(8):
                    ins_ = t.matmul(p_sims[:], nfm[:, 128 * fc:128 * fc + 128],
                                    ghat_fm[:, 512 * fc:512 * fc + 512],
                                    start=(fc == 0), stop=(fc == 7))
                    if fc == 7:
                        ins_.then_inc(stt, 1)
                t.wait_ge(sv, SV0 + 9 * i + 6)
                for gc in range(4):
                    ins_ = t.transpose(p_tr[:, 128 * gc:128 * gc + 128],
                                       gw04[:, 128 * gc:128 * gc + 128], ident_sb[:])
                    if gc == 3:
                        ins_.then_inc(stt, 1)
                t.wait_ge(sv, SV0 + 9 * i + 7)
                if i >= 1:
                    t.wait_ge(sv, SV0 + 9 * (i - 1) + 8)
                for half in range(2):
                    for gc in range(4):
                        ins_ = t.matmul(p_infl[:, 512 * half:512 * half + 512],
                                        gwT[:, 128 * gc:128 * gc + 128],
                                        g_gm[:, 1024 * gc + 512 * half:1024 * gc + 512 * half + 512],
                                        start=(gc == 0), stop=(gc == 3))
                        if half == 1 and gc == 3:
                            ins_.then_inc(stt, 1)
            for m in range(A_NS):
                Bc, Bm = bandsel(m)
                t.wait_ge(sv, SV0 + 9 * (m + 1) + 3)
                if m >= 1:
                    t.wait_ge(sv, SV1 + 4 * (m - 1) + 1)
                t.matmul(p_den[:, 0:1], Bc, mwb[:, m:m + 1], start=True, stop=False)
                t.matmul(p_den[:, 0:1], Bm, mwb[:, m + 1:m + 2], start=False, stop=True).then_inc(stt, 1)
                t.wait_ge(sv, SV0 + 9 * (m + 1) + 9)
                if m >= 1:
                    t.wait_ge(sv, SV1 + 4 * (m - 1) + 2)
                for half in range(2):
                    t.matmul(p_num[:, 512 * half:512 * half + 512], Bc, mbl_t(m, half),
                             start=True, stop=False)
                    ins_ = t.matmul(p_num[:, 512 * half:512 * half + 512], Bm, mbl_t(m + 1, half),
                                    start=False, stop=True)
                    if half == 1:
                        ins_.then_inc(stt, 1)

    return x1_out, so0, so1





B_NTQ, B_NFC, B_NTT = 2, 16, 8
B_NG = 32
PE_TQ = 16 + 16 * 48
VE_TQ = 24 + 16 * 48 + 16
SA_TQ = 16 + 16 * 32 + 32


def host_inputs_b(x1_full, wr, wgt, wgate, wup, wdown, core, fused=False):
    b, q = core // 4, core % 4
    r0 = q * 2048
    d = {
        "wrg": np.ascontiguousarray(np.concatenate([wr, wgt], axis=1), dtype=np.float32),
        "w_gate": np.ascontiguousarray(wgate, dtype=np.float32),
        "w_up": np.ascontiguousarray(wup, dtype=np.float32),
        "w_down": np.ascontiguousarray(wdown, dtype=np.float32),
    }
    if not fused:
        d["x1"] = np.ascontiguousarray(x1_full[b, r0:r0 + 2048], dtype=np.float32)
    return d


def build_phase_b(nc, x1_ext=None, gate=None):
    x1_ap = x1_ext if x1_ext is not None else \
        nc.dram_tensor("x1", [2048, 1024], F32, kind="ExternalInput").ap()
    x2_ap = nc.dram_tensor("x2", [2048, 1024], F32, kind="ExternalOutput").ap()
    wrg = nc.dram_tensor("wrg", [1024, 16], F32, kind="ExternalInput").ap()
    wgate = nc.dram_tensor("w_gate", [8, 1024, 4096], F32, kind="ExternalInput").ap()
    wup = nc.dram_tensor("w_up", [8, 1024, 4096], F32, kind="ExternalInput").ap()
    wdown = nc.dram_tensor("w_down", [8, 4096, 1024], F32, kind="ExternalInput").ap()
    ident_c = nc.inline_tensor(np.eye(128, dtype=ml_dtypes.bfloat16), name="identb").ap()

    def pe_b0(TT, k):
        return PE_TQ * (TT // 8) + 2 * (TT % 8) + k
    def pe_gu(G, idx):
        return PE_TQ * (G // 16) + 16 + 48 * (G % 16) + idx + 1
    def pe_y(G, yidx):
        return PE_TQ * (G // 16) + 16 + 48 * (G % 16) + 32 + yidx + 1
    def ve_b0(TT, k):
        return VE_TQ * (TT // 8) + 3 * (TT % 8) + k
    def ve_h(G, idx):
        return VE_TQ * (G // 16) + 24 + 48 * (G % 16) + idx + 1
    def ve_ev(G, yidx):
        return VE_TQ * (G // 16) + 24 + 48 * (G % 16) + 32 + yidx + 1
    def ve_ln(tq, tile, k):
        return VE_TQ * tq + 24 + 16 * 48 + 2 * tile + k
    def sa_b0(TT, k):
        return SA_TQ * (TT // 8) + 2 * (TT % 8) + k
    def sa_silu(G, idx):
        return SA_TQ * (G // 16) + 16 + 32 * (G % 16) + idx + 1
    def sa_ln(tq, tile, k):
        return SA_TQ * tq + 16 + 16 * 32 + 4 * tile + k

    with contextlib.ExitStack() as _st:
        ident_sb = _st.enter_context(nc.sbuf_tensor("b_ident", [128, 128], BF16))
        wrgb = _st.enter_context(nc.sbuf_tensor("b_wrgb", [128, 8 * 16], BF16))
        x_fm = _st.enter_context(nc.sbuf_tensor("b_xfm", [128, 8 * 1024], BF16))
        xbf = _st.enter_context(nc.sbuf_tensor("b_xbf", [128, 2 * 1024], BF16))
        cw = _st.enter_context(nc.sbuf_tensor("b_cw", [128, 8 * 8], F32))
        ccols = _st.enter_context(nc.sbuf_tensor("b_ccols", [128, 8 * 8], F32))
        eg = _st.enter_context(nc.sbuf_tensor("b_eg", [128, 16], F32))
        pgb = _st.enter_context(nc.sbuf_tensor("b_pg", [128, 8], F32))
        wgb = _st.enter_context(nc.sbuf_tensor("b_wgb", [128, 2 * 8 * 512], BF16))
        wub = _st.enter_context(nc.sbuf_tensor("b_wub", [128, 2 * 8 * 512], BF16))
        wdb = _st.enter_context(nc.sbuf_tensor("b_wdb", [128, 2 * 16 * 1024], BF16))
        hb = _st.enter_context(nc.sbuf_tensor("b_h", [128, 16 * 1024], BF16))
        sgb = _st.enter_context(nc.sbuf_tensor("b_sgb", [128, 2 * 512], BF16))
        y_sb = _st.enter_context(nc.sbuf_tensor("b_ysb", [128, 8 * 1024], F32))
        lncols = _st.enter_context(nc.sbuf_tensor("b_lncols", [128, 8 * 8], F32))
        sqscr = _st.enter_context(nc.sbuf_tensor("b_sqscr", [128, 1024], F32))
        x2b = _st.enter_context(nc.sbuf_tensor("b_x2b", [128, 2 * 1024], F32))
        p_tr = _st.enter_context(nc.psum_tensor("b_ptr", [128, 1024], BF16))
        p_rg = _st.enter_context(nc.psum_tensor("b_prg", [128, 16], F32))
        gps = _st.enter_context(nc.psum_tensor("b_gps", [128, 2 * 512], F32))
        ups = _st.enter_context(nc.psum_tensor("b_ups", [128, 2 * 512], F32))
        yps = _st.enter_context(nc.psum_tensor("b_yps", [128, 2 * 512], F32))
        lc = _st.enter_context(nc.semaphore("b_lc"))
        lcg = _st.enter_context(nc.semaphore("b_lcg"))
        lx0 = _st.enter_context(nc.semaphore("b_lx0"))
        lx1 = _st.enter_context(nc.semaphore("b_lx1"))
        lg0 = _st.enter_context(nc.semaphore("b_lg0"))
        lg1 = _st.enter_context(nc.semaphore("b_lg1"))
        lu0 = _st.enter_context(nc.semaphore("b_lu0"))
        lu1 = _st.enter_context(nc.semaphore("b_lu1"))
        lw0 = _st.enter_context(nc.semaphore("b_lw0"))
        lw1 = _st.enter_context(nc.semaphore("b_lw1"))
        ly = _st.enter_context(nc.semaphore("b_ly"))
        so0 = _st.enter_context(nc.semaphore("b_so0"))
        so1 = _st.enter_context(nc.semaphore("b_so1"))
        sa = _st.enter_context(nc.semaphore("b_sa"))
        sv = _st.enter_context(nc.semaphore("b_sv"))
        stp = _st.enter_context(nc.semaphore("b_stp"))
        sgv = _st.enter_context(nc.semaphore("b_sgv"))
        sga = _st.enter_context(nc.semaphore("b_sga"))
        blk = _st.enter_context(nc.Block())

        lxp, lgp, lup, lwp, sop = [lx0, lx1], [lg0, lg1], [lu0, lu1], [lw0, lw1], [so0, so1]
        sgc = {"v": 0, "a": 0}
        def settle(eng, ins, which="v"):
            gsem = {"v": sgv, "a": sga}[which]
            sgc[which] += 1
            ins.then_inc(gsem, 1)
            eng.wait_ge(gsem, sgc[which])

        def xfm_sl(kc, t0, n):
            return x_fm[:, 1024 * kc + t0:1024 * kc + t0 + n]
        def h_sl(fc, t0, n):
            return hb[:, 1024 * fc + t0:1024 * fc + t0 + n]
        def ysb_t(tile):
            return y_sb[:, 1024 * tile:1024 * tile + 1024]
        def cw_col(tile, e):
            return cw[:, 8 * tile + e:8 * tile + e + 1]
        def ccol(tile, j):
            return ccols[:, 8 * tile + j:8 * tile + j + 1]
        def lcol(tile, j):
            return lncols[:, 8 * tile + j:8 * tile + j + 1]
        RMX, RSUM, GMX, GSUM, CPROD, CWS, CRS = range(7)
        SMU, SVAR, SSQS, SSUM, SRSTD = range(5)

        def glist():
            out = []
            for tq in range(2):
                for e in range(8):
                    for fh in range(2):
                        out.append((tq * 16 + e * 2 + fh, tq, e, fh))
            return out

        @blk.sync
        def _(sync):
            sync.dma_start(out=ident_sb[:], in_=ident_c).then_inc(lc, 16)
            for tq in range(2):
                for tile in range(8):
                    if tq >= 1:
                        sync.wait_ge(sv, ve_ln(tq - 1, tile, 2))
                    sync.dma_start(out=ysb_t(tile),
                                   in_=x1_ap[1024 * tq + 128 * tile:1024 * tq + 128 * tile + 128, :]).then_inc(ly, 16)
                for tile in range(8):
                    sync.wait_ge(sv, ve_ln(tq, tile, 2))
                    sync.dma_start(out=x2_ap[1024 * tq + 128 * tile:1024 * tq + 128 * tile + 128, :],
                                   in_=x2b[:, 1024 * (tile % 2):1024 * (tile % 2) + 1024]).then_inc(sop[tile % 2], 16)
            sync.wait_ge(so0, 16 * 8)
            sync.wait_ge(so1, 16 * 8)

        @blk.gpsimd
        def _(g):
            if gate:
                for _gs, _gv in gate:
                    g.wait_ge(_gs, _gv)
            g.dma_start(out=wrgb[:].rearrange("p (kc e) -> p kc e", kc=8),
                        in_=wrg.rearrange("(kc p) e -> p kc e", p=128)).then_inc(lcg, 16)
            for tq in range(2):
                for tile in range(8):
                    TT = 8 * tq + tile
                    if TT >= 2:
                        g.wait_ge(stp, pe_b0(TT - 2, 1))
                    g.dma_start(out=xbf[:, 1024 * (tile % 2):1024 * (tile % 2) + 1024],
                                in_=x1_ap[1024 * tq + 128 * tile:1024 * tq + 128 * tile + 128, :]
                                ).then_inc(lxp[tile % 2], 16)
                for e in range(8):
                    for fh in range(2):
                        G = tq * 16 + e * 2 + fh
                        for p in range(4):
                            if G == 0 and p < 2:
                                pass
                            elif p < 2:
                                g.wait_ge(stp, pe_gu(G - 1, 8 * (p + 2) + 7))
                            else:
                                g.wait_ge(stp, pe_gu(G, 8 * (p - 2) + 7))
                            src = wgate[e, :, 2048 * fh + 512 * p:2048 * fh + 512 * (p + 1)]
                            g.dma_start(out=wgb[:, 4096 * (p % 2):4096 * (p % 2) + 4096]
                                        .rearrange("p (kc f) -> p kc f", kc=8),
                                        in_=src.rearrange("(kc q) f -> q kc f", q=128)
                                        ).then_inc(lgp[p % 2], 16)
                            src = wup[e, :, 2048 * fh + 512 * p:2048 * fh + 512 * (p + 1)]
                            g.dma_start(out=wub[:, 4096 * (p % 2):4096 * (p % 2) + 4096]
                                        .rearrange("p (kc f) -> p kc f", kc=8),
                                        in_=src.rearrange("(kc q) f -> q kc f", q=128)
                                        ).then_inc(lup[p % 2], 16)
                        for p in range(4):
                            if G >= 2 and p == 0:
                                g.wait_ge(stp, pe_y(G - 2, 15))
                            src = wdown[e, 2048 * fh + 512 * p:2048 * fh + 512 * (p + 1), :]
                            g.dma_start(out=wdb[:, 16384 * (G % 2) + 4096 * p:16384 * (G % 2) + 4096 * (p + 1)]
                                        .rearrange("p (fc y) -> p fc y", fc=4),
                                        in_=src.rearrange("(fc q) y -> q fc y", q=128)
                                        ).then_inc(lwp[G % 2], 16)

        @blk.tensor
        def _(t):
            t.wait_ge(lc, 16)
            t.wait_ge(lcg, 16)
            for tq in range(2):
                for tile in range(8):
                    TT = 8 * tq + tile
                    t.wait_ge(lxp[tile % 2], 16 * (4 * tq + tile // 2 + 1))
                    if TT >= 1:
                        t.wait_ge(sv, ve_b0(TT - 1, 1))
                    for fc in range(8):
                        ins = t.transpose(p_tr[:, 128 * fc:128 * fc + 128],
                                          xbf[:, 1024 * (tile % 2) + 128 * fc:1024 * (tile % 2) + 128 * fc + 128],
                                          ident_sb[:])
                        if fc == 7:
                            ins.then_inc(stp, 1)
                    t.wait_ge(sv, ve_b0(TT, 1))
                    if TT >= 1:
                        t.wait_ge(sa, sa_b0(TT - 1, 2))
                    for kc in range(8):
                        ins = t.matmul(p_rg[:], xfm_sl(kc, 128 * tile, 128),
                                       wrgb[:, 16 * kc:16 * kc + 16],
                                       start=(kc == 0), stop=(kc == 7))
                        if kc == 7:
                            ins.then_inc(stp, 1)
                for e in range(8):
                    for fh in range(2):
                        G = tq * 16 + e * 2 + fh
                        for fc in range(16):
                            p = fc // 4
                            for ts in range(2):
                                idx = fc * 2 + ts
                                if ts == 0 and fc % 4 == 0:
                                    t.wait_ge(lgp[p % 2], 16 * (2 * G + p // 2 + 1))
                                    t.wait_ge(lup[p % 2], 16 * (2 * G + p // 2 + 1))
                                prev = idx - 2
                                if prev >= 0:
                                    t.wait_ge(sa, sa_silu(G, prev))
                                    t.wait_ge(sv, ve_h(G, prev))
                                elif G >= 1:
                                    t.wait_ge(sa, sa_silu(G - 1, prev + 32))
                                    t.wait_ge(sv, ve_h(G - 1, prev + 32))
                                par = idx % 2
                                for kc in range(8):
                                    t.matmul(gps[:, 512 * par:512 * par + 512],
                                             wgb[:, 4096 * (p % 2) + 512 * kc + 128 * (fc % 4):
                                                 4096 * (p % 2) + 512 * kc + 128 * (fc % 4) + 128],
                                             xfm_sl(kc, 512 * ts, 512),
                                             start=(kc == 0), stop=(kc == 7))
                                for kc in range(8):
                                    ins = t.matmul(ups[:, 512 * par:512 * par + 512],
                                                   wub[:, 4096 * (p % 2) + 512 * kc + 128 * (fc % 4):
                                                       4096 * (p % 2) + 512 * kc + 128 * (fc % 4) + 128],
                                                   xfm_sl(kc, 512 * ts, 512),
                                                   start=(kc == 0), stop=(kc == 7))
                                    if kc == 7:
                                        ins.then_inc(stp, 1)
                        t.wait_ge(sv, ve_h(G, 31))
                        t.wait_ge(lwp[G % 2], 16 * 4 * (G // 2 + 1))
                        for tt in range(8):
                            for ysl in range(2):
                                yidx = tt * 2 + ysl
                                prev = yidx - 2
                                if prev >= 0:
                                    t.wait_ge(sv, ve_ev(G, prev))
                                elif G >= 1:
                                    t.wait_ge(sv, ve_ev(G - 1, prev + 16))
                                for fc in range(16):
                                    ins = t.matmul(yps[:, 512 * (yidx % 2):512 * (yidx % 2) + 512],
                                                   h_sl(fc, 128 * tt, 128),
                                                   wdb[:, 16384 * (G % 2) + 1024 * fc + 512 * ysl:
                                                       16384 * (G % 2) + 1024 * fc + 512 * ysl + 512],
                                                   start=(fc == 0), stop=(fc == 15))
                                    if fc == 15:
                                        ins.then_inc(stp, 1)

        @blk.scalar
        def _(s):
            for tq in range(2):
                for tile in range(8):
                    TT = 8 * tq + tile
                    s.wait_ge(stp, pe_b0(TT, 2))
                    s.wait_ge(sv, ve_b0(TT, 2))
                    if TT >= 1:
                        s.wait_ge(sv, ve_b0(TT - 1, 3))
                    s.activation(eg[:, 0:8], p_rg[:, 0:8], AF.Exp, bias=ccol(tile, RMX),
                                 accum_out=ccol(tile, RSUM)).then_inc(sa, 1)
                    s.activation(eg[:, 8:16], p_rg[:, 8:16], AF.Exp, scale=10.0, bias=ccol(tile, GMX),
                                 accum_out=ccol(tile, GSUM)).then_inc(sa, 1)
                for e in range(8):
                    for fh in range(2):
                        G = tq * 16 + e * 2 + fh
                        for idx in range(32):
                            s.wait_ge(stp, pe_gu(G, idx))
                            prev = idx - 2
                            if prev >= 0:
                                s.wait_ge(sv, ve_h(G, prev))
                            elif G >= 1:
                                s.wait_ge(sv, ve_h(G - 1, prev + 32))
                            s.activation(sgb[:, 512 * (idx % 2):512 * (idx % 2) + 512],
                                         gps[:, 512 * (idx % 2):512 * (idx % 2) + 512],
                                         AF.Silu).then_inc(sa, 1)
                for tile in range(8):
                    s.wait_ge(sv, ve_ev(tq * 16 + 15, 2 * tile + 2))
                    s.activation(sqscr[:], ysb_t(tile), AF.Square,
                                 accum_out=lcol(tile, SSQS)).then_inc(sa, 1)
                    s.activation(sqscr[:], ysb_t(tile), AF.Copy,
                                 accum_out=lcol(tile, SSUM)).then_inc(sa, 1)
                    s.wait_ge(sv, ve_ln(tq, tile, 1))
                    s.activation(lcol(tile, SRSTD), lcol(tile, SVAR), AF.Ln).then_inc(sa, 1)
                    s.wait_ge(sa, sa_ln(tq, tile, 3))
                    s.activation(lcol(tile, SRSTD), lcol(tile, SRSTD), AF.Exp, scale=-0.5).then_inc(sa, 1)

        @blk.vector
        def _(v):
            for tq in range(2):
                for tile in range(8):
                    TT = 8 * tq + tile
                    v.wait_ge(stp, pe_b0(TT, 1))
                    v.tensor_copy(
                        x_fm[:].rearrange("p (kc t) -> p kc t", kc=8)[:, :, 128 * tile:128 * tile + 128],
                        p_tr[:].rearrange("p (kc t) -> p kc t", kc=8)).then_inc(sv, 1)
                    v.wait_ge(stp, pe_b0(TT, 2))
                    settle(v, v.reduce_max(ccol(tile, RMX), p_rg[:, 0:8], AX.X))
                    v.tensor_scalar_mul(ccol(tile, RMX), ccol(tile, RMX), -1.0)
                    settle(v, v.reduce_max(ccol(tile, GMX), p_rg[:, 8:16], AX.X))
                    v.tensor_scalar_mul(ccol(tile, GMX), ccol(tile, GMX), -10.0).then_inc(sv, 1)
                    v.wait_ge(sa, sa_b0(TT, 2))
                    settle(v, v.tensor_mul(pgb[:], eg[:, 0:8], eg[:, 8:16]))
                    settle(v, v.reciprocal(ccol(tile, CPROD), ccol(tile, RSUM)))
                    settle(v, v.reciprocal(ccol(tile, CWS), ccol(tile, GSUM)))
                    settle(v, v.tensor_mul(ccol(tile, CPROD), ccol(tile, CPROD), ccol(tile, CWS)))
                    settle(v, v.tensor_scalar_mul(pgb[:], pgb[:], ccol(tile, CPROD)))
                    settle(v, v.reduce_sum(ccol(tile, CWS), pgb[:], AX.X))
                    settle(v, v.tensor_scalar_add(ccol(tile, CWS), ccol(tile, CWS), 1e-9))
                    settle(v, v.reciprocal(ccol(tile, CRS), ccol(tile, CWS)))
                    v.tensor_scalar_mul(cw[:, 8 * tile:8 * tile + 8], pgb[:],
                                        ccol(tile, CRS)).then_inc(sv, 1)
                for e in range(8):
                    for fh in range(2):
                        G = tq * 16 + e * 2 + fh
                        for idx in range(32):
                            fc, ts = idx // 2, idx % 2
                            v.wait_ge(sa, sa_silu(G, idx))
                            v.tensor_mul(h_sl(fc, 512 * ts, 512),
                                         sgb[:, 512 * (idx % 2):512 * (idx % 2) + 512],
                                         ups[:, 512 * (idx % 2):512 * (idx % 2) + 512]).then_inc(sv, 1)
                        v.wait_ge(ly, 16 * 8 * (tq + 1))
                        for tt in range(8):
                            for ysl in range(2):
                                yidx = tt * 2 + ysl
                                v.wait_ge(stp, pe_y(G, yidx))
                                v.scalar_tensor_tensor(
                                    y_sb[:, 1024 * tt + 512 * ysl:1024 * tt + 512 * ysl + 512],
                                    yps[:, 512 * (yidx % 2):512 * (yidx % 2) + 512],
                                    cw_col(tt, e),
                                    y_sb[:, 1024 * tt + 512 * ysl:1024 * tt + 512 * ysl + 512],
                                    ALU.mult, ALU.add).then_inc(sv, 1)
                for tile in range(8):
                    v.wait_ge(sa, sa_ln(tq, tile, 2))
                    settle(v, v.tensor_scalar_mul(lcol(tile, SMU), lcol(tile, SSUM), 1.0 / 1024))
                    v.tensor_scalar_mul(lcol(tile, SVAR), lcol(tile, SSQS), 1.0 / 1024)
                    settle(v, v.tensor_mul(lcol(tile, SSQS), lcol(tile, SMU), lcol(tile, SMU)))
                    settle(v, v.tensor_sub(lcol(tile, SVAR), lcol(tile, SVAR), lcol(tile, SSQS)))
                    v.tensor_scalar_add(lcol(tile, SVAR), lcol(tile, SVAR), 1e-5).then_inc(sv, 1)
                    v.wait_ge(sa, sa_ln(tq, tile, 4))
                    if 8 * tq + tile >= 2:
                        TTs = 8 * tq + tile
                        v.wait_ge(sop[tile % 2], 16 * (4 * tq + tile // 2) if tile >= 2 else 16 * 4 * tq)
                    v.tensor_scalar(x2b[:, 1024 * (tile % 2):1024 * (tile % 2) + 1024],
                                    ysb_t(tile), lcol(tile, SMU), lcol(tile, SRSTD),
                                    ALU.subtract, ALU.mult).then_inc(sv, 1)
    return nc





D14 = [0] + [1 << j for j in range(13)]


def host_inputs_c1(x2_full, Wq, bq, Wk, bk, Wv, bv, core):
    b, q = core // 4, core % 4
    r0 = q * 2048
    return {
        "x2": np.ascontiguousarray(x2_full[b, r0:r0 + 2048], dtype=np.float32),
        "wqT": np.ascontiguousarray(Wq.T, dtype=np.float32),
        "wkT": np.ascontiguousarray(Wk.T, dtype=np.float32),
        "wvT": np.ascontiguousarray(Wv.T, dtype=np.float32),
        "bqkv": np.ascontiguousarray(np.stack([bq, bk, bv])[None], dtype=np.float32),
    }


def host_inputs_c2(x2_full, qfm_all, kfm_all, vtm_all, vsum_all, Wo, bo, core):
    b, q = core // 4, core % 4
    r0 = q * 2048
    kwin = np.zeros((1024, 6144), dtype=ml_dtypes.bfloat16)
    vg = np.zeros((6144, 1024), dtype=ml_dtypes.bfloat16)
    for j, qq in enumerate([q - 2, q - 1, q]):
        if qq >= 0:
            kwin[:, 2048 * j:2048 * (j + 1)] = kfm_all[4 * b + qq]
            vg[2048 * j:2048 * (j + 1), :] = vtm_all[4 * b + qq]
    vsum4 = np.concatenate([vsum_all[4 * b + j] for j in range(4)], axis=1)
    return {
        "qfm": np.ascontiguousarray(qfm_all[core]),
        "kwin": np.ascontiguousarray(kwin),
        "vg": np.ascontiguousarray(vg),
        "vsum4": np.ascontiguousarray(vsum4, dtype=np.float32),
        "x2": np.ascontiguousarray(x2_full[b, r0:r0 + 2048], dtype=np.float32),
        "woT": np.ascontiguousarray(Wo.T, dtype=np.float32),
        "bo": np.ascontiguousarray(bo[None], dtype=np.float32),
    }


def build_c1(nc):
    x2 = nc.dram_tensor("x2", [2048, 1024], F32, kind="ExternalInput").ap()
    wqT = nc.dram_tensor("wqT", [1024, 1024], F32, kind="ExternalInput").ap()
    wkT = nc.dram_tensor("wkT", [1024, 1024], F32, kind="ExternalInput").ap()
    wvT = nc.dram_tensor("wvT", [1024, 1024], F32, kind="ExternalInput").ap()
    bqkv = nc.dram_tensor("bqkv", [1, 3, 1024], F32, kind="ExternalInput").ap()
    qfm_o = nc.dram_tensor("qfm", [1024, 2048], BF16, kind="ExternalOutput").ap()
    kfm_o = nc.dram_tensor("kfm", [1024, 2048], BF16, kind="ExternalOutput").ap()
    vtm_o = nc.dram_tensor("vtm", [2048, 1024], BF16, kind="ExternalOutput").ap()
    vs_o = nc.dram_tensor("vsump", [1, 1024], F32, kind="ExternalOutput").ap()
    ident_c = nc.inline_tensor(np.eye(128, dtype=ml_dtypes.bfloat16), name="identc1").ap()
    ones_c = nc.inline_tensor(np.ones((1, 512), dtype=ml_dtypes.bfloat16), name="onesc1").ap()
    onescol_c = nc.inline_tensor(np.ones((128, 1), dtype=ml_dtypes.bfloat16), name="onescolc1").ap()
    onesrow_c = nc.inline_tensor(np.ones((1, 128), dtype=ml_dtypes.bfloat16), name="onesrowc1").ap()

    def pe_tile(i):
        return i + 1
    def pe_proj(j):
        return 16 + j + 1 if j < 64 else 81 + 2 * (j - 64)
    def pe_vs(j):
        return 82 + 2 * (j - 64)
    PE_END = 144
    def ve_tile(i):
        return i + 1
    def ve_proj(j):
        return 16 + j + 1
    VE_VS = 16 + 96 + 1

    with contextlib.ExitStack() as _st:
        ident = _st.enter_context(nc.sbuf_tensor("c_ident", [128, 128], BF16))
        ones_r = _st.enter_context(nc.sbuf_tensor("c_ones", [1, 512], BF16))
        ones_cl = _st.enter_context(nc.sbuf_tensor("c_onescol", [128, 1], BF16))
        ones_rw = _st.enter_context(nc.sbuf_tensor("c_onesrow", [1, 128], BF16))
        bqkv_b = _st.enter_context(nc.sbuf_tensor("c_bqkv", [1, 3 * 1024], BF16))
        wq_b = _st.enter_context(nc.sbuf_tensor("c_wq", [128, 8 * 1024], BF16))
        wk_b = _st.enter_context(nc.sbuf_tensor("c_wk", [128, 8 * 1024], BF16))
        wv_b = _st.enter_context(nc.sbuf_tensor("c_wv", [128, 8 * 1024], BF16))
        x_fm = _st.enter_context(nc.sbuf_tensor("c_xfm", [128, 8 * 2048], BF16))
        xbf = _st.enter_context(nc.sbuf_tensor("c_xbf", [128, 2 * 1024], BF16))
        ev = _st.enter_context(nc.sbuf_tensor("c_ev", [128, 2 * 512], BF16))
        vs_sb = _st.enter_context(nc.sbuf_tensor("c_vs", [1, 1024], F32))
        p_tr = _st.enter_context(nc.psum_tensor("c_ptr", [128, 1024], BF16))
        p_mm = _st.enter_context(nc.psum_tensor("c_pmm", [128, 2 * 512], F32))
        p_vs0 = _st.enter_context(nc.psum_tensor("c_pvs0", [1, 512], F32))
        p_vs1 = _st.enter_context(nc.psum_tensor("c_pvs1", [1, 512], F32))
        lc = _st.enter_context(nc.semaphore("c_lc"))
        lcg = _st.enter_context(nc.semaphore("c_lcg"))
        lx0 = _st.enter_context(nc.semaphore("c_lx0"))
        lx1 = _st.enter_context(nc.semaphore("c_lx1"))
        so0 = _st.enter_context(nc.semaphore("c_so0"))
        so1 = _st.enter_context(nc.semaphore("c_so1"))
        sv = _st.enter_context(nc.semaphore("c_sv"))
        stp = _st.enter_context(nc.semaphore("c_stp"))
        blk = _st.enter_context(nc.Block())
        lxp, sop = [lx0, lx1], [so0, so1]

        @blk.sync
        def _(sync):
            sync.dma_start(out=ident[:], in_=ident_c).then_inc(lc, 16)
            sync.dma_start(out=ones_r[:], in_=ones_c).then_inc(lc, 16)
            sync.dma_start(out=ones_cl[:], in_=onescol_c).then_inc(lc, 16)
            sync.dma_start(out=ones_rw[:], in_=onesrow_c).then_inc(lc, 16)
            for j in range(96):
                sync.wait_ge(sv, ve_proj(j))
                par = j % 2
                if j < 32:
                    oc, ts = j // 4, j % 4
                    dst = qfm_o[128 * oc:128 * oc + 128, 512 * ts:512 * ts + 512]
                elif j < 64:
                    oc, ts = (j - 32) // 4, (j - 32) % 4
                    dst = kfm_o[128 * oc:128 * oc + 128, 512 * ts:512 * ts + 512]
                else:
                    tile, os_ = (j - 64) // 2, (j - 64) % 2
                    dst = vtm_o[128 * tile:128 * tile + 128, 512 * os_:512 * os_ + 512]
                sync.dma_start(out=dst, in_=ev[:, 512 * par:512 * par + 512]).then_inc(sop[par], 16)
            sync.wait_ge(sv, VE_VS)
            sync.dma_start(out=vs_o, in_=vs_sb[:]).then_inc(so0, 16)
            sync.wait_ge(so0, 16 * 49)
            sync.wait_ge(so1, 16 * 48)

        @blk.gpsimd
        def _(g):
            g.dma_start(out=bqkv_b[:].rearrange("p (a f) -> p a f", a=3),
                        in_=bqkv).then_inc(lcg, 16)
            for w_b, w_in in ((wq_b, wqT), (wk_b, wkT), (wv_b, wvT)):
                g.dma_start(out=w_b[:].rearrange("p (ic o) -> p ic o", ic=8),
                            in_=w_in.rearrange("(ic p) o -> p ic o", p=128)).then_inc(lcg, 16)
            for i in range(16):
                if i >= 2:
                    g.wait_ge(stp, pe_tile(i - 2))
                g.dma_start(out=xbf[:, 1024 * (i % 2):1024 * (i % 2) + 1024],
                            in_=x2[128 * i:128 * i + 128, :]).then_inc(lxp[i % 2], 16)

        @blk.tensor
        def _(t):
            t.wait_ge(lc, 64)
            t.wait_ge(lcg, 64)
            for i in range(16):
                t.wait_ge(lxp[i % 2], 16 * (i // 2 + 1))
                if i >= 1:
                    t.wait_ge(sv, ve_tile(i - 1))
                for fc in range(8):
                    ins = t.transpose(p_tr[:, 128 * fc:128 * fc + 128],
                                      xbf[:, 1024 * (i % 2) + 128 * fc:1024 * (i % 2) + 128 * fc + 128],
                                      ident[:])
                    if fc == 7:
                        ins.then_inc(stp, 1)
            for j in range(96):
                par = j % 2
                if j >= 2:
                    t.wait_ge(sv, ve_proj(j - 2))
                if j < 64:
                    isq = j < 32
                    oc, ts = (j % 32) // 4, (j % 32) % 4
                    wb = wq_b if isq else wk_b
                    for ic in range(8):
                        t.matmul(p_mm[:, 512 * par:512 * par + 512],
                                 wb[:, 1024 * ic + 128 * oc:1024 * ic + 128 * oc + 128],
                                 x_fm[:, 2048 * ic + 512 * ts:2048 * ic + 512 * ts + 512],
                                 start=(ic == 0), stop=False)
                    t.matmul(p_mm[:, 512 * par:512 * par + 512],
                             bqkv_b[0:1, 1024 * (0 if isq else 1) + 128 * oc:
                                    1024 * (0 if isq else 1) + 128 * oc + 128],
                             ones_r[:], start=False, stop=True).then_inc(stp, 1)
                else:
                    tile, os_ = (j - 64) // 2, (j - 64) % 2
                    for ic in range(8):
                        t.matmul(p_mm[:, 512 * par:512 * par + 512],
                                 x_fm[:, 2048 * ic + 128 * tile:2048 * ic + 128 * tile + 128],
                                 wv_b[:, 1024 * ic + 512 * os_:1024 * ic + 512 * os_ + 512],
                                 start=(ic == 0), stop=False)
                    t.matmul(p_mm[:, 512 * par:512 * par + 512],
                             ones_rw[:],
                             bqkv_b[0:1, 2048 + 512 * os_:2048 + 512 * os_ + 512],
                             start=False, stop=True).then_inc(stp, 1)
                    t.wait_ge(sv, ve_proj(j))
                    pv = p_vs0 if os_ == 0 else p_vs1
                    t.matmul(pv[:], ones_cl[:], ev[:, 512 * par:512 * par + 512],
                             start=(tile == 0), stop=(tile == 15)).then_inc(stp, 1)

        @blk.vector
        def _(v):
            for i in range(16):
                v.wait_ge(stp, pe_tile(i))
                v.tensor_copy(
                    x_fm[:].rearrange("p (ic t) -> p ic t", ic=8)[:, :, 128 * i:128 * i + 128],
                    p_tr[:].rearrange("p (ic t) -> p ic t", ic=8)).then_inc(sv, 1)
            for j in range(96):
                par = j % 2
                v.wait_ge(stp, pe_proj(j))
                if j >= 2:
                    v.wait_ge(sop[par], 16 * (j // 2))
                    if j - 2 >= 64:
                        v.wait_ge(stp, pe_vs(j - 2))
                v.tensor_copy(ev[:, 512 * par:512 * par + 512],
                              p_mm[:, 512 * par:512 * par + 512]).then_inc(sv, 1)
            v.wait_ge(stp, PE_END)
            v.tensor_copy(vs_sb[:, 0:512], p_vs0[:])
            v.tensor_copy(vs_sb[:, 512:1024], p_vs1[:]).then_inc(sv, 1)

    return nc




def build_c2(nc):
    qfm_i = nc.dram_tensor("qfm", [1024, 2048], BF16, kind="ExternalInput").ap()
    kwin_i = nc.dram_tensor("kwin", [1024, 6144], BF16, kind="ExternalInput").ap()
    vg_i = nc.dram_tensor("vg", [6144, 1024], BF16, kind="ExternalInput").ap()
    vs4_i = nc.dram_tensor("vsum4", [1, 4096], F32, kind="ExternalInput").ap()
    x2_i = nc.dram_tensor("x2", [2048, 1024], F32, kind="ExternalInput").ap()
    woT_i = nc.dram_tensor("woT", [1024, 1024], F32, kind="ExternalInput").ap()
    bo_i = nc.dram_tensor("bo", [1, 1024], F32, kind="ExternalInput").ap()
    x3_o = nc.dram_tensor("x3", [2048, 1024], F32, kind="ExternalOutput").ap()
    ident_c = nc.inline_tensor(np.eye(128, dtype=ml_dtypes.bfloat16), name="identc2").ap()
    onesrow_c = nc.inline_tensor(np.ones((1, 128), dtype=ml_dtypes.bfloat16), name="onesrowc2").ap()
    hs = np.zeros((128, 128), dtype=ml_dtypes.bfloat16)
    for c in range(8):
        hs[0:64, 16 * c + 2 * c] = 1
        hs[64:128, 16 * c + 2 * c + 1] = 1
    hsel_c = nc.inline_tensor(np.ascontiguousarray(hs), name="hselc2").ap()

    C_ND = 14
    def pe_sc(di, c):
        return 9 * di + c + 1
    def pe_sct(di):
        return 9 * di + 9
    PE_SC = 9 * C_ND
    PE_VSB = PE_SC + 1
    def pe_op(qt, k):
        return PE_VSB + 3 * qt + k
    def ve_prod(di, c):
        return 1 + 10 * di + c + 1
    def ve_wev(di):
        return 1 + 10 * di + 9
    def ve_wtmev(di):
        return 1 + 10 * di + 10
    VE_SC = 1 + 10 * C_ND
    VE_VSB = VE_SC + 1
    def ve_qt(qt, k):
        return VE_VSB + 20 * qt + k
    def sa_exp(di, ts):
        return 4 * di + ts + 1
    SA_SC = 4 * C_ND
    def sa_qt(qt, k):
        return SA_SC + 5 * qt + k

    with contextlib.ExitStack() as _st:
        ident = _st.enter_context(nc.sbuf_tensor("d_ident", [128, 128], BF16))
        ones_rw = _st.enter_context(nc.sbuf_tensor("d_onesrow", [1, 128], BF16))
        hsel = _st.enter_context(nc.sbuf_tensor("d_hsel", [128, 128], BF16))
        qfm = _st.enter_context(nc.sbuf_tensor("d_qfm", [128, 8 * 2048], BF16))
        kwin = _st.enter_context(nc.sbuf_tensor("d_kwin", [128, 8 * 6144], BF16))
        woT = _st.enter_context(nc.sbuf_tensor("d_woT", [128, 8 * 1024], BF16))
        bo_b = _st.enter_context(nc.sbuf_tensor("d_bo", [1, 1024], BF16))
        vs4 = _st.enter_context(nc.sbuf_tensor("d_vs4", [1, 1024], F32))
        vsrow = _st.enter_context(nc.sbuf_tensor("d_vsrow", [1, 1024], BF16))
        vsb = _st.enter_context(nc.sbuf_tensor("d_vsb", [128, 1024], BF16))
        prodb = _st.enter_context(nc.sbuf_tensor("d_prod", [128, 2 * 2048], BF16))
        wscb = _st.enter_context(nc.sbuf_tensor("d_wscb", [16, 2048], BF16))
        wtm = _st.enter_context(nc.sbuf_tensor("d_wtm", [128, 16 * 224], BF16))
        zc = _st.enter_context(nc.sbuf_tensor("d_zc", [128, 16 * 16], F32))
        ztmp = _st.enter_context(nc.sbuf_tensor("d_ztmp", [128, 16], F32))
        vshb = _st.enter_context(nc.sbuf_tensor("d_vsh", [128, 2 * 1024], BF16))
        acc = _st.enter_context(nc.sbuf_tensor("d_acc", [128, 2 * 1024], F32))
        tmpb = _st.enter_context(nc.sbuf_tensor("d_tmp", [128, 1024], BF16))
        obf = _st.enter_context(nc.sbuf_tensor("d_obf", [128, 1024], BF16))
        ofm = _st.enter_context(nc.sbuf_tensor("d_ofm", [128, 1024], BF16))
        x2t = _st.enter_context(nc.sbuf_tensor("d_x2t", [128, 1024], F32))
        res = _st.enter_context(nc.sbuf_tensor("d_res", [128, 1024], F32))
        lncols = _st.enter_context(nc.sbuf_tensor("d_lncols", [128, 16 * 16], F32))
        x3b = _st.enter_context(nc.sbuf_tensor("d_x3b", [128, 1024], F32))
        p_pool = _st.enter_context(nc.psum_tensor("d_pool", [128, 2048], F32))
        p_tr = _st.enter_context(nc.psum_tensor("d_ptr", [128, 1024], BF16))
        p_s = p_pool[0:16, :]
        p_vs = p_pool[:, 0:1024]
        p_o = p_pool[:, 0:1024]
        p_scr = p_pool[:, 1024:2048]
        lc = _st.enter_context(nc.semaphore("d_lc"))
        lcg = _st.enter_context(nc.semaphore("d_lcg"))
        lv0 = _st.enter_context(nc.semaphore("d_lv0"))
        lv1 = _st.enter_context(nc.semaphore("d_lv1"))
        lx = _st.enter_context(nc.semaphore("d_lx"))
        so = _st.enter_context(nc.semaphore("d_so"))
        lvs = _st.enter_context(nc.semaphore("d_lvs"))
        svs = _st.enter_context(nc.semaphore("d_svs"))
        sa = _st.enter_context(nc.semaphore("d_sa"))
        sv = _st.enter_context(nc.semaphore("d_sv"))
        stp = _st.enter_context(nc.semaphore("d_stp"))
        sgv = _st.enter_context(nc.semaphore("d_sgv"))
        sga = _st.enter_context(nc.semaphore("d_sga"))
        blk = _st.enter_context(nc.Block())
        lvp = [lv0, lv1]
        sgc = {"v": 0, "a": 0}
        def settle(eng, ins, which="v"):
            gsem = {"v": sgv, "a": sga}[which]
            sgc[which] += 1
            ins.then_inc(gsem, 1)
            eng.wait_ge(gsem, sgc[which])

        SMU, SVAR, SSQS, SSUM, SRSTD = range(5)
        def lcol(qt, j):
            return lncols[:, 16 * qt + j:16 * qt + j + 1]
        def wtm_d(qt, di):
            return wtm[:, 224 * qt + 16 * di:224 * qt + 16 * di + 16]

        @blk.sync
        def _(sync):
            sync.dma_start(out=ident[:], in_=ident_c).then_inc(lc, 16)
            sync.dma_start(out=ones_rw[:], in_=onesrow_c).then_inc(lc, 16)
            sync.dma_start(out=hsel[:], in_=hsel_c).then_inc(lc, 16)
            sync.dma_start(out=qfm[:].rearrange("p (c t) -> p c t", c=8),
                           in_=qfm_i.rearrange("(c p) t -> p c t", p=128)).then_inc(lc, 16)
            sync.dma_start(out=kwin[:].rearrange("p (c t) -> p c t", c=8),
                           in_=kwin_i.rearrange("(c p) t -> p c t", p=128)).then_inc(lc, 16)

            for qt in range(16):
                for di, d in enumerate(D14):
                    gi = qt * C_ND + di
                    par = gi % 2
                    if gi >= 2:
                        pqt, pdi = (gi - 2) // C_ND, (gi - 2) % C_ND
                        sync.wait_ge(sv, ve_qt(pqt, 2 + pdi))
                    sync.dma_start(out=vshb[:, 1024 * par:1024 * par + 1024],
                                   in_=vg_i[4096 + 128 * qt - d:4096 + 128 * qt - d + 128, :]
                                   ).then_inc(lvp[par], 16)
                if qt >= 1:
                    sync.wait_ge(sv, ve_qt(qt - 1, 18))
                sync.dma_start(out=x2t[:],
                               in_=x2_i[128 * qt:128 * qt + 128, :]).then_inc(lx, 16)
                if qt >= 1:
                    sync.wait_ge(sv, ve_qt(qt - 1, 20))
                    sync.dma_start(out=x3_o[128 * (qt - 1):128 * qt, :],
                                   in_=x3b[:]).then_inc(so, 16)
            sync.wait_ge(sv, ve_qt(15, 20))
            sync.dma_start(out=x3_o[128 * 15:128 * 16, :], in_=x3b[:]).then_inc(so, 16)
            sync.wait_ge(so, 16 * 16)

        @blk.gpsimd
        def _(g):
            g.dma_start(out=woT[:].rearrange("p (ic o) -> p ic o", ic=8),
                        in_=woT_i.rearrange("(ic p) o -> p ic o", p=128)).then_inc(lcg, 16)
            g.dma_start(out=bo_b[:], in_=bo_i).then_inc(lcg, 16)
            for pp in range(4):
                g.wait_ge(lvs, 16 * pp)
                g.dma_start(out=vs4[:], in_=vs4_i[:, 1024 * pp:1024 * pp + 1024],
                            accum_op=(ALU.bypass if pp == 0 else ALU.add)).then_inc(lvs, 16)

        @blk.tensor
        def _(t):
            t.wait_ge(lc, 16 * 5)
            t.wait_ge(lcg, 32)
            for di, d in enumerate(D14):
                for c in range(8):
                    t.wait_ge(sv, ve_prod(di, c))
                    if c == 0 and di >= 1:
                        t.wait_ge(sv, ve_wev(di - 1))
                    for ts in range(4):
                        ins = t.matmul(p_s[:, 512 * ts:512 * ts + 512],
                                       hsel[:, 16 * c:16 * c + 16],
                                       prodb[:, 2048 * (c % 2) + 512 * ts:2048 * (c % 2) + 512 * ts + 512],
                                       start=(c == 0), stop=(c == 7))
                        if ts == 3:
                            ins.then_inc(stp, 1)
                t.wait_ge(sv, ve_wev(di))
                if di >= 1:
                    t.wait_ge(sv, ve_wtmev(di - 1))
                for qt in range(16):
                    ins = t.transpose(p_tr[:, 16 * qt:16 * qt + 16],
                                      wscb[0:16, 128 * qt:128 * qt + 128],
                                      ident[0:16, 0:16])
                    if qt == 15:
                        ins.then_inc(stp, 1)
            t.wait_ge(sv, 1)
            t.matmul(p_vs[:, 0:512], ones_rw[:], vsrow[0:1, 0:512], start=True, stop=True)
            t.matmul(p_vs[:, 512:1024], ones_rw[:], vsrow[0:1, 512:1024],
                     start=True, stop=True).then_inc(stp, 1)
            for qt in range(16):
                t.wait_ge(sa, sa_qt(qt, 1))
                if qt >= 1:
                    t.wait_ge(sv, ve_qt(qt - 1, 17))
                else:
                    t.wait_ge(sv, ve_wtmev(C_ND - 1))
                for fc in range(8):
                    ins = t.transpose(p_tr[:, 128 * fc:128 * fc + 128],
                                      obf[:, 128 * fc:128 * fc + 128], ident[:])
                    if fc == 7:
                        ins.then_inc(stp, 1)
                t.wait_ge(sv, ve_qt(qt, 17))
                if qt >= 1:
                    t.wait_ge(sv, ve_qt(qt - 1, 18))
                for os_ in range(2):
                    for ic in range(8):
                        t.matmul(p_o[:, 512 * os_:512 * os_ + 512],
                                 ofm[:, 128 * ic:128 * ic + 128],
                                 woT[:, 1024 * ic + 512 * os_:1024 * ic + 512 * os_ + 512],
                                 start=(ic == 0), stop=False)
                    ins = t.matmul(p_o[:, 512 * os_:512 * os_ + 512],
                                   ones_rw[:], bo_b[0:1, 512 * os_:512 * os_ + 512],
                                   start=False, stop=True)
                    if os_ == 1:
                        ins.then_inc(stp, 1)
                    else:
                        ins.then_inc(stp, 1)

        @blk.scalar
        def _(s):
            for di, d in enumerate(D14):
                for ts in range(4):
                    if ts == 0:
                        s.wait_ge(stp, pe_sc(di, 7))
                    s.activation(p_s[:, 512 * ts:512 * ts + 512],
                                 p_s[:, 512 * ts:512 * ts + 512], AF.Exp,
                                 scale=0.125).then_inc(sa, 1)
            for qt in range(16):
                s.wait_ge(sv, ve_qt(qt, 16))
                if qt >= 1:
                    s.wait_ge(stp, pe_op(qt - 1, 1))
                s.activation(obf[:], acc[:, 1024 * (qt % 2):1024 * (qt % 2) + 1024],
                             AF.Copy).then_inc(sa, 1)
                s.wait_ge(sv, ve_qt(qt, 18))
                s.activation(p_scr, res[:], AF.Square, accum_out=lcol(qt, SSQS)).then_inc(sa, 1)
                s.activation(p_scr, res[:], AF.Copy, accum_out=lcol(qt, SSUM)).then_inc(sa, 1)
                s.wait_ge(sv, ve_qt(qt, 19))
                s.activation(lcol(qt, SRSTD), lcol(qt, SVAR), AF.Ln).then_inc(sa, 1)
                s.wait_ge(sa, sa_qt(qt, 4))
                s.activation(lcol(qt, SRSTD), lcol(qt, SRSTD), AF.Exp, scale=-0.5).then_inc(sa, 1)

        @blk.vector
        def _(v):
            v.wait_ge(lc, 16 * 5)
            v.wait_ge(lvs, 64)
            v.tensor_copy(vsrow[:], vs4[:]).then_inc(sv, 1)
            for di, d in enumerate(D14):
                for c in range(8):
                    if di * 8 + c >= 2:
                        gi = di * 8 + c - 2
                        v.wait_ge(stp, pe_sc(gi // 8, gi % 8))
                    v.tensor_mul(prodb[:, 2048 * (c % 2):2048 * (c % 2) + 2048],
                                 qfm[:, 2048 * c:2048 * c + 2048],
                                 kwin[:, 6144 * c + 4096 - d:6144 * c + 4096 - d + 2048]
                                 ).then_inc(sv, 1)
                v.wait_ge(sa, sa_exp(di, 3))
                if di >= 1:
                    v.wait_ge(stp, pe_sct(di - 1))
                v.tensor_scalar_add(wscb[0:16, 0:2048],
                                    p_s.rearrange("p (ts f) -> p ts f", ts=4).rearrange("p ts f -> p (ts f)"),
                                    -1.0).then_inc(sv, 1)
                v.wait_ge(stp, pe_sct(di))
                v.tensor_copy(
                    wtm[:].rearrange("p (qt x) -> p qt x", qt=16)[:, :, 16 * di:16 * di + 16],
                    p_tr[:, 0:256].rearrange("p (qt x) -> p qt x", qt=16)).then_inc(sv, 1)
            v.wait_ge(stp, PE_VSB)
            v.tensor_copy(vsb[:], p_vs).then_inc(sv, 1)
            for qt in range(16):
                settle(v, v.tensor_reduce(
                    ztmp[:].unsqueeze(2),
                    wtm[:, 224 * qt:224 * qt + 224].rearrange("p (dd h) -> p h dd", dd=C_ND),
                    AX.X, ALU.add))
                settle(v, v.tensor_scalar_add(ztmp[:], ztmp[:], 8192.0))
                settle(v, v.reciprocal(ztmp[:], ztmp[:]))
                v.tensor_copy(zc[:, 16 * qt:16 * qt + 16], ztmp[:]).then_inc(sv, 1)
                for di, d in enumerate(D14):
                    gi = qt * C_ND + di
                    par = gi % 2
                    v.wait_ge(lvp[par], 16 * (gi // 2 + 1))
                    if di == 0 and qt >= 2:
                        v.wait_ge(sa, sa_qt(qt - 2, 1))
                    v.tensor_mul(tmpb[:].rearrange("p (h f) -> p h f", h=16),
                                 vshb[:, 1024 * par:1024 * par + 1024].rearrange("p (h f) -> p h f", h=16),
                                 wtm_d(qt, di).unsqueeze(2).broadcast_to([128, 16, 64]))
                    if di == 0:
                        v.tensor_add(acc[:, 1024 * (qt % 2):1024 * (qt % 2) + 1024],
                                     vsb[:], tmpb[:]).then_inc(sv, 1)
                    else:
                        v.tensor_add(acc[:, 1024 * (qt % 2):1024 * (qt % 2) + 1024],
                                     acc[:, 1024 * (qt % 2):1024 * (qt % 2) + 1024],
                                     tmpb[:]).then_inc(sv, 1)
                v.tensor_mul(acc[:, 1024 * (qt % 2):1024 * (qt % 2) + 1024].rearrange("p (h f) -> p h f", h=16),
                             acc[:, 1024 * (qt % 2):1024 * (qt % 2) + 1024].rearrange("p (h f) -> p h f", h=16),
                             zc[:, 16 * qt:16 * qt + 16].unsqueeze(2).broadcast_to([128, 16, 64])
                             ).then_inc(sv, 1)
                v.wait_ge(stp, pe_op(qt, 1))
                v.tensor_copy(ofm[:], p_tr[:]).then_inc(sv, 1)
                v.wait_ge(stp, pe_op(qt, 3))
                v.wait_ge(lx, 16 * (qt + 1))
                if qt >= 1:
                    v.wait_ge(sa, sa_qt(qt - 1, 3))
                v.tensor_add(res[:], p_o, x2t[:]).then_inc(sv, 1)
                v.wait_ge(sa, sa_qt(qt, 3))
                settle(v, v.tensor_scalar_mul(lcol(qt, SMU), lcol(qt, SSUM), 1.0 / 1024))
                v.tensor_scalar_mul(lcol(qt, SVAR), lcol(qt, SSQS), 1.0 / 1024)
                settle(v, v.tensor_mul(lcol(qt, SSQS), lcol(qt, SMU), lcol(qt, SMU)))
                settle(v, v.tensor_sub(lcol(qt, SVAR), lcol(qt, SVAR), lcol(qt, SSQS)))
                v.tensor_scalar_add(lcol(qt, SVAR), lcol(qt, SVAR), 1e-5).then_inc(sv, 1)
                v.wait_ge(sa, sa_qt(qt, 5))
                if qt >= 1:
                    v.wait_ge(so, 16 * qt)
                v.tensor_scalar(x3b[:],
                                res[:], lcol(qt, SMU), lcol(qt, SRSTD),
                                ALU.subtract, ALU.mult).then_inc(sv, 1)

    return nc





D_NE, D_NA, D_NO = 24, 17, 16


def rotary_tables(r0):
    t = np.arange(r0 - 1024, r0 + 2048, dtype=np.float64)
    half = np.arange(0, 64, 2, dtype=np.float64)
    inv_freq = 1.0 / 10000.0 ** (half / 64.0)
    fr = t[:, None] * inv_freq[None, :]
    freqs = np.concatenate([fr, fr], axis=1)
    base = (half + 0.4 * 64) / (1.4 * 64)
    power = (t - 4096.0) / 256.0
    sc = base[None, :] ** power[:, None]
    sc = np.concatenate([sc, sc], axis=1)
    cos, sin = np.cos(freqs), np.sin(freqs)
    sgn = np.concatenate([-np.ones(32), np.ones(32)])
    T1q = 8.0 * cos * sc
    T2q = 8.0 * sin * sc * sgn[None, :]
    T1k = cos / sc
    T2k = (sin / sc) * sgn[None, :]
    def rep(a):
        return np.ascontiguousarray(np.tile(a, (1, 8)), dtype=np.float32)
    return rep(T1q), rep(T2q), rep(T1k), rep(T2k)


def make_masks(r0):
    m = np.zeros((17, 128, 640), dtype=np.float64)
    i_idx = r0 - 128 + 128 * np.arange(17)[:, None, None] + np.arange(128)[None, :, None]
    j_idx = r0 - 640 + 128 * np.arange(17)[:, None, None] + np.arange(640)[None, None, :]
    bad = (j_idx < i_idx - 512) | (j_idx > i_idx) | (j_idx < 0) | (i_idx < 0)
    m[bad] = -1e30
    return m.astype(ml_dtypes.bfloat16)


def host_inputs_d(xin_full, z, l, core, x3own_full=None):
    b, q = core // 4, core % 4
    r0 = q * 2048
    lo = r0 - 1152
    sl = xin_full[b, max(0, lo):r0 + 2048]
    xin = np.pad(sl, ((max(0, -lo), 0), (0, 0)))
    T1q, T2q, T1k, T2k = rotary_tables(r0)
    out = {
        "xin": np.ascontiguousarray(xin, dtype=np.float32),
        "wqkv": np.ascontiguousarray(z["L1_ffn_wqkv"][l], dtype=np.float32),
        "wo": np.ascontiguousarray(z["L1_ffn_wo"][l], dtype=np.float32),
        "lng": np.ascontiguousarray(z["L1_ffn_ln_g"][l][None], dtype=np.float32),
        "lnb": np.ascontiguousarray(z["L1_ffn_ln_b"][l][None], dtype=np.float32),
        "w1": np.ascontiguousarray(z["L1_ffn_w1"][l], dtype=np.float32),
        "b1": np.ascontiguousarray(z["L1_ffn_b1"][l][None], dtype=np.float32),
        "w2": np.ascontiguousarray(z["L1_ffn_w2"][l], dtype=np.float32),
        "b2": np.ascontiguousarray(z["L1_ffn_b2"][l][None], dtype=np.float32),
        "qs512": np.ascontiguousarray(np.tile(z["L1_ffn_q_scale"][l], 8)[None], dtype=np.float32),
        "ks512": np.ascontiguousarray(np.tile(z["L1_ffn_k_scale"][l], 8)[None], dtype=np.float32),
        "T1q": T1q, "T2q": T2q, "T1k": T1k, "T2k": T2k,
        "masks": make_masks(r0),
    }
    if x3own_full is not None:
        out["x3own"] = np.ascontiguousarray(x3own_full[b, r0:r0 + 2048], dtype=np.float32)
    return out


def w1_slice_parts(s):
    ns = 340 if s == 10 else 512
    c0, c1 = 512 * s, 512 * s + ns
    a = (0, min(c1, 2730) - c0) if c0 < 2730 else None
    if a is not None and a[1] <= a[0]:
        a = None
    g = (max(c0, 2730) - c0, c1 - c0) if c1 > 2730 else None
    if g is not None and g[1] <= g[0]:
        g = None
    return ns, a, g


def build_d(nc, final):
    xin = nc.dram_tensor("xin", [3200, 1024], F32, kind="ExternalInput").ap()
    wqkv_i = nc.dram_tensor("wqkv", [1024, 1536], F32, kind="ExternalInput").ap()
    wo_i = nc.dram_tensor("wo", [512, 1024], F32, kind="ExternalInput").ap()
    lng_i = nc.dram_tensor("lng", [1, 1024], F32, kind="ExternalInput").ap()
    lnb_i = nc.dram_tensor("lnb", [1, 1024], F32, kind="ExternalInput").ap()
    w1_i = nc.dram_tensor("w1", [1024, 5460], F32, kind="ExternalInput").ap()
    b1_i = nc.dram_tensor("b1", [1, 5460], F32, kind="ExternalInput").ap()
    w2_i = nc.dram_tensor("w2", [2730, 1024], F32, kind="ExternalInput").ap()
    b2_i = nc.dram_tensor("b2", [1, 1024], F32, kind="ExternalInput").ap()
    qs_i = nc.dram_tensor("qs512", [1, 512], F32, kind="ExternalInput").ap()
    ks_i = nc.dram_tensor("ks512", [1, 512], F32, kind="ExternalInput").ap()
    t1q_i = nc.dram_tensor("T1q", [3072, 512], F32, kind="ExternalInput").ap()
    t2q_i = nc.dram_tensor("T2q", [3072, 512], F32, kind="ExternalInput").ap()
    t1k_i = nc.dram_tensor("T1k", [3072, 512], F32, kind="ExternalInput").ap()
    t2k_i = nc.dram_tensor("T2k", [3072, 512], F32, kind="ExternalInput").ap()
    masks_i = nc.dram_tensor("masks", [17, 128, 640], BF16, kind="ExternalInput").ap()
    x3own_i = (nc.dram_tensor("x3own", [2048, 1024], F32, kind="ExternalInput").ap()
               if final else None)
    y_o = nc.dram_tensor("y", [2048, 1024], F32, kind="ExternalOutput").ap()
    x4scr = nc.dram_tensor("x4scr", [17 * 128, 1024], F32, kind="Internal").ap()
    ident_c = nc.inline_tensor(np.eye(128, dtype=ml_dtypes.bfloat16), name="identd").ap()
    onesrow_c = nc.inline_tensor(np.ones((1, 128), dtype=ml_dtypes.bfloat16), name="onesrowd").ap()

    def pe_xt(i): return i + 1
    PE_QSB = 26
    def pe_mm(j, sl): return 26 + 4 * j + sl + 1
    def pe_tr2(j): return 26 + 4 * j + 4
    PE_D2 = 26 + 96
    def ve_xt(i): return i + 1
    VE_QSB = 26
    def ve_vev(j): return 26 + 4 * j + 1
    def ve_qproc(j): return 26 + 4 * j + 2
    def ve_kproc(j): return 26 + 4 * j + 3
    def ve_trev(j): return 26 + 4 * j + 4
    VE_D2 = 26 + 96
    def sa_d2(j, k): return 6 * j + k
    SA_D2 = 6 * D_NE

    def pe_sim(qt, h): return PE_D2 + 26 * qt + 3 * h + 1
    def pe_antr(qt, h): return PE_D2 + 26 * qt + 3 * h + 2
    def pe_ov(qt, h): return PE_D2 + 26 * qt + 3 * h + 3
    def pe_otr(qt): return PE_D2 + 26 * qt + 25
    def pe_wo(qt): return PE_D2 + 26 * qt + 26
    PE_D3 = PE_D2 + 26 * D_NA
    def ve_smb(qt, h): return VE_D2 + 27 * qt + 3 * h + 1
    def ve_an(qt, h): return VE_D2 + 27 * qt + 3 * h + 2
    def ve_anT(qt, h): return VE_D2 + 27 * qt + 3 * h + 3
    def ve_oev(qt): return VE_D2 + 27 * qt + 25
    def ve_ofm(qt): return VE_D2 + 27 * qt + 26
    def ve_res(qt): return VE_D2 + 27 * qt + 27
    VE_D3 = VE_D2 + 27 * D_NA
    def sa_eb(qt, h): return SA_D2 + 8 * qt + h + 1
    SA_D3 = SA_D2 + 8 * D_NA

    PE_GB = PE_D3 + 1
    PE_BB = PE_D3 + 2
    def pe4(i, k): return PE_D3 + 2 + 17 * i + k
    VT = 16 if final else 15
    VE_GB = VE_D3 + 1
    VE_BB = VE_D3 + 2
    def ve4(i, k): return VE_D3 + 2 + VT * i + k
    AT = 18 if final else 12
    def sa4(i, k): return SA_D3 + AT * i + k

    with contextlib.ExitStack() as _st:
        blk = _st.enter_context(nc.Block())
        ident = _st.enter_context(nc.sbuf_tensor("e_ident", [128, 128], BF16))
        ones_rw = _st.enter_context(nc.sbuf_tensor("e_onesrow", [1, 128], BF16))
        lc = _st.enter_context(nc.semaphore("e_lc"))
        lcg = _st.enter_context(nc.semaphore("e_lcg"))
        lt0 = _st.enter_context(nc.semaphore("e_lt0"))
        lt1 = _st.enter_context(nc.semaphore("e_lt1"))
        lx0 = _st.enter_context(nc.semaphore("e_lx0"))
        lx1 = _st.enter_context(nc.semaphore("e_lx1"))
        lm0 = _st.enter_context(nc.semaphore("e_lm0"))
        lm1 = _st.enter_context(nc.semaphore("e_lm1"))
        so = _st.enter_context(nc.semaphore("e_so"))
        sx4 = _st.enter_context(nc.semaphore("e_sx4"))
        sa = _st.enter_context(nc.semaphore("e_sa"))
        sv = _st.enter_context(nc.semaphore("e_sv"))
        stp = _st.enter_context(nc.semaphore("e_stp"))
        sgv = _st.enter_context(nc.semaphore("e_sgv"))
        sga = _st.enter_context(nc.semaphore("e_sga"))
        ltp, lxp, lmp = [lt0, lt1], [lx0, lx1], [lm0, lm1]
        p_pool = _st.enter_context(nc.psum_tensor("e_pool", [128, 2048], F32))
        p_tr = _st.enter_context(nc.psum_tensor("e_ptr", [128, 1024], BF16))
        sgc = {"v": 0, "a": 0}
        def settle(eng, ins, which="v"):
            gsem = {"v": sgv, "a": sga}[which]
            sgc[which] += 1
            ins.then_inc(gsem, 1)
            eng.wait_ge(gsem, sgc[which])

        cap = {}

        with contextlib.ExitStack() as stA:
            qfm = stA.enter_context(nc.sbuf_tensor("e_qfm", [128, 4 * 3072], BF16))
            kfm = stA.enter_context(nc.sbuf_tensor("e_kfm", [128, 4 * 3072], BF16))
            vtm = stA.enter_context(nc.sbuf_tensor("e_vtm", [128, 24 * 512], BF16))

            with contextlib.ExitStack() as st2:
                xe_fm = st2.enter_context(nc.sbuf_tensor("e_xefm", [128, 8 * 3200], BF16))
                wqkvb = st2.enter_context(nc.sbuf_tensor("e_wqkv", [128, 8 * 1536], BF16))
                qsb = st2.enter_context(nc.sbuf_tensor("e_qsb", [128, 1024], F32))
                qsrow = st2.enter_context(nc.sbuf_tensor("e_qsrow", [1, 1024], BF16))
                tbl = st2.enter_context(nc.sbuf_tensor("e_tbl", [128, 2 * 2048], F32))
                xbf = st2.enter_context(nc.sbuf_tensor("e_xbf", [128, 2 * 1024], BF16))
                sqb = st2.enter_context(nc.sbuf_tensor("e_sqb", [128, 1024], F32))
                ncol = st2.enter_context(nc.sbuf_tensor("e_ncol", [128, 16], F32))
                qn = st2.enter_context(nc.sbuf_tensor("e_qn", [128, 512], F32))
                tr1 = st2.enter_context(nc.sbuf_tensor("e_tr1", [128, 512], F32))
                tr2 = st2.enter_context(nc.sbuf_tensor("e_tr2", [128, 512], F32))
                qrb = st2.enter_context(nc.sbuf_tensor("e_qrb", [128, 1024], BF16))
                p_q = p_pool[:, 0:1024]
                p_v = p_pool[:, 1024:1536]
                p_qs = p_pool[:, 1024:2048]

                @blk.sync
                def _(sync):
                    sync.dma_start(out=ident[:], in_=ident_c).then_inc(lc, 16)
                    sync.dma_start(out=ones_rw[:], in_=onesrow_c).then_inc(lc, 16)

                    for j in range(D_NE):
                        par = j % 2
                        if j >= 2:
                            sync.wait_ge(sv, ve_kproc(j - 2))
                        sync.dma_start(out=tbl[:, 2048 * par + 0:2048 * par + 512],
                                       in_=t1q_i[128 * j:128 * j + 128, :]).then_inc(ltp[par], 16)
                        sync.dma_start(out=tbl[:, 2048 * par + 512:2048 * par + 1024],
                                       in_=t2q_i[128 * j:128 * j + 128, :]).then_inc(ltp[par], 16)
                        sync.dma_start(out=tbl[:, 2048 * par + 1024:2048 * par + 1536],
                                       in_=t1k_i[128 * j:128 * j + 128, :]).then_inc(ltp[par], 16)
                        sync.dma_start(out=tbl[:, 2048 * par + 1536:2048 * par + 2048],
                                       in_=t2k_i[128 * j:128 * j + 128, :]).then_inc(ltp[par], 16)

                @blk.gpsimd
                def _(g):
                    g.dma_start(out=wqkvb[:].rearrange("p (c o) -> p c o", c=8),
                                in_=wqkv_i.rearrange("(c p) o -> p c o", p=128)).then_inc(lcg, 16)
                    g.dma_start(out=qsrow[0:1, 0:512], in_=qs_i).then_inc(lcg, 16)
                    g.dma_start(out=qsrow[0:1, 512:1024], in_=ks_i).then_inc(lcg, 16)
                    for i in range(25):
                        if i >= 2:
                            g.wait_ge(stp, pe_xt(i - 2))
                        g.dma_start(out=xbf[:, 1024 * (i % 2):1024 * (i % 2) + 1024],
                                    in_=xin[128 * i:128 * i + 128, :]).then_inc(lxp[i % 2], 16)

                @blk.vector
                def _(v):
                    for i in range(25):
                        v.wait_ge(stp, pe_xt(i))
                        v.tensor_copy(
                            xe_fm[:].rearrange("p (c t) -> p c t", c=8)[:, :, 128 * i:128 * i + 128],
                            p_tr[:].rearrange("p (c t) -> p c t", c=8)).then_inc(sv, 1)
                    v.wait_ge(stp, PE_QSB)
                    v.tensor_copy(qsb[:], p_qs).then_inc(sv, 1)
                    for j in range(D_NE):
                        v.wait_ge(stp, pe_mm(j, 2))
                        v.tensor_copy(vtm[:, 512 * j:512 * j + 512], p_v).then_inc(sv, 1)
                        v.wait_ge(ltp[j % 2], 64 * (j // 2 + 1))
                        for isq in (True, False):
                            half = 0 if isq else 512
                            v.wait_ge(sa, sa_d2(j, 1 if isq else 2))
                            settle(v, v.tensor_reduce(
                                ncol[:, half // 64:half // 64 + 8].unsqueeze(2),
                                sqb[:, half:half + 512].rearrange("p (h f) -> p h f", h=8),
                                AX.X, ALU.add))
                            settle(v, v.tensor_scalar_max(
                                ncol[:, half // 64:half // 64 + 8],
                                ncol[:, half // 64:half // 64 + 8], 1e-30))
                            cap[("n2", j, isq)] = sgc["v"]
                            v.wait_ge(sa, sa_d2(j, 4 if isq else 6))
                            v.tensor_scalar_max(ncol[:, half // 64:half // 64 + 8],
                                                ncol[:, half // 64:half // 64 + 8], 1e-12)
                            settle(v, v.reciprocal(ncol[:, half // 64:half // 64 + 8],
                                                   ncol[:, half // 64:half // 64 + 8]))
                            v.tensor_mul(qn[:].rearrange("p (h f) -> p h f", h=8),
                                         p_q[:, half:half + 512].rearrange("p (h f) -> p h f", h=8),
                                         ncol[:, half // 64:half // 64 + 8].unsqueeze(2)
                                         .broadcast_to([128, 8, 64]))
                            v.tensor_mul(qn[:], qn[:], qsb[:, half:half + 512])
                            toff = 2048 * (j % 2) + (0 if isq else 1024)
                            v.tensor_mul(tr1[:], qn[:], tbl[:, toff:toff + 512])
                            qv = qn[:].rearrange("p (h s f) -> p h s f", h=8, s=2)
                            t2v = tbl[:, toff + 512:toff + 1024].rearrange("p (h s f) -> p h s f", h=8, s=2)
                            o2 = tr2[:].rearrange("p (h s f) -> p h s f", h=8, s=2)
                            v.tensor_mul(o2[:, :, 0:1, :], qv[:, :, 1:2, :], t2v[:, :, 0:1, :])
                            v.tensor_mul(o2[:, :, 1:2, :], qv[:, :, 0:1, :], t2v[:, :, 1:2, :])
                            v.tensor_add(qrb[:, half:half + 512], tr1[:], tr2[:]).then_inc(sv, 1)
                        v.wait_ge(stp, pe_tr2(j))
                        v.tensor_copy(
                            qfm[:].rearrange("p (hp t) -> p hp t", hp=4)[:, :, 128 * j:128 * j + 128],
                            p_tr[:, 0:512].rearrange("p (hp t) -> p hp t", hp=4))
                        v.tensor_copy(
                            kfm[:].rearrange("p (hp t) -> p hp t", hp=4)[:, :, 128 * j:128 * j + 128],
                            p_tr[:, 512:1024].rearrange("p (hp t) -> p hp t", hp=4)).then_inc(sv, 1)

                @blk.tensor
                def _(t):
                    t.wait_ge(lc, 32)
                    t.wait_ge(lcg, 48)
                    for i in range(25):
                        t.wait_ge(lxp[i % 2], 16 * (i // 2 + 1))
                        if i >= 1:
                            t.wait_ge(sv, ve_xt(i - 1))
                        for fc in range(8):
                            ins = t.transpose(p_tr[:, 128 * fc:128 * fc + 128],
                                              xbf[:, 1024 * (i % 2) + 128 * fc:1024 * (i % 2) + 128 * fc + 128],
                                              ident[:])
                            if fc == 7:
                                ins.then_inc(stp, 1)
                    t.wait_ge(sv, 25)
                    t.matmul(p_qs[:, 0:512], ones_rw[:], qsrow[0:1, 0:512], start=True, stop=True)
                    t.matmul(p_qs[:, 512:1024], ones_rw[:], qsrow[0:1, 512:1024],
                             start=True, stop=True).then_inc(stp, 1)
                    for j in range(D_NE):
                        for sl in range(3):
                            if sl == 0:
                                t.wait_ge(sv, ve_qproc(j - 1) if j >= 1 else VE_QSB)
                            elif sl == 1:
                                if j >= 1:
                                    t.wait_ge(sv, ve_kproc(j - 1))
                            else:
                                t.wait_ge(sv, ve_vev(j - 1) if j >= 1 else VE_QSB)
                            dst = (p_q[:, 0:512] if sl == 0 else
                                   p_q[:, 512:1024] if sl == 1 else p_v)
                            for c in range(8):
                                off = 128 if c < 4 else 127
                                ins = t.matmul(dst,
                                               xe_fm[:, 3200 * c + off + 128 * j:3200 * c + off + 128 * j + 128],
                                               wqkvb[:, 1536 * c + 512 * sl:1536 * c + 512 * sl + 512],
                                               start=(c == 0), stop=(c == 7))
                                if c == 7:
                                    ins.then_inc(stp, 1)
                        t.wait_ge(sv, ve_kproc(j))
                        if j >= 1:
                            t.wait_ge(sv, ve_trev(j - 1))
                        for hp in range(4):
                            t.transpose(p_tr[:, 128 * hp:128 * hp + 128],
                                        qrb[:, 128 * hp:128 * hp + 128], ident[:])
                        for hp in range(4):
                            ins = t.transpose(p_tr[:, 512 + 128 * hp:512 + 128 * hp + 128],
                                              qrb[:, 512 + 128 * hp:512 + 128 * hp + 128], ident[:])
                            if hp == 3:
                                ins.then_inc(stp, 1)

                @blk.scalar
                def _(s):
                    for j in range(D_NE):
                        s.wait_ge(stp, pe_mm(j, 0))
                        if j >= 1:
                            s.wait_ge(sgv, cap[("n2", j - 1, False)])
                        s.activation(sqb[:, 0:512], p_q[:, 0:512], AF.Square).then_inc(sa, 1)
                        s.wait_ge(stp, pe_mm(j, 1))
                        s.activation(sqb[:, 512:1024], p_q[:, 512:1024], AF.Square).then_inc(sa, 1)
                        s.wait_ge(sgv, cap[("n2", j, True)])
                        s.activation(ncol[:, 0:8], ncol[:, 0:8], AF.Ln).then_inc(sa, 1)
                        s.wait_ge(sa, sa_d2(j, 3))
                        s.activation(ncol[:, 0:8], ncol[:, 0:8], AF.Exp, scale=0.5).then_inc(sa, 1)
                        s.wait_ge(sgv, cap[("n2", j, False)])
                        s.activation(ncol[:, 8:16], ncol[:, 8:16], AF.Ln).then_inc(sa, 1)
                        s.wait_ge(sa, sa_d2(j, 5))
                        s.activation(ncol[:, 8:16], ncol[:, 8:16], AF.Exp, scale=0.5).then_inc(sa, 1)

            with contextlib.ExitStack() as st3:
                maskb = st3.enter_context(nc.sbuf_tensor("e_maskb", [128, 2 * 640], BF16))
                smb = st3.enter_context(nc.sbuf_tensor("e_smb", [128, 640], F32))
                eb = st3.enter_context(nc.sbuf_tensor("e_eb", [128, 640], BF16))
                anb = st3.enter_context(nc.sbuf_tensor("e_anb", [128, 640], BF16))
                anT = st3.enter_context(nc.sbuf_tensor("e_anT", [128, 640], BF16))
                otm = st3.enter_context(nc.sbuf_tensor("e_otm", [128, 512], BF16))
                ofm = st3.enter_context(nc.sbuf_tensor("e_ofm", [128, 512], BF16))
                wob = st3.enter_context(nc.sbuf_tensor("e_wob", [128, 4 * 1024], BF16))
                xres = st3.enter_context(nc.sbuf_tensor("e_xres", [128, 2 * 1024], F32))
                x4b = st3.enter_context(nc.sbuf_tensor("e_x4b", [128, 1024], F32))
                accol = st3.enter_context(nc.sbuf_tensor("e_accol", [128, 2 * 17], F32))
                p_s = p_pool[:, 0:640]
                p_o3 = p_pool[:, 1024:1536]
                p_wo = p_pool[:, 0:1024]

                @blk.sync
                def _(sync):
                    sync.wait_ge(stp, PE_D2)
                    sync.wait_ge(sv, VE_D2)
                    sync.wait_ge(sa, SA_D2)
                    for qt in range(D_NA):
                        par = qt % 2
                        if qt >= 2:
                            sync.wait_ge(sv, ve_res(qt - 2))
                        sync.dma_start(out=maskb[:, 640 * par:640 * par + 640],
                                       in_=masks_i[qt]).then_inc(lmp[par], 16)
                        sync.dma_start(out=xres[:, 1024 * par:1024 * par + 1024],
                                       in_=xin[1024 + 128 * qt:1024 + 128 * qt + 128, :]
                                       ).then_inc(lmp[par], 16)
                        sync.wait_ge(sv, ve_res(qt))
                        sync.dma_start(out=x4scr[128 * qt:128 * qt + 128, :],
                                       in_=x4b[:]).then_inc(sx4, 16)

                @blk.gpsimd
                def _(g):
                    g.wait_ge(stp, PE_D2)
                    g.wait_ge(sv, VE_D2)
                    g.wait_ge(sa, SA_D2)
                    g.dma_start(out=wob[:].rearrange("p (c o) -> p c o", c=4),
                                in_=wo_i.rearrange("(c p) o -> p c o", p=128)).then_inc(lcg, 16)

                @blk.vector
                def _(v):
                    for qt in range(D_NA):
                        par = qt % 2
                        for h in range(8):
                            v.wait_ge(stp, pe_sim(qt, h))
                            if h == 0:
                                v.wait_ge(lmp[par], 32 * (qt // 2) + 16)
                            v.tensor_add(smb[:], p_s, maskb[:, 640 * par:640 * par + 640]
                                         ).then_inc(sv, 1)
                            settle(v, v.reduce_max(accol[:, 17 * 0 + qt:17 * 0 + qt + 1], smb[:], AX.X))
                            settle(v, v.tensor_scalar_mul(accol[:, qt:qt + 1],
                                                          accol[:, qt:qt + 1], -1.0))
                            cap[("nm", qt, h)] = sgc["v"]
                            v.wait_ge(sa, sa_eb(qt, h))
                            settle(v, v.reciprocal(accol[:, 17 + qt:17 + qt + 1],
                                                   accol[:, 17 + qt:17 + qt + 1]))
                            v.tensor_scalar_mul(anb[:], eb[:], accol[:, 17 + qt:17 + qt + 1]
                                                ).then_inc(sv, 1)
                            v.wait_ge(stp, pe_antr(qt, h))
                            v.tensor_copy(anT[:], p_tr[:, 0:640]).then_inc(sv, 1)
                        v.wait_ge(stp, pe_ov(qt, 7))
                        v.tensor_copy(otm[:], p_o3).then_inc(sv, 1)
                        v.wait_ge(stp, pe_otr(qt))
                        v.tensor_copy(ofm[:], p_tr[:, 0:512]).then_inc(sv, 1)
                        v.wait_ge(stp, pe_wo(qt))
                        v.wait_ge(lmp[par], 32 * (qt // 2 + 1))
                        v.tensor_add(x4b[:], p_wo, xres[:, 1024 * par:1024 * par + 1024]
                                     ).then_inc(sv, 1)

                @blk.tensor
                def _(t):
                    t.wait_ge(lcg, 64)
                    for qt in range(D_NA):
                        for h in range(8):
                            hp, hi = h // 2, h % 2
                            if h >= 1:
                                t.wait_ge(sv, ve_smb(qt, h - 1))
                            elif qt >= 1:
                                t.wait_ge(sv, ve_res(qt - 1))
                            for kb in range(5):
                                kt = qt + 3 + kb
                                ins = t.matmul(
                                    p_s[:, 128 * kb:128 * kb + 128],
                                    qfm[64 * hi:64 * hi + 64,
                                        3072 * hp + 128 * (qt + 7):3072 * hp + 128 * (qt + 7) + 128],
                                    kfm[64 * hi:64 * hi + 64,
                                        3072 * hp + 128 * kt:3072 * hp + 128 * kt + 128],
                                    start=True, stop=True)
                                if kb == 4:
                                    ins.then_inc(stp, 1)
                            t.wait_ge(sv, ve_an(qt, h))
                            if h >= 1:
                                t.wait_ge(sv, ve_anT(qt, h - 1))
                            elif qt >= 1:
                                t.wait_ge(sv, ve_ofm(qt - 1))
                            for kb in range(5):
                                ins = t.transpose(p_tr[:, 128 * kb:128 * kb + 128],
                                                  anb[:, 128 * kb:128 * kb + 128], ident[:])
                                if kb == 4:
                                    ins.then_inc(stp, 1)
                            t.wait_ge(sv, ve_anT(qt, h))
                            for kb in range(5):
                                kt = qt + 3 + kb
                                ins = t.matmul(p_o3[:, 64 * h:64 * h + 64],
                                               anT[:, 128 * kb:128 * kb + 128],
                                               vtm[:, 512 * kt + 64 * h:512 * kt + 64 * h + 64],
                                               start=(kb == 0), stop=(kb == 4))
                                if kb == 4:
                                    ins.then_inc(stp, 1)
                        t.wait_ge(sv, ve_oev(qt))
                        for ic in range(4):
                            ins = t.transpose(p_tr[:, 128 * ic:128 * ic + 128],
                                              otm[:, 128 * ic:128 * ic + 128], ident[:])
                            if ic == 3:
                                ins.then_inc(stp, 1)
                        t.wait_ge(sv, ve_ofm(qt))
                        for os_ in range(2):
                            for ic in range(4):
                                ins = t.matmul(p_wo[:, 512 * os_:512 * os_ + 512],
                                               ofm[:, 128 * ic:128 * ic + 128],
                                               wob[:, 1024 * ic + 512 * os_:1024 * ic + 512 * os_ + 512],
                                               start=(ic == 0), stop=(ic == 3))
                                if os_ == 1 and ic == 3:
                                    ins.then_inc(stp, 1)

                @blk.scalar
                def _(s):
                    for qt in range(D_NA):
                        for h in range(8):
                            s.wait_ge(sgv, cap[("nm", qt, h)])
                            if qt * 8 + h >= 1:
                                ph = (qt * 8 + h - 1)
                                s.wait_ge(sv, ve_an(ph // 8, ph % 8))
                            s.activation(eb[:], smb[:], AF.Exp, bias=accol[:, qt:qt + 1],
                                         accum_out=accol[:, 17 + qt:17 + qt + 1]).then_inc(sa, 1)


        with contextlib.ExitStack() as st4:
            w1b = st4.enter_context(nc.sbuf_tensor("e_w1b", [128, 8 * 5460], BF16))
            w2b = st4.enter_context(nc.sbuf_tensor("e_w2b", [128, 22 * 1024], BF16))
            lngb = st4.enter_context(nc.sbuf_tensor("e_lngb", [1, 2048], BF16))
            b1b = st4.enter_context(nc.sbuf_tensor("e_b1b", [1, 5460], BF16))
            b2b = st4.enter_context(nc.sbuf_tensor("e_b2b", [1, 1024], BF16))
            gb = st4.enter_context(nc.sbuf_tensor("e_gb", [128, 1024], F32))
            bb = st4.enter_context(nc.sbuf_tensor("e_bb", [128, 1024], F32))
            x4c = st4.enter_context(nc.sbuf_tensor("e_x4c", [128, 1024], F32))
            x4s = st4.enter_context(nc.sbuf_tensor("e_x4s", [128, 1024], F32))
            h2n = st4.enter_context(nc.sbuf_tensor("e_h2n", [128, 1024], F32))
            h2nb = st4.enter_context(nc.sbuf_tensor("e_h2nb", [128, 1024], BF16))
            h2fm = st4.enter_context(nc.sbuf_tensor("e_h2fm", [128, 1024], BF16))
            w1a = st4.enter_context(nc.sbuf_tensor("e_w1a", [128, 2730], BF16))
            w1g = st4.enter_context(nc.sbuf_tensor("e_w1g", [128, 2730], BF16))
            agb = st4.enter_context(nc.sbuf_tensor("e_agb", [128, 2730], BF16))
            agfm = st4.enter_context(nc.sbuf_tensor("e_agfm", [128, 2816], BF16))
            ybuf = st4.enter_context(nc.sbuf_tensor("e_ybuf", [128, 1024], F32))
            res2 = st4.enter_context(nc.sbuf_tensor("e_res2", [128, 1024], F32)) if final else None
            x3ob = st4.enter_context(nc.sbuf_tensor("e_x3ob", [128, 1024], F32)) if final else None
            sscr = st4.enter_context(nc.sbuf_tensor("e_sscr", [128, 512], F32))
            ln4 = st4.enter_context(nc.sbuf_tensor("e_ln4", [128, 16 * 16], F32))
            SMU, SVAR, SSQS, SSUM, SRSTD, SQ2, SM2, SV2, SR2 = range(9)
            def l4(i, k):
                return ln4[:, 16 * i + k:16 * i + k + 1]

            NLD = 3 if final else 2

            @blk.sync
            def _(sync):
                sync.wait_ge(stp, PE_D3)
                sync.wait_ge(sv, VE_D3)
                sync.wait_ge(sa, SA_D3)
                sync.wait_ge(sx4, 16 * D_NA)
                for i in range(D_NO):
                    par = i % 2
                    if i >= 1:
                        sync.wait_ge(sv, ve4(i - 1, 14))
                    r = 128 * (i + 1)
                    sync.dma_start(out=x4c[:], in_=x4scr[r:r + 128, :]).then_inc(ltp[par], 16)
                    sync.dma_start(out=x4s[:], in_=x4scr[r - 1:r + 127, :]).then_inc(ltp[par], 16)
                    if final:
                        sync.dma_start(out=x3ob[:], in_=x3own_i[128 * i:128 * i + 128, :]
                                       ).then_inc(ltp[par], 16)
                    sync.wait_ge(sv, ve4(i, VT))
                    sync.dma_start(out=y_o[128 * i:128 * i + 128, :],
                                   in_=ybuf[:] if not final else res2[:]).then_inc(so, 16)
                sync.wait_ge(so, 16 * D_NO)

            @blk.gpsimd
            def _(g):
                g.wait_ge(stp, PE_D3)
                g.wait_ge(sv, VE_D3)
                g.wait_ge(sa, SA_D3)
                g.dma_start(out=lngb[0:1, 0:1024], in_=lng_i).then_inc(lcg, 16)
                g.dma_start(out=lngb[0:1, 1024:2048], in_=lnb_i).then_inc(lcg, 16)
                g.dma_start(out=b1b[:], in_=b1_i).then_inc(lcg, 16)
                g.dma_start(out=b2b[:], in_=b2_i).then_inc(lcg, 16)
                g.dma_start(out=w1b[:].rearrange("p (c f) -> p c f", c=8),
                            in_=w1_i.rearrange("(c p) f -> p c f", p=128)).then_inc(lcg, 16)
                g.dma_start(out=w2b[:, 0:21 * 1024].rearrange("p (c y) -> p c y", c=21),
                            in_=w2_i[0:2688, :].rearrange("(c p) y -> p c y", p=128)
                            ).then_inc(lcg, 16)
                g.dma_start(out=w2b[0:42, 21 * 1024:22 * 1024],
                            in_=w2_i[2688:2730, :]).then_inc(lcg, 16)

            @blk.vector
            def _(v):
                v.wait_ge(stp, PE_GB)
                v.tensor_copy(gb[:], p_pool[:, 0:1024]).then_inc(sv, 1)
                v.wait_ge(stp, PE_BB)
                v.tensor_copy(bb[:], p_pool[:, 0:1024]).then_inc(sv, 1)
                for i in range(D_NO):
                    par = i % 2
                    v.wait_ge(sa, sa4(i, 4))
                    settle(v, v.tensor_add(l4(i, SSQS), l4(i, SSQS), l4(i, SQ2)))
                    settle(v, v.tensor_add(l4(i, SSUM), l4(i, SSUM), l4(i, SM2)))
                    settle(v, v.tensor_scalar_mul(l4(i, SMU), l4(i, SSUM), 1.0 / 1024))
                    v.tensor_scalar_mul(l4(i, SVAR), l4(i, SSQS), 1.0 / 1024)
                    settle(v, v.tensor_mul(l4(i, SQ2), l4(i, SMU), l4(i, SMU)))
                    settle(v, v.tensor_sub(l4(i, SVAR), l4(i, SVAR), l4(i, SQ2)))
                    settle(v, v.tensor_scalar_add(l4(i, SVAR), l4(i, SVAR), 1e-5))
                    cap[("var", i)] = sgc["v"]
                    v.wait_ge(sa, sa4(i, 6))
                    v.tensor_scalar(h2n[:, 0:512], x4c[:, 0:512],
                                    l4(i, SMU), l4(i, SRSTD), ALU.subtract, ALU.mult)
                    v.tensor_scalar(h2n[:, 512:1024], x4s[:, 512:1024],
                                    l4(i, SMU), l4(i, SRSTD), ALU.subtract, ALU.mult)
                    v.tensor_mul(h2n[:], h2n[:], gb[:])
                    v.tensor_add(h2nb[:], h2n[:], bb[:]).then_inc(sv, 1)
                    v.wait_ge(stp, pe4(i, 1))
                    v.tensor_copy(h2fm[:], p_tr[:]).then_inc(sv, 1)
                    for s_ in range(6):
                        ns, a, gpart = w1_slice_parts(s_)
                        v.wait_ge(stp, pe4(i, 2 + s_))
                        pa = p_pool[:, 512 * (s_ % 2):512 * (s_ % 2) + 512]
                        v.tensor_copy(w1a[:, 512 * s_ + a[0]:512 * s_ + a[1]],
                                      pa[:, a[0]:a[1]]).then_inc(sv, 1)
                    v.wait_ge(sa, sa4(i, 12))
                    v.tensor_mul(agb[:], w1a[:], w1g[:]).then_inc(sv, 1)
                    for g3 in range(3):
                        v.wait_ge(stp, pe4(i, 13 + g3))
                        if g3 < 2:
                            v.tensor_copy(agfm[:, 1024 * g3:1024 * g3 + 1024],
                                          p_tr[:]).then_inc(sv, 1)
                        else:
                            v.tensor_copy(agfm[:, 2048:2048 + 640], p_tr[:, 0:640])
                            v.tensor_copy(agfm[0:42, 2688:2816],
                                          p_tr[0:42, 640:768]).then_inc(sv, 1)
                    for os_ in range(2):
                        v.wait_ge(stp, pe4(i, 16 + os_))
                        if i >= 1 and os_ == 0:
                            v.wait_ge(so, 16 * i)
                        v.tensor_add(ybuf[:, 512 * os_:512 * os_ + 512],
                                     p_pool[:, 512 * os_:512 * os_ + 512],
                                     x4c[:, 512 * os_:512 * os_ + 512]).then_inc(sv, 1)
                    if final:
                        v.tensor_add(res2[:], ybuf[:], x3ob[:]).then_inc(sv, 1)
                        v.wait_ge(sa, sa4(i, 16))
                        settle(v, v.tensor_add(l4(i, SQ2), l4(i, SQ2), l4(i, SV2)))
                        settle(v, v.tensor_add(l4(i, SM2), l4(i, SM2), l4(i, 9)))
                        settle(v, v.tensor_scalar_mul(l4(i, SM2), l4(i, SM2), 1.0 / 1024))
                        v.tensor_scalar_mul(l4(i, SV2), l4(i, SQ2), 1.0 / 1024)
                        settle(v, v.tensor_mul(l4(i, SQ2), l4(i, SM2), l4(i, SM2)))
                        settle(v, v.tensor_sub(l4(i, SV2), l4(i, SV2), l4(i, SQ2)))
                        settle(v, v.tensor_scalar_add(l4(i, SV2), l4(i, SV2), 1e-5))
                        cap[("var2", i)] = sgc["v"]
                        v.wait_ge(sa, sa4(i, 18))
                        v.tensor_scalar(res2[:], res2[:], l4(i, SM2), l4(i, SR2),
                                        ALU.subtract, ALU.mult).then_inc(sv, 1)
                    else:
                        v.engine_nop().then_inc(sv, 1)

            @blk.tensor
            def _(t):
                t.wait_ge(lcg, 16 * 11)
                t.wait_ge(sv, ve_res(D_NA - 1))
                for half in range(2):
                    ins = t.matmul(p_pool[:, 512 * half:512 * half + 512], ones_rw[:],
                                   lngb[0:1, 512 * half:512 * half + 512], start=True, stop=True)
                    if half == 1:
                        ins.then_inc(stp, 1)
                t.wait_ge(sv, VE_GB)
                for half in range(2):
                    ins = t.matmul(p_pool[:, 512 * half:512 * half + 512], ones_rw[:],
                                   lngb[0:1, 1024 + 512 * half:1024 + 512 * half + 512],
                                   start=True, stop=True)
                    if half == 1:
                        ins.then_inc(stp, 1)
                for i in range(D_NO):
                    t.wait_ge(sv, ve4(i, 1))
                    if i >= 1:
                        t.wait_ge(sv, ve4(i - 1, 12))
                    else:
                        t.wait_ge(sv, VE_BB)
                    for fc in range(8):
                        ins = t.transpose(p_tr[:, 128 * fc:128 * fc + 128],
                                          h2nb[:, 128 * fc:128 * fc + 128], ident[:])
                        if fc == 7:
                            ins.then_inc(stp, 1)
                    t.wait_ge(sv, ve4(i, 2))
                    for s_ in range(11):
                        ns, a, gpart = w1_slice_parts(s_)
                        prev = s_ - 2
                        if prev >= 0:
                            _, pa_, pg_ = w1_slice_parts(prev)
                            if pa_ is not None:
                                t.wait_ge(sv, ve4(i, 3 + prev))
                            if pg_ is not None:
                                t.wait_ge(sa, sa4(i, 7 + (prev - 5)))
                        elif i >= 1:
                            t.wait_ge(sv, ve4(i - 1, 13 + s_))
                            pg_ = w1_slice_parts(prev + 11)[2]
                            if pg_ is not None:
                                t.wait_ge(sa, sa4(i - 1, 7 + (prev + 11 - 5)))
                        pd = p_pool[:, 512 * (s_ % 2):512 * (s_ % 2) + ns]
                        for c in range(8):
                            t.matmul(pd, h2fm[:, 128 * c:128 * c + 128],
                                     w1b[:, 5460 * c + 512 * s_:5460 * c + 512 * s_ + ns],
                                     start=(c == 0), stop=False)
                        t.matmul(pd, ones_rw[:], b1b[0:1, 512 * s_:512 * s_ + ns],
                                 start=False, stop=True).then_inc(stp, 1)
                    t.wait_ge(sv, ve4(i, 9))
                    for g3 in range(3):
                        ncc = 8 if g3 < 2 else 6
                        if g3 >= 1:
                            t.wait_ge(sv, ve4(i, 10 + g3 - 1))
                        for cc in range(ncc):
                            gcc = 8 * g3 + cc
                            nc_ = 42 if gcc == 21 else 128
                            ins = t.transpose(p_tr[0:nc_, 128 * cc:128 * cc + 128],
                                              agb[:, 128 * gcc:128 * gcc + nc_],
                                              ident[:])
                            if cc == ncc - 1:
                                ins.then_inc(stp, 1)
                    t.wait_ge(sv, ve4(i, 12))
                    for os_ in range(2):
                        pd = p_pool[:, 512 * os_:512 * os_ + 512]
                        for cc in range(22):
                            nc_ = 42 if cc == 21 else 128
                            t.matmul(pd, agfm[0:nc_, 128 * cc:128 * cc + 128],
                                     w2b[0:nc_, 1024 * cc + 512 * os_:1024 * cc + 512 * os_ + 512],
                                     start=(cc == 0), stop=False)
                        t.matmul(pd, ones_rw[:], b2b[0:1, 512 * os_:512 * os_ + 512],
                                 start=False, stop=True).then_inc(stp, 1)

            @blk.scalar
            def _(s):
                for i in range(D_NO):
                    par = i % 2
                    s.wait_ge(ltp[par], 16 * (48 + NLD * (i // 2 + 1)))
                    if i >= 1:
                        s.wait_ge(sv, ve4(i - 1, 1))
                    s.activation(sscr[:, 0:512], x4c[:, 0:512], AF.Square,
                                 accum_out=l4(i, SSQS)).then_inc(sa, 1)
                    s.activation(sscr[:, 0:512], x4s[:, 512:1024], AF.Square,
                                 accum_out=l4(i, SQ2)).then_inc(sa, 1)
                    s.activation(sscr[:, 0:512], x4c[:, 0:512], AF.Copy,
                                 accum_out=l4(i, SSUM)).then_inc(sa, 1)
                    s.activation(sscr[:, 0:512], x4s[:, 512:1024], AF.Copy,
                                 accum_out=l4(i, SM2)).then_inc(sa, 1)
                    s.wait_ge(sgv, cap[("var", i)])
                    s.activation(l4(i, SRSTD), l4(i, SVAR), AF.Ln).then_inc(sa, 1)
                    s.wait_ge(sa, sa4(i, 5))
                    s.activation(l4(i, SRSTD), l4(i, SRSTD), AF.Exp, scale=-0.5).then_inc(sa, 1)
                    for s_ in range(5, 11):
                        ns, a, gpart = w1_slice_parts(s_)
                        s.wait_ge(stp, pe4(i, 2 + s_))
                        pa = p_pool[:, 512 * (s_ % 2):512 * (s_ % 2) + ns]
                        gcol0 = 512 * s_ + gpart[0] - 2730
                        s.activation(w1g[:, gcol0:gcol0 + (gpart[1] - gpart[0])],
                                     pa[:, gpart[0]:gpart[1]], AF.Gelu).then_inc(sa, 1)
                    if final:
                        s.wait_ge(sv, ve4(i, 15))
                        s.activation(sscr[:], res2[:, 0:512], AF.Square,
                                     accum_out=l4(i, SQ2)).then_inc(sa, 1)
                        s.activation(sscr[:], res2[:, 512:1024], AF.Square,
                                     accum_out=l4(i, SV2)).then_inc(sa, 1)
                        s.activation(sscr[:], res2[:, 0:512], AF.Copy,
                                     accum_out=l4(i, SM2)).then_inc(sa, 1)
                        s.activation(sscr[:], res2[:, 512:1024], AF.Copy,
                                     accum_out=l4(i, 9)).then_inc(sa, 1)
                        s.wait_ge(sgv, cap[("var2", i)])
                        s.activation(l4(i, SR2), l4(i, SV2), AF.Ln).then_inc(sa, 1)
                        s.wait_ge(sa, sa4(i, 17))
                        s.activation(l4(i, SR2), l4(i, SR2), AF.Exp, scale=-0.5).then_inc(sa, 1)

    return nc


def _nc():
    return bass.Bass("TRN2", target_bir_lowering=False, detect_race_conditions=False)


def kernel(**inputs):
    z = inputs
    x0 = np.asarray(z["L0_attn_embeddings"])
    g = np.asarray(z["L0_attn_guidance_trajectories"])
    cores = list(range(8))

    import contextlib
    nc = _nc()
    with contextlib.ExitStack() as _sems:
        x1_ap, a_so0, a_so1 = build_phase_a(nc, sem_st=_sems)
        build_phase_b(nc, x1_ext=x1_ap, gate=[(a_so0, 16 * 8), (a_so1, 16 * 8)])
        in_maps = []
        for c in cores:
            m = host_inputs_a(x0, g, c)
            m.update(host_inputs_b(None, z["L0_ffn_w_router"], z["L0_ffn_w_gating"],
                                   z["L0_ffn_w_gate"], z["L0_ffn_w_up"], z["L0_ffn_w_down"],
                                   c, fused=True))
            in_maps.append(m)
        res = run_bass_kernel_spmd(nc, in_maps, core_ids=cores)
    x2 = np.zeros((2, 8192, 1024), np.float32)
    for c in cores:
        x2[c // 4, (c % 4) * 2048:(c % 4 + 1) * 2048] = res.results[c]["x2"]

    nc = _nc()
    build_c1(nc)
    res = run_bass_kernel_spmd(
        nc, [host_inputs_c1(x2, z["L1_attn_Wq"], z["L1_attn_bq"], z["L1_attn_Wk"], z["L1_attn_bk"],
                            z["L1_attn_Wv"], z["L1_attn_bv"], c) for c in cores],
        core_ids=cores)
    qfm_all = [res.results[c]["qfm"] for c in cores]
    kfm_all = [res.results[c]["kfm"] for c in cores]
    vtm_all = [res.results[c]["vtm"] for c in cores]
    vsum_all = [res.results[c]["vsump"] for c in cores]

    nc = _nc()
    build_c2(nc)
    res = run_bass_kernel_spmd(
        nc, [host_inputs_c2(x2, qfm_all, kfm_all, vtm_all, vsum_all,
                            z["L1_attn_Wo"], z["L1_attn_bo"], c) for c in cores],
        core_ids=cores)
    x3 = np.zeros((2, 8192, 1024), np.float32)
    for c in cores:
        x3[c // 4, (c % 4) * 2048:(c % 4 + 1) * 2048] = res.results[c]["x3"]

    nc = _nc()
    build_d(nc, final=False)
    res = run_bass_kernel_spmd(nc, [host_inputs_d(x3, z, 0, c) for c in cores], core_ids=cores)
    y1 = np.zeros((2, 8192, 1024), np.float32)
    for c in cores:
        y1[c // 4, (c % 4) * 2048:(c % 4 + 1) * 2048] = res.results[c]["y"]

    nc = _nc()
    build_d(nc, final=True)
    res = run_bass_kernel_spmd(nc, [host_inputs_d(y1, z, 1, c, x3own_full=x3) for c in cores],
                               core_ids=cores)
    out = np.zeros((2, 8192, 1024), np.float32)
    for c in cores:
        out[c // 4, (c % 4) * 2048:(c % 4 + 1) * 2048] = res.results[c]["y"]
    return out
